# version 09
# baseline (speedup 1.0000x reference)
import numpy as np
import ml_dtypes
from contextlib import ExitStack
from types import SimpleNamespace as NS
import concourse.bass as bass
import concourse.mybir as mybir
from concourse.bass_utils import run_bass_kernel_spmd

F32 = mybir.dt.float32
BF16 = mybir.dt.bfloat16
F32R = mybir.dt.float32r
AF = mybir.ActivationFunctionType
ALU = mybir.AluOpType
AX = mybir.AxisListType

DBG = False
STAGES = 5
PARANOID = False
P5_PART = 3

B, T, C = 2, 8192, 1024
SEG = 2048
NCH = C // 128
MCH_J = [8, 4, 2, 1]
NQT = SEG // 512
NTT = SEG // 128
EPS = 1e-5
SCALE_F = 1.0 / 32.0
SCALE_L = 0.125


class EM:
    ENGS = ("sync", "gpsimd", "tensor", "vector", "scalar")

    def __init__(self, nc, ex):
        self.nc = nc
        self.ops = {k: [] for k in self.ENGS}
        self.sems = {}
        self.cnt = {}
        self.lastw = {k: {} for k in self.ENGS}
        for s in ("dS", "dG", "tP", "tA", "tV"):
            self.sems[s] = ex.enter_context(nc.semaphore(s))
            self.cnt[s] = 0

    def raw(self, eng, fn):
        self.ops[eng].append(fn)

    def wait(self, eng, sem, val):
        if val is None or val <= 0:
            return
        if self.lastw[eng].get(sem, 0) >= val:
            return
        self.lastw[eng][sem] = val
        h = self.sems[sem]
        self.raw(eng, lambda e, h=h, v=val: e.wait_ge(h, v))

    def op(self, eng, fn, inc=None, amt=1):
        if PARANOID is True or (PARANOID and eng in PARANOID):
            sems = ("dS", "dG", "tP", "tA", "tV")
            if isinstance(PARANOID, dict):
                sems = PARANOID[eng]
            for s in sems:
                self.wait(eng, s, self.cnt[s])
            if inc is None:
                inc = {"sync": "dS", "gpsimd": "dG", "tensor": "tP",
                       "vector": "tV", "scalar": "tA"}[eng]
                amt = 16 if inc in ("dS", "dG") else 1
                if eng in ("sync", "gpsimd"):
                    pass
        if inc is None:
            self.raw(eng, fn)
            return None
        self.cnt[inc] += amt
        h = self.sems[inc]
        self.raw(eng, lambda e, fn=fn, h=h, a=amt: fn(e).then_inc(h, a))
        return self.cnt[inc]

    def dma(self, q, out, in_, slow=False):
        inc = "dS" if q == "sync" else "dG"
        return self.op(q, lambda e, o=out, i=in_, s=slow: e.dma_start(
            out=o, in_=i, allow_slow_non_contiguous=s), inc=inc, amt=16)

    def mm(self, out, lhsT, rhs, start, stop, inc=None, tile=None):
        return self.op("tensor",
                       lambda e, o=out, l=lhsT, r=rhs, s0=start, s1=stop,
                       tp=tile: e.matmul(o, l, r, start=s0, stop=s1,
                                         tile_position=tp), inc=inc)

    def tr(self, out, in_, ident, inc=None):
        return self.op("tensor",
                       lambda e, o=out, i=in_, d=ident: e.transpose(o, i, d),
                       inc=inc)

    def act_ps(self, fn):
        n = self.op("scalar", fn, inc="tA")
        self.wait("scalar", "tA", n - 2)
        return n

    def act(self, fn):
        return self.op("scalar", fn, inc="tA")

    def vec(self, fn):
        self.wait("vector", "tV", self.cnt["tV"])
        return self.op("vector", fn, inc="tV")


def build_program(stages=STAGES):
    nc = bass.Bass("TRN2", target_bir_lowering=False)
    ex = ExitStack()
    c = NS()
    c.nc, c.ex = nc, ex

    di = lambda n, s, d=F32: nc.dram_tensor(n, s, d, kind="ExternalInput").ap()
    do = lambda n, s, d=F32: nc.dram_tensor(n, s, d, kind="ExternalOutput").ap()
    dscr = lambda n, s, d=F32: nc.dram_tensor(
        n, s, d, kind=("ExternalOutput" if DBG else "Internal")).ap()

    c.xb = di("xb", [T, C]); c.xq = di("xq", [SEG, C]); c.xh = di("xh", [64, C])
    c.wqkv = di("wqkv", [C, 3 * C]); c.bqkv = di("bqkv", [3 * C])
    c.wout = di("wout", [C, C]); c.bout = di("bout", [C])
    c.ln1g = di("ln1g", [C]); c.ln1b = di("ln1b", [C])
    c.ln2g = di("ln2g", [C]); c.ln2b = di("ln2b", [C])
    c.w1 = di("w1", [C, 4 * C]); c.b1 = di("b1", [4 * C])
    c.w2 = di("w2", [4 * C, C]); c.b2 = di("b2", [C])
    c.nw = di("nw", [C]); c.nb = di("nb", [C])
    c.qw = di("qw", [C, C]); c.qbias = di("qbias", [C])
    c.kw = di("kw", [C, C]); c.kbias = di("kbias", [C])
    c.vw = di("vw", [C, C]); c.vbias = di("vbias", [C])
    c.ow = di("ow", [C, C]); c.obias = di("obias", [C])
    c.ident32 = di("ident32", [128, 128])
    c.identb = di("identb", [128, 128], BF16)
    c.ones1 = di("ones1", [1, 128])
    c.onesb = di("onesb", [128, 1], BF16)
    c.onesrb = di("onesrb", [1, 128], BF16)
    c.selmat = di("selmat", [128, 16], BF16)
    c.maskA = di("maskA", [128, 128], BF16)
    c.maskB = di("maskB", [64, 128], BF16)
    c.cmaskv = di("cmaskv", [C])
    c.cmcol = di("cmcol", [1, C])
    c.hmaskv = di("hmaskv", [128, 1])

    c.y = do("y", [SEG, C])

    c.kscr = dscr("kscr", [C, T], BF16)
    c.vscr = dscr("vscr", [8, T, 128], BF16)
    c.qscr = dscr("qscr", [C, SEG], BF16)
    c.xnscr = dscr("xnscr", [NCH, 128, SEG])
    c.xascr = dscr("xascr", [NCH, 128, SEG])
    c.accscr = dscr("accscr", [NCH, 128, SEG])

    em = EM(nc, ex)
    c.em = em
    E = ex.enter_context
    sbt = lambda n, s, d=F32: E(nc.sbuf_tensor(n, s, d))

    c.t_bqkv = sbt("t_bqkv", [128, 24])
    c.t_bout = sbt("t_bout", [128, 8])
    c.t_b1 = sbt("t_b1", [128, 32])
    c.t_b2 = sbt("t_b2", [128, 8])
    c.t_ln2g = sbt("t_ln2g", [128, 8]); c.t_ln2b = sbt("t_ln2b", [128, 8])
    c.t_qb = sbt("t_qb", [128, 8]); c.t_kb = sbt("t_kb", [128, 8])
    c.t_ob = sbt("t_ob", [128, 8])
    c.t_cm = sbt("t_cm", [128, 8])
    c.t_hm = sbt("t_hm", [128, 1])
    c.t_eps = sbt("t_eps", [128, 1])
    c.t_id32 = sbt("t_id32", [128, 128])
    c.t_idb = sbt("t_idb", [128, 128], BF16)
    c.t_ones1 = sbt("t_ones1", [1, 128], F32R)
    c.t_onesb = sbt("t_onesb", [128, 1], BF16)
    c.t_onesrb = sbt("t_onesrb", [1, 128], BF16)
    c.t_c3b = sbt("t_c3b", [1, 1], BF16)
    c.t_sel = sbt("t_sel", [128, 16], BF16)
    c.t_mA = sbt("t_mA", [128, 128], BF16)
    c.t_mB = sbt("t_mB", [64, 128], BF16)
    c.t_cmcol = sbt("t_cmcol", [1, C], F32R)
    c.t_cb4 = sbt("t_cb4", [128, 1])
    c.t_cb2 = sbt("t_cb2", [128, 1])
    c.t_c1 = sbt("t_c1", [128, 1])
    c.t_c2 = sbt("t_c2", [1, 128])
    c.t_c3 = sbt("t_c3", [1, 1])
    p1x = ExitStack()
    sb1 = lambda n, sh, d=F32: p1x.enter_context(nc.sbuf_tensor(n, sh, d))
    c.t_vbrep = sb1("t_vbrep", [128, C])
    c.t_nwrep = sb1("t_nwrep", [128, C]); c.t_nbrep = sb1("t_nbrep", [128, C])
    c.t_nwmrep = sb1("t_nwmrep", [128, C]); c.t_nbmrep = sb1("t_nbmrep", [128, C])
    c.t_l1grep = sb1("t_l1grep", [128, C]); c.t_l1brep = sb1("t_l1brep", [128, C])
    c.t_cmrep = sb1("t_cmrep", [128, C])
    c.t_st = [sb1(f"t_st{i}", [128, 32]) for i in range(2)]
    c.t_sq = [sb1(f"t_sq{i}", [128, 1024]) for i in range(2)]
    c.t_xt = [sb1(f"t_xt{i}", [128, 1024]) for i in range(2)]
    c.t_xn = [sb1(f"t_xn{i}", [128, 1024]) for i in range(2)]
    c.t_t2 = [sb1(f"t_t2{i}", [128, 1024], BF16) for i in range(2)]
    c.t_th = [sb1(f"t_th{i}", [128, 1024], BF16) for i in range(2)]
    c.t_row = sb1("t_row", [1, C], F32R)

    c.PS = [p1x.enter_context(nc.psum_tensor(f"ps{i}", [128, 512], F32))
            for i in range(8)]
    c.psb = lambda i: c.PS[i][:].bitcast(BF16)
    c.bank_evac = {}

    def bank_ready(bank):
        em.wait("tensor", "tA", c.bank_evac.get(bank, 0))
    c.bank_ready = bank_ready

    for (tile, src, n) in ((c.t_bqkv, c.bqkv, 24), (c.t_bout, c.bout, 8),
                           (c.t_b1, c.b1, 32), (c.t_b2, c.b2, 8),
                           (c.t_ln2g, c.ln2g, 8), (c.t_ln2b, c.ln2b, 8),
                           (c.t_qb, c.qbias, 8), (c.t_kb, c.kbias, 8),
                           (c.t_ob, c.obias, 8), (c.t_cm, c.cmaskv, 8)):
        ap = bass.AP(src.tensor, 0, [[1, 128], [128, n]])
        em.dma("sync", tile[:], ap, slow=True)
    em.dma("sync", c.t_hm[:], c.hmaskv)
    em.dma("sync", c.t_id32[:], c.ident32)
    em.dma("sync", c.t_idb[:], c.identb)
    em.dma("sync", c.t_ones1[:], c.ones1.bitcast(F32R))
    em.dma("sync", c.t_onesb[:], c.onesb)
    em.dma("sync", c.t_onesrb[:], c.onesrb)
    em.dma("sync", c.t_sel[:], c.selmat)
    em.dma("sync", c.t_mA[:], c.maskA)
    em.dma("sync", c.t_mB[:], c.maskB)
    n_ld0 = em.dma("sync", c.t_cmcol[:], c.cmcol.bitcast(F32R))
    em.wait("vector", "dS", n_ld0)
    em.vec(lambda e: e.memset(c.t_eps[:], EPS))
    em.wait("tensor", "dS", n_ld0)

    def replicate_row(src_vec, dst):
        em.wait("sync", "tP", em.cnt["tP"])
        n = em.dma("sync", c.t_row[:],
                   bass.AP(src_vec.tensor, 0, [[0, 1], [1, C]]).bitcast(F32R))
        em.wait("tensor", "dS", n)
        last = None
        for half in range(2):
            bank_ready(half)
            last = em.mm(c.PS[half][:], c.t_ones1[:],
                         c.t_row[0:1, half * 512:(half + 1) * 512],
                         True, True, inc="tP")
        em.wait("scalar", "tP", last)
        for half in range(2):
            na = em.act_ps(lambda e, h=half, d=dst: e.copy(
                d[:, h * 512:(h + 1) * 512], c.PS[h][:]))
            c.bank_evac[half] = na

    replicate_row(c.vbias, c.t_vbrep)
    replicate_row(c.nw, c.t_nwrep)
    replicate_row(c.nb, c.t_nbrep)
    replicate_row(c.ln1g, c.t_l1grep)
    replicate_row(c.ln1b, c.t_l1brep)
    replicate_row(c.cmaskv, c.t_cmrep)
    em.wait("vector", "tA", em.cnt["tA"])
    em.vec(lambda e: e.tensor_mul(c.t_nwmrep[:], c.t_nwrep[:], c.t_cmrep[:]))
    n_nwm = em.vec(lambda e: e.tensor_mul(c.t_nbmrep[:], c.t_nbrep[:],
                                          c.t_cmrep[:]))
    c.n_init_v = n_nwm

    def ln_stats(src_ap, P, col, st, sq):
        em.vec(lambda e: e.reduce_sum(st[0:P, col + 3:col + 4], src_ap, AX.X))
        na = em.act(lambda e: e.activation(
            sq[0:P, :], src_ap, AF.Square, bias=0.0, scale=1.0,
            accum_out=st[0:P, col + 4:col + 5]))
        em.wait("vector", "tA", na)
        em.vec(lambda e: e.tensor_scalar_mul(
            st[0:P, col:col + 1], st[0:P, col + 3:col + 4], 1.0 / C))
        em.vec(lambda e: e.tensor_scalar_mul(
            st[0:P, col + 5:col + 6], st[0:P, col + 4:col + 5], 1.0 / C))
        em.vec(lambda e: e.tensor_mul(
            st[0:P, col + 6:col + 7], st[0:P, col:col + 1],
            st[0:P, col:col + 1]))
        nv = em.vec(lambda e: e.tensor_sub(
            st[0:P, col + 7:col + 8], st[0:P, col + 5:col + 6],
            st[0:P, col + 6:col + 7]))
        em.wait("scalar", "tV", nv)
        na = em.act(lambda e: e.activation(
            st[0:P, col + 8:col + 9], st[0:P, col + 7:col + 8],
            AF.Sqrt, bias=c.t_eps[0:P, :], scale=1.0))
        em.wait("vector", "tA", na)
        em.vec(lambda e: e.reciprocal(
            st[0:P, col + 1:col + 2], st[0:P, col + 8:col + 9]))
        nv = em.vec(lambda e: e.scalar_tensor_tensor(
            st[0:P, col + 2:col + 3], st[0:P, col:col + 1], -1.0,
            st[0:P, col + 1:col + 2], ALU.mult, ALU.mult))
        return nv

    def norm_apply_affine(P, grep, brep, out_bf, st, sq, xn):
        em.vec(lambda e: e.scalar_tensor_tensor(
            sq[0:P, :], xn[0:P, :], st[0:P, 17:18],
            st[0:P, 18:19].broadcast_to([P, C]), ALU.mult, ALU.add))
        em.vec(lambda e: e.tensor_mul(sq[0:P, :], sq[0:P, :], grep[0:P, :]))
        return em.vec(lambda e: e.tensor_add(out_bf[0:P, :], sq[0:P, :],
                                             brep[0:P, :]))

    def phase_barrier():
        for eng in EM.ENGS:
            for s in ("dS", "dG", "tP", "tA", "tV"):
                em.wait(eng, s, em.cnt[s])
    c.phase_barrier = phase_barrier

    c.ln_stats = ln_stats
    c.norm_apply_affine = norm_apply_affine

    p1 = ExitStack()
    E1 = p1.enter_context
    c.kwT = E1(nc.sbuf_tensor("kwT", [128, NCH * C], BF16))
    c.vwT = E1(nc.sbuf_tensor("vwT", [128, NCH * C], BF16))
    c.qwT = E1(nc.sbuf_tensor("qwT", [128, NCH * C], BF16))
    c.wpan = E1(nc.sbuf_tensor("wpan", [128, C], BF16))
    c.xnn = E1(nc.sbuf_tensor("xnn", [128, NCH * SEG], BF16))
    c.kstg = [E1(nc.sbuf_tensor(f"kstg{i}", [128, 512], BF16)) for i in range(2)]
    c.vtmp = E1(nc.sbuf_tensor("vtmp", [128, 512], F32))
    c.vstg = [E1(nc.sbuf_tensor(f"vstg{i}", [128, 512], BF16)) for i in range(2)]
    c.qstg = [E1(nc.sbuf_tensor(f"qstg{i}", [128, 512], BF16)) for i in range(2)]
    c.xstg = [E1(nc.sbuf_tensor(f"xstg{i}", [128, 512], F32)) for i in range(2)]
    c.h_ctx = ExitStack()
    c.h_fm = c.h_ctx.enter_context(
        nc.sbuf_tensor("h_fm", [128, NCH * (SEG + 64)], BF16,
                       side="right"))
    HW = SEG + 64

    def build_wT(src, dst_sb):
        for r in range(8):
            em.wait("gpsimd", "tP", em.cnt["tP"])
            n = em.dma("gpsimd", c.wpan[:], src[r * 128:(r + 1) * 128, :])
            em.wait("tensor", "dG", n)
            for g in range(2):
                bank = g
                bank_ready(bank)
                last = None
                for i in range(4):
                    mc = g * 4 + i
                    last = em.tr(c.psb(bank)[:, i * 128:(i + 1) * 128],
                                 c.wpan[:, mc * 128:(mc + 1) * 128],
                                 c.t_idb[:], inc=(None if i < 3 else "tP"))
                em.wait("scalar", "tP", last)
                dv = dst_sb[:].rearrange("p (mc w) -> p mc w", w=C)
                na = em.act_ps(lambda e, b=bank, g=g, r=r, dv=dv: e.copy(
                    dv[:, g * 4:g * 4 + 4, r * 128:(r + 1) * 128],
                    c.psb(b)[:, 0:512].rearrange("p (a w) -> p a w", w=128)))
                c.bank_evac[bank] = na

    build_wT(c.kw, c.kwT)
    build_wT(c.vw, c.vwT)
    build_wT(c.qw, c.qwT)

    c.tile_users = [dict(tA=0, tV=0, tP=0), dict(tA=0, tV=0, tP=0)]

    def stream_tile(src_ap, P, grep, brep, par):
        u = c.tile_users[par]
        em.wait("sync", "tA", u["tA"])
        em.wait("sync", "tV", u["tV"])
        em.wait("sync", "tP", u["tP"])
        st, sq, xt, xn = c.t_st[par], c.t_sq[par], c.t_xt[par], c.t_xn[par]
        n = em.dma("sync", xt[0:P, :], src_ap)
        em.wait("vector", "dS", n)
        em.wait("scalar", "dS", n)
        nv0 = ln_stats(xt[0:P, :], P, 0, st, sq)
        em.wait("scalar", "tV", nv0)
        na = em.act(lambda e, P=P, st=st, xt=xt, xn=xn: e.activation(
            xn[0:P, :], xt[0:P, :], AF.Identity,
            bias=st[0:P, 2:3], scale=st[0:P, 1:2]))
        em.wait("vector", "tA", na)
        em.wait("scalar", "tA", na)
        ln_stats(xn[0:P, :], P, 16, st, sq)
        return norm_apply_affine(P, grep, brep, c.t_t2[par], st, sq, xn)

    for j in range(4):
        mch = MCH_J[j]
        for t in range(NTT):
            par = t % 2
            nv = stream_tile(c.xb[j * SEG + t * 128: j * SEG + (t + 1) * 128, :],
                             128, c.t_nwrep, c.t_nbrep, par)
            em.wait("tensor", "tV", nv)
            bank = 2 + par
            bank_ready(bank)
            last = None
            for ch in range(mch):
                last = em.tr(c.psb(bank)[:, ch * 128:(ch + 1) * 128],
                             c.t_t2[par][:, ch * 128:(ch + 1) * 128],
                             c.t_idb[:], inc=(None if ch < mch - 1 else "tP"))
            em.wait("scalar", "tP", last)
            xv = c.xnn[:].rearrange("p (ch w) -> p ch w", w=SEG)
            na = em.act_ps(lambda e, b=bank, t=t, mch=mch, xv=xv: e.copy(
                xv[:, 0:mch, t * 128:(t + 1) * 128],
                c.psb(b)[:].rearrange("p (ch w) -> p ch w", w=128)[:, 0:mch, :]))
            c.bank_evac[bank] = na
            c.tile_users[par] = dict(tA=na, tV=em.cnt["tV"], tP=last)
        n_xnn_done = em.cnt["tA"]

        em.wait("tensor", "tA", n_xnn_done)
        for dk in range(8):
            for qt in range(NQT):
                bank = 4 + ((dk * NQT + qt) % 2)
                bank_ready(bank)
                last = None
                for i in range(mch):
                    last = em.mm(
                        c.PS[bank][:],
                        c.kwT[:, i * C + dk * 128: i * C + (dk + 1) * 128],
                        c.xnn[:, i * SEG + qt * 512: i * SEG + (qt + 1) * 512],
                        i == 0, i == mch - 1,
                        inc=(None if i < mch - 1 else "tP"))
                em.wait("scalar", "tP", last)
                stg = c.kstg[(dk * NQT + qt) % 2]
                em.wait("scalar", "dS", c.bank_evac.get(("kstg", (dk * NQT + qt) % 2), 0))
                na = em.act_ps(lambda e, b=bank, dk=dk, s=stg: e.activation(
                    s[:], c.PS[b][:], AF.Identity,
                    bias=c.t_kb[:, dk:dk + 1], scale=1.0))
                c.bank_evac[bank] = na
                em.wait("sync", "tA", na)
                nd = em.dma("sync",
                            c.kscr[dk * 128:(dk + 1) * 128,
                                   j * SEG + qt * 512: j * SEG + (qt + 1) * 512],
                            stg[:])
                c.bank_evac[("kstg", (dk * NQT + qt) % 2)] = nd

        for tt in range(NTT):
            for hf in range(2):
                bank = 6 + ((tt * 2 + hf) % 2)
                bank_ready(bank)
                last = None
                for i in range(mch):
                    last = em.mm(
                        c.PS[bank][:],
                        c.xnn[:, i * SEG + tt * 128: i * SEG + (tt + 1) * 128],
                        c.vwT[:, i * C + hf * 512: i * C + (hf + 1) * 512],
                        i == 0, i == mch - 1,
                        inc=(None if i < mch - 1 else "tP"))
                em.wait("scalar", "tP", last)
                em.wait("scalar", "tV", c.bank_evac.get("vtmp", 0))
                na = em.act_ps(lambda e, b=bank: e.copy(c.vtmp[:], c.PS[b][:]))
                c.bank_evac[bank] = na
                em.wait("vector", "tA", na)
                sidx = (tt * 2 + hf) % 2
                em.wait("vector", "dS",
                        c.bank_evac.get(("vstg", sidx), 0))
                nv = em.vec(lambda e, hf=hf, s=c.vstg[sidx]: e.tensor_add(
                    s[:], c.vtmp[:], c.t_vbrep[:, hf * 512:(hf + 1) * 512]))
                c.bank_evac["vtmp"] = nv
                em.wait("sync", "tV", nv)
                ndl = 0
                for hh in range(4):
                    h = hf * 4 + hh
                    ndl = em.dma("sync",
                                 c.vscr[h, j * SEG + tt * 128:
                                        j * SEG + (tt + 1) * 128, :],
                                 c.vstg[sidx][:, hh * 128:(hh + 1) * 128])
                c.bank_evac[("vstg", sidx)] = ndl

    def own_tile(src_ap, P, hcol, t):
        par = t % 2 if t >= 0 else 0
        nv = stream_tile(src_ap, P, c.t_nwmrep, c.t_nbmrep, par)
        last_tr = 0
        if t >= 0:
            em.wait("tensor", "tV", nv)
            bank = 2 + par
            bank_ready(bank)
            last = None
            for ch in range(8):
                last = em.tr(c.psb(bank)[:, ch * P:ch * P + P],
                             c.t_t2[par][0:P, ch * 128:(ch + 1) * 128],
                             c.t_idb[0:P, 0:P], inc=(None if ch < 7 else "tP"))
            em.wait("scalar", "tP", last)
            last_tr = last
            xv = c.xnn[:].rearrange("p (ch w) -> p ch w", w=SEG)
            na = em.act_ps(lambda e, b=bank, t=t, xv=xv: e.copy(
                xv[:, :, t * 128:(t + 1) * 128],
                c.psb(b)[:].rearrange("p (ch w) -> p ch w", w=128)))
            c.bank_evac[bank] = na
        st, sq, xn = c.t_st[par], c.t_sq[par], c.t_xn[par]
        nv2 = norm_apply_affine(P, c.t_l1grep, c.t_l1brep, c.t_th[par],
                                st, sq, xn)
        em.wait("tensor", "tV", nv2)
        bank2 = 2 + ((t + 1) % 2) if t >= 0 else 3
        bank_ready(bank2)
        last = None
        for ch in range(8):
            last = em.tr(c.psb(bank2)[:, ch * P:ch * P + P],
                         c.t_th[par][0:P, ch * 128:(ch + 1) * 128],
                         c.t_idb[0:P, 0:P], inc=(None if ch < 7 else "tP"))
        em.wait("scalar", "tP", last)
        hv = c.h_fm[:].rearrange("p (ch w) -> p ch w", w=HW)
        na = em.act_ps(lambda e, b=bank2, hcol=hcol, P=P, hv=hv: e.copy(
            hv[:, :, hcol:hcol + P],
            c.psb(b)[:].rearrange("p (ch w) -> p ch w", w=P)[:, 0:8, :]))
        c.bank_evac[bank2] = na
        if t >= 0:
            for g in range(2):
                bank3 = 4 + g
                bank_ready(bank3)
                last = None
                for i in range(4):
                    ch = g * 4 + i
                    last = em.tr(c.PS[bank3][:, i * 128:(i + 1) * 128],
                                 c.t_xn[par][:, ch * 128:(ch + 1) * 128],
                                 c.t_id32[:], inc=(None if i < 3 else "tP"))
                em.wait("scalar", "tP", last)
                sidx = g
                em.wait("scalar", "dS", c.bank_evac.get(("xstg", sidx), 0))
                na = em.act_ps(lambda e, b=bank3, s=c.xstg[sidx]: e.copy(
                    s[:], c.PS[b][:]))
                c.bank_evac[bank3] = na
                em.wait("sync", "tA", na)
                nd = 0
                for i in range(4):
                    ch = g * 4 + i
                    nd = em.dma("sync",
                                c.xnscr[ch, :, t * 128:(t + 1) * 128],
                                c.xstg[sidx][:, i * 128:(i + 1) * 128])
                c.bank_evac[("xstg", sidx)] = nd
        c.tile_users[par] = dict(tA=em.cnt["tA"], tV=em.cnt["tV"],
                                 tP=em.cnt["tP"])

    own_tile(c.xh[0:64, :], 64, 0, -1)
    for t in range(NTT):
        own_tile(c.xq[t * 128:(t + 1) * 128, :], 128, 64 + t * 128, t)
    n_xnnq_done = em.cnt["tA"]

    em.wait("tensor", "tA", n_xnnq_done)
    for dq in range(8):
        for qt in range(NQT):
            bank = 6 + ((dq * NQT + qt) % 2)
            bank_ready(bank)
            last = None
            for i in range(8):
                last = em.mm(
                    c.PS[bank][:],
                    c.qwT[:, i * C + dq * 128: i * C + (dq + 1) * 128],
                    c.xnn[:, i * SEG + qt * 512: i * SEG + (qt + 1) * 512],
                    i == 0, i == 7, inc=(None if i < 7 else "tP"))
            em.wait("scalar", "tP", last)
            sidx = (dq * NQT + qt) % 2
            em.wait("scalar", "dS", c.bank_evac.get(("qstg", sidx), 0))
            na = em.act_ps(lambda e, b=bank, dq=dq, s=c.qstg[sidx]: e.activation(
                s[:], c.PS[b][:], AF.Identity, bias=c.t_qb[:, dq:dq + 1], scale=1.0))
            c.bank_evac[bank] = na
            em.wait("sync", "tA", na)
            nd = em.dma("sync",
                        c.qscr[dq * 128:(dq + 1) * 128,
                               qt * 512:(qt + 1) * 512],
                        c.qstg[sidx][:])
            c.bank_evac[("qstg", sidx)] = nd

    if DBG:
        dbg1 = do("dbg_xnn", [128, NCH * SEG], BF16)
        dbg2 = do("dbg_h", [128, NCH * HW], BF16)
        phase_barrier()
        em.dma("sync", dbg1, c.xnn[:])
        em.dma("sync", dbg2, c.h_fm[:])

    phase_barrier()
    p1.close()
    p1x.close()

    if isinstance(stages, int):
        stages = ["p1", "p2", "p3", "p4", "p5"][:stages]
    if "p2" in stages:
        emit_phase2(c)
        phase_barrier()
    if "p3" in stages:
        c.h_ctx.close()
        emit_phase3(c)
        phase_barrier()
    if "p4" in stages:
        if "p3" not in stages:
            c.h_ctx.close()
        emit_phase4(c)
        phase_barrier()
    if "p5" in stages:
        emit_phase5(c)
    else:
        nd = em.dma("sync", c.y[0:128, :],
                    bass.AP(c.xq.tensor, 0, [[1024, 128], [1, 1024]]))
        em.wait("sync", "dS", nd)

    with nc.Block() as block:
        @block.sync
        def _(e):
            for f in em.ops["sync"]:
                f(e)

        @block.gpsimd
        def _(e):
            for f in em.ops["gpsimd"]:
                f(e)

        @block.tensor
        def _(e):
            for f in em.ops["tensor"]:
                f(e)

        @block.vector
        def _(e):
            for f in em.ops["vector"]:
                f(e)

        @block.scalar
        def _(e):
            for f in em.ops["scalar"]:
                f(e)

    ex.close()
    return nc


def emit_phase2(c):
    nc, em = c.nc, c.em
    HW = SEG + 64
    p2 = ExitStack()
    E2 = p2.enter_context
    wqkv_sb = E2(nc.sbuf_tensor("wqkv_sb", [128, NCH * 3 * C], BF16))
    wout_sb = E2(nc.sbuf_tensor("wout_sb", [128, NCH * C], BF16))
    qkvl = E2(nc.sbuf_tensor("qkvl", [128, 24 * 576], BF16))
    vtl = E2(nc.sbuf_tensor("vtl", [128, 5 * 1024], BF16))
    ptA = [E2(nc.sbuf_tensor(f"ptA{i}", [128, 1024], BF16)) for i in range(2)]
    ptB = [E2(nc.sbuf_tensor(f"ptB{i}", [64, 1024], BF16)) for i in range(2)]
    rbl = E2(nc.sbuf_tensor("rbl", [128, 1024], F32))
    outl = E2(nc.sbuf_tensor("outl", [128, 8 * 512], BF16))
    l2 = E2(nc.sbuf_tensor("l2", [1, 1024], F32))
    r2 = E2(nc.sbuf_tensor("r2", [1, 1024], F32))
    r2r = E2(nc.sbuf_tensor("r2r", [1, 1024], F32R))
    xat = [E2(nc.sbuf_tensor(f"xat{i}", [128, 512], F32)) for i in range(2)]
    xnt = [E2(nc.sbuf_tensor(f"xnt{i}", [128, 512], F32)) for i in range(2)]
    xas = [E2(nc.sbuf_tensor(f"xas{i}", [128, 512], F32)) for i in range(2)]
    pSA = E2(nc.psum_tensor("pSA", [128, 1024], F32))
    pSB = E2(nc.psum_tensor("pSB", [64, 1024], F32))
    pPV = E2(nc.psum_tensor("pPV", [128, 512], F32))
    pL2 = E2(nc.psum_tensor("pL2", [128, 512], F32))
    pX2 = [E2(nc.psum_tensor(f"pX2{i}", [128, 512], F32)) for i in range(2)]

    ev = c.bank_evac

    nw_ = None
    for r in range(8):
        nw_ = em.dma("gpsimd", wqkv_sb[:, r * 3 * C:(r + 1) * 3 * C],
                     c.wqkv[r * 128:(r + 1) * 128, :])
        nw_ = em.dma("gpsimd", wout_sb[:, r * C:(r + 1) * C],
                     c.wout[r * 128:(r + 1) * 128, :])
    em.wait("tensor", "dG", nw_)
    cb2_done = [False]

    def emit_scores(cc, qb, grp):
        kc0 = qb * 128
        qc0 = 64 + qb * 128
        em.wait("tensor", "tA", ev.get("pSA", 0))
        em.wait("tensor", "tA", ev.get("pSB", 0))
        lastB = None
        for hl in range(8):
            hh = grp * 8 + hl
            kch = 8 + hh // 2
            qch = hh // 2
            pb = (hh % 2) * 64
            lhsA = qkvl[pb:pb + 64, kch * 576 + kc0:kch * 576 + kc0 + 128]
            lhsB = qkvl[pb:pb + 64, kch * 576 + kc0 + 128:
                        kch * 576 + kc0 + 192]
            rhsq = qkvl[pb:pb + 64, qch * 576 + qc0:qch * 576 + qc0 + 128]
            em.mm(pSA[:, hl * 128:(hl + 1) * 128], lhsA, rhsq, True, True,
                  inc=("tP" if hl == 7 else None))
            lastB = em.mm(pSB[:, hl * 128:(hl + 1) * 128], lhsB, rhsq,
                          True, True, inc=("tP" if hl == 7 else None))
        return lastB

    def emit_exp(par, nscores):
        em.wait("scalar", "tP", nscores)
        if not cb2_done[0]:
            na = em.act_ps(lambda e: e.copy(rbl[:, 0:512], pSA[:, 0:512]))
            em.wait("vector", "tA", na)
            nv = em.vec(lambda e: e.reduce_max(c.t_c1[:], rbl[:, 0:512], AX.X))
            em.wait("tensor", "tV", nv)
            ntr = em.tr(pX2[0][0:1, 0:128], c.t_c1[:], c.t_id32[:], inc="tP")
            em.wait("scalar", "tP", ntr)
            na = em.act_ps(lambda e: e.copy(c.t_c2[:], pX2[0][0:1, 0:128]))
            em.wait("vector", "tA", na)
            em.vec(lambda e: e.reduce_max(c.t_c3[:], c.t_c2[:], AX.X))
            em.vec(lambda e: e.tensor_scalar_mul(c.t_c3[:], c.t_c3[:], -SCALE_L))
            nv = em.vec(lambda e: e.tensor_copy(c.t_c3b[:], c.t_c3[:]))
            em.wait("tensor", "tV", nv)
            nmm2 = em.mm(pX2[0][:, 0:1], c.t_onesrb[:], c.t_c3b[:],
                         True, True, inc="tP")
            em.wait("scalar", "tP", nmm2)
            na = em.act_ps(lambda e: e.copy(c.t_cb2[:], pX2[0][:, 0:1]))
            ev["pX20"] = na
            cb2_done[0] = True
        em.wait("scalar", "tV", ev.get(f"pt_use{par}", 0))
        em.wait("scalar", "tP", ev.get(f"pt_pv{par}", 0))
        naA = em.act_ps(lambda e, par=par: e.activation(
            ptA[par][:], pSA[:], AF.Exp, bias=c.t_cb2[:], scale=SCALE_L))
        ev["pSA"] = naA
        naB = em.act_ps(lambda e, par=par: e.activation(
            ptB[par][:], pSB[:], AF.Exp, bias=c.t_cb2[0:64, :], scale=SCALE_L))
        ev["pSB"] = naB
        return naB

    def emit_rest(cc, qb, grp, par, nexp):
        em.wait("vector", "tA", nexp)
        vA = ptA[par][:].rearrange("p (o w) -> p o w", w=128)
        mAb = c.t_mA[:].rearrange("p (o w) -> p o w", w=128).broadcast_to(
            [128, 8, 128])
        em.vec(lambda e: e.tensor_mul(vA, vA, mAb))
        vB = ptB[par][:].rearrange("p (o w) -> p o w", w=128)
        mBb = c.t_mB[:].rearrange("p (o w) -> p o w", w=128).broadcast_to(
            [64, 8, 128])
        nv = em.vec(lambda e: e.tensor_mul(vB, vB, mBb))
        em.wait("tensor", "tV", nv)
        em.wait("tensor", "tA", ev.get("pL2", 0))
        em.wait("tensor", "tA", ev.get("pX21", 0))
        em.mm(pL2[0:1, :], c.t_onesb[:], ptA[par][:, 0:512], True, False)
        em.mm(pL2[0:1, :], c.t_onesb[0:64, :], ptB[par][:, 0:512], False, True)
        em.mm(pX2[1][0:1, :], c.t_onesb[:], ptA[par][:, 512:1024], True, False)
        nl = em.mm(pX2[1][0:1, :], c.t_onesb[0:64, :], ptB[par][:, 512:1024],
                   False, True, inc="tP")
        em.wait("scalar", "tP", nl)
        na = em.act_ps(lambda e: e.copy(l2[0:1, 0:512], pL2[0:1, :]))
        ev["pL2"] = na
        na = em.act_ps(lambda e: e.copy(l2[0:1, 512:1024], pX2[1][0:1, :]))
        ev["pX21"] = na
        em.wait("vector", "tA", na)
        em.vec(lambda e: e.reciprocal(r2[:], l2[:]))
        nv = em.vec(lambda e: e.tensor_copy(r2r[:], r2[:]))
        em.wait("tensor", "tV", nv)
        for half in range(2):
            em.wait("tensor", "tA", ev.get("pX20", 0))
            nmm = em.mm(pX2[0][:], c.t_ones1[:],
                        r2r[0:1, half * 512:(half + 1) * 512], True, True,
                        inc="tP")
            em.wait("scalar", "tP", nmm)
            na = em.act_ps(lambda e, half=half: e.copy(
                rbl[:, half * 512:(half + 1) * 512], pX2[0][:]))
            ev["pX20"] = na
        em.wait("vector", "tA", ev["pX20"])
        rA = rbl[:].rearrange("p (o w) -> p o w", w=128)
        em.vec(lambda e: e.tensor_mul(vA, vA, rA))
        rB = rbl[0:64, :].rearrange("p (o w) -> p o w", w=128)
        nv = em.vec(lambda e: e.tensor_mul(vB, vB, rB))
        ev[f"pt_use{par}"] = nv
        em.wait("tensor", "tV", nv)
        em.wait("tensor", "tA", ev.get("pPV", 0))
        last = None
        for hl in range(8):
            hh = grp * 8 + hl
            p = hl // 2
            pb = (hh % 2) * 64
            dvc = (hh % 2) * 64
            rc = grp * 4 + p
            lhsA = vtl[:, qb * 1024 + rc * 128 + dvc:
                       qb * 1024 + rc * 128 + dvc + 64]
            lhsB = vtl[0:64, (qb + 1) * 1024 + rc * 128 + dvc:
                       (qb + 1) * 1024 + rc * 128 + dvc + 64]
            outap = pPV[pb:pb + 64, p * 128:(p + 1) * 128]
            tp = (0, pb)
            em.op("tensor", lambda e, o=outap, l=lhsA,
                  r=ptA[par][:, hl * 128:(hl + 1) * 128], tp=tp:
                  e.matmul(o, l, r, start=True, stop=False, tile_position=tp))
            last = em.op("tensor", lambda e, o=outap, l=lhsB,
                         r=ptB[par][:, hl * 128:(hl + 1) * 128], tp=tp:
                         e.matmul(o, l, r, start=False, stop=True,
                                  tile_position=tp),
                         inc=("tP" if hl == 7 else None))
        ev[f"pt_pv{par}"] = last
        em.wait("scalar", "tP", last)
        ov = outl[:].rearrange("p (o w) -> p o w", w=512)
        na = em.act_ps(lambda e, grp=grp, qb=qb, ov=ov: e.copy(
            ov[:, grp * 4:(grp + 1) * 4, qb * 128:(qb + 1) * 128],
            pPV[:].rearrange("p (o w) -> p o w", w=128)))
        ev["pPV"] = na

    for cc in range(4):
        hw0 = cc * 512
        for oc in range(24):
            em.wait("tensor", "tA", ev.get("pX20", 0))
            em.wait("tensor", "tA", ev.get("pX21", 0))
            lastA = lastB = None
            for i in range(NCH):
                lastA = em.mm(
                    pX2[0][:],
                    wqkv_sb[:, i * 3 * C + oc * 128:i * 3 * C + (oc + 1) * 128],
                    c.h_fm[:, i * HW + hw0:i * HW + hw0 + 512],
                    i == 0, i == NCH - 1)
                lastB = em.mm(
                    pX2[1][:, 0:64],
                    wqkv_sb[:, i * 3 * C + oc * 128:i * 3 * C + (oc + 1) * 128],
                    c.h_fm[:, i * HW + hw0 + 512:i * HW + hw0 + 576],
                    i == 0, i == NCH - 1,
                    inc=("tP" if i == NCH - 1 else None))
            em.wait("scalar", "tP", lastB)
            na = em.act_ps(lambda e, oc=oc: e.activation(
                qkvl[:, oc * 576:oc * 576 + 512], pX2[0][:],
                AF.Identity, bias=c.t_bqkv[:, oc:oc + 1], scale=1.0))
            ev["pX20"] = na
            na = em.act_ps(lambda e, oc=oc: e.activation(
                qkvl[:, oc * 576 + 512:oc * 576 + 576], pX2[1][:, 0:64],
                AF.Identity, bias=c.t_bqkv[:, oc:oc + 1], scale=1.0))
            ev["pX21"] = na
        n_qkv = em.cnt["tA"]
        if cc == 0:
            em.wait("vector", "tA", n_qkv)
            nv = None
            for oc in range(8, 24):
                nv = em.vec(lambda e, oc=oc: e.tensor_scalar_mul(
                    qkvl[:, oc * 576:oc * 576 + 64],
                    qkvl[:, oc * 576:oc * 576 + 64], c.t_hm[:, 0:1]))
            em.wait("tensor", "tV", nv)
        em.wait("tensor", "tA", n_qkv)
        for vc in range(8):
            for tt in range(5):
                w = 128 if tt < 4 else 64
                par = (vc * 5 + tt) % 2
                em.wait("tensor", "tA", ev.get(f"pX2{par}", 0))
                ntr = em.tr(pX2[par][:].bitcast(BF16)[0:w, 0:128],
                            qkvl[:, (16 + vc) * 576 + tt * 128:
                                 (16 + vc) * 576 + tt * 128 + w],
                            c.t_idb[:], inc="tP")
                em.wait("scalar", "tP", ntr)
                na = em.act_ps(lambda e, par=par, vc=vc, tt=tt, w=w: e.copy(
                    vtl[0:w, tt * 1024 + vc * 128:tt * 1024 + (vc + 1) * 128],
                    pX2[par][:].bitcast(BF16)[0:w, 0:128]))
                ev[f"pX2{par}"] = na
        n_vtl = em.cnt["tA"]
        em.wait("tensor", "tA", n_vtl)
        its = [(qb, grp) for qb in range(4) for grp in range(2)]
        ns = emit_scores(cc, *its[0])
        pend_exp = emit_exp(0, ns)
        for i, it in enumerate(its):
            ne = pend_exp
            if i + 1 < len(its):
                ns = emit_scores(cc, *its[i + 1])
                pend_exp = emit_exp((i + 1) % 2, ns)
            emit_rest(cc, it[0], it[1], i % 2, ne)
        for oc in range(8):
            par = oc % 2
            em.wait("tensor", "tA", ev.get(f"pX2{par}", 0))
            last = None
            for i in range(NCH):
                last = em.mm(pX2[par][:],
                             wout_sb[:, i * C + oc * 128:i * C + (oc + 1) * 128],
                             outl[:, i * 512:(i + 1) * 512],
                             i == 0, i == NCH - 1,
                             inc=("tP" if i == NCH - 1 else None))
            em.wait("scalar", "tP", last)
            em.wait("scalar", "tV", ev.get(f"xat{par}", 0))
            na = em.act_ps(lambda e, par=par: e.copy(xat[par][:], pX2[par][:]))
            ev[f"pX2{par}"] = na
            nd = em.dma("sync", xnt[par][:],
                        c.xnscr[oc, :, cc * 512:(cc + 1) * 512])
            em.wait("vector", "tA", na)
            em.wait("vector", "dS", nd)
            em.wait("vector", "dS", ev.get(f"xas{par}", 0))
            nv = em.vec(lambda e, par=par, oc=oc: e.scalar_tensor_tensor(
                xas[par][:], xat[par][:], c.t_bout[:, oc:oc + 1],
                xnt[par][:], ALU.add, ALU.add))
            ev[f"xat{par}"] = nv
            em.wait("sync", "tV", nv)
            nd = em.dma("sync", c.xascr[oc, :, cc * 512:(cc + 1) * 512],
                        xas[par][:])
            ev[f"xas{par}"] = nd

    if DBG:
        c.phase_barrier()
        dbg = nc.dram_tensor("dbg_h2", [128, NCH * HW], BF16,
                             kind="ExternalOutput").ap()
        nd = em.dma("sync", dbg, c.h_fm[:])
        em.wait("sync", "dS", nd)

    c.phase_barrier()
    p2.close()


def emit_phase3(c):
    nc, em = c.nc, c.em
    p3 = ExitStack()
    E3 = p3.enter_context
    w2sb = E3(nc.sbuf_tensor("w2sb", [128, 32 * C], BF16))
    w1g = [E3(nc.sbuf_tensor(f"w1g{i}", [128, NCH * 512], BF16))
           for i in range(2)]
    xat3 = E3(nc.sbuf_tensor("xat3", [128, NCH * 512], F32))
    xab = E3(nc.sbuf_tensor("xab", [128, 4096], BF16))
    xsqb = E3(nc.sbuf_tensor("xsqb", [128, 4096], BF16))
    h2b = E3(nc.sbuf_tensor("h2b", [128, 4096], BF16))
    gsb = E3(nc.sbuf_tensor("gsb", [128, 32 * 512], BF16))
    st2 = E3(nc.sbuf_tensor("st2", [2, 512], F32))
    st3 = E3(nc.sbuf_tensor("st3", [1, 512 * 6], F32))
    strr = E3(nc.sbuf_tensor("strr", [1, 1024], F32R))
    murep = E3(nc.sbuf_tensor("murep", [128, 512], F32))
    rsrep = E3(nc.sbuf_tensor("rsrep", [128, 512], F32))
    t3a = E3(nc.sbuf_tensor("t3a", [128, 512], F32))
    t3b = E3(nc.sbuf_tensor("t3b", [128, 512], F32))
    m2t = [E3(nc.sbuf_tensor(f"m2t{i}", [128, 512], F32)) for i in range(2)]
    xnt3 = [E3(nc.sbuf_tensor(f"xnt3{i}", [128, 512], F32)) for i in range(2)]
    acs = [E3(nc.sbuf_tensor(f"acs{i}", [128, 512], F32)) for i in range(2)]
    p3F = [E3(nc.psum_tensor(f"p3F{i}", [128, 512], F32)) for i in range(4)]
    p3L = E3(nc.psum_tensor("p3L", [128, 512], F32))
    p3X = E3(nc.psum_tensor("p3X", [128, 512], F32))

    ev = c.bank_evac
    nw_ = None
    for r in range(32):
        nw_ = em.dma("gpsimd", w2sb[:, r * C:(r + 1) * C],
                     c.w2[r * 128:(r + 1) * 128, :])
    n_w2 = nw_

    def load_w1g(grp, par):
        n = None
        em.wait("gpsimd", "tP", ev.get(f"w1g{par}", 0))
        for r in range(8):
            n = em.dma("gpsimd", w1g[par][:, r * 512:(r + 1) * 512],
                       c.w1[r * 128:(r + 1) * 128, grp * 512:(grp + 1) * 512])
        return n

    w1_ld = load_w1g(0, 0)

    for qt in range(4):
        em.wait("sync", "tV", ev.get("xat3_use", 0))
        nd = None
        for ch in range(NCH):
            nd = em.dma("sync", xat3[:, ch * 512:(ch + 1) * 512],
                        c.xascr[ch, :, qt * 512:(qt + 1) * 512])
        em.wait("vector", "dS", nd)
        nv = em.vec(lambda e: e.tensor_copy(xab[:], xat3[:]))
        em.wait("scalar", "tV", nv)
        na = em.act(lambda e: e.activation(xsqb[:], xab[:], AF.Square,
                                           bias=0.0, scale=1.0))
        em.wait("tensor", "tA", na)
        em.wait("tensor", "tV", nv)
        em.wait("tensor", "tA", ev.get("p3L", 0))
        last = None
        for ch in range(NCH):
            em.mm(p3L[0:2, :], c.t_sel[:, 0:2], xab[:, ch * 512:(ch + 1) * 512],
                  ch == 0, False)
            last = em.mm(p3L[0:2, :], c.t_sel[:, 4:6],
                         xsqb[:, ch * 512:(ch + 1) * 512],
                         False, ch == NCH - 1,
                         inc=("tP" if ch == NCH - 1 else None))
        em.wait("scalar", "tP", last)
        na = em.act_ps(lambda e: e.copy(st2[:], p3L[0:2, :]))
        ev["p3L"] = na
        em.wait("sync", "tA", na)
        nd = em.dma("sync", st3[0:1, 512:1024], st2[1:2, :])
        nd = em.dma("sync", st3[0:1, 0:512], st2[0:1, :])
        em.wait("vector", "dS", nd)
        em.vec(lambda e: e.tensor_scalar_mul(st3[0:1, 0:1024],
                                             st3[0:1, 0:1024], 1.0 / C))
        em.vec(lambda e: e.tensor_mul(st3[0:1, 1024:1536], st3[0:1, 0:512],
                                      st3[0:1, 0:512]))
        nv = em.vec(lambda e: e.tensor_sub(st3[0:1, 1536:2048],
                                           st3[0:1, 512:1024],
                                           st3[0:1, 1024:1536]))
        em.wait("scalar", "tV", nv)
        na = em.act(lambda e: e.activation(st3[0:1, 2048:2560],
                                           st3[0:1, 1536:2048], AF.Sqrt,
                                           bias=c.t_eps[0:1, :], scale=1.0))
        em.wait("vector", "tA", na)
        em.vec(lambda e: e.reciprocal(st3[0:1, 2560:3072],
                                      st3[0:1, 2048:2560]))
        em.vec(lambda e: e.tensor_copy(strr[0:1, 0:512], st3[0:1, 0:512]))
        nv = em.vec(lambda e: e.tensor_copy(strr[0:1, 512:1024],
                                            st3[0:1, 2560:3072]))
        em.wait("tensor", "tV", nv)
        for k, dst in ((0, murep), (1, rsrep)):
            em.wait("tensor", "tA", ev.get("p3X", 0))
            nmm = em.mm(p3X[:], c.t_ones1[:], strr[0:1, k * 512:(k + 1) * 512],
                        True, True, inc="tP")
            em.wait("scalar", "tP", nmm)
            na = em.act_ps(lambda e, dst=dst: e.copy(dst[:], p3X[:]))
            ev["p3X"] = na
        em.wait("vector", "tA", ev["p3X"])
        nvh = None
        for ch in range(NCH):
            em.vec(lambda e, ch=ch: e.tensor_sub(
                t3a[:], xat3[:, ch * 512:(ch + 1) * 512], murep[:]))
            em.vec(lambda e: e.tensor_mul(t3b[:], t3a[:], rsrep[:]))
            nvh = em.vec(lambda e, ch=ch: e.scalar_tensor_tensor(
                h2b[:, ch * 512:(ch + 1) * 512], t3b[:],
                c.t_ln2g[:, ch:ch + 1],
                c.t_ln2b[:, ch:ch + 1].broadcast_to([128, 512]),
                ALU.mult, ALU.add))
        em.wait("tensor", "tV", nvh)
        for grp in range(8):
            par = grp % 2
            em.wait("tensor", "dG", w1_ld)
            have_next = (grp < 7) or (qt < 3)
            if have_next:
                w1_ld_next = load_w1g((grp + 1) % 8, 1 - par)
            for dh in range(4):
                em.wait("tensor", "tA", ev.get(f"p3F{dh}", 0))
                last = None
                for i in range(NCH):
                    last = em.mm(p3F[dh][:],
                                 w1g[par][:, i * 512 + dh * 128:
                                          i * 512 + (dh + 1) * 128],
                                 h2b[:, i * 512:(i + 1) * 512],
                                 i == 0, i == NCH - 1,
                                 inc=("tP" if i == NCH - 1 else None))
                em.wait("scalar", "tP", last)
                na = em.act_ps(lambda e, dh=dh, grp=grp: e.activation(
                    gsb[:, (grp * 4 + dh) * 512:(grp * 4 + dh + 1) * 512],
                    p3F[dh][:], AF.Gelu,
                    bias=c.t_b1[:, grp * 4 + dh:grp * 4 + dh + 1], scale=1.0))
                ev[f"p3F{dh}"] = na
            ev[f"w1g{par}"] = em.cnt["tP"]
            if have_next:
                w1_ld = w1_ld_next
        n_gelu = em.cnt["tA"]
        em.wait("tensor", "tA", n_gelu)
        em.wait("tensor", "dG", n_w2)
        for oc in range(NCH):
            par = oc % 2
            em.wait("tensor", "tA", ev.get(f"p3F{par}", 0))
            last = None
            for i in range(32):
                last = em.mm(p3F[par][:],
                             w2sb[:, i * C + oc * 128:i * C + (oc + 1) * 128],
                             gsb[:, i * 512:(i + 1) * 512],
                             i == 0, i == 31,
                             inc=("tP" if i == 31 else None))
            em.wait("scalar", "tP", last)
            em.wait("scalar", "tV", ev.get(f"m2t{par}", 0))
            na = em.act_ps(lambda e, par=par: e.copy(m2t[par][:], p3F[par][:]))
            ev[f"p3F{par}"] = na
            nd = em.dma("sync", xnt3[par][:],
                        c.xnscr[oc, :, qt * 512:(qt + 1) * 512])
            em.wait("vector", "tA", na)
            em.wait("vector", "dS", nd)
            em.wait("vector", "dS", ev.get(f"acs{par}", 0))
            em.vec(lambda e, par=par, oc=oc: e.scalar_tensor_tensor(
                acs[par][:], m2t[par][:], c.t_b2[:, oc:oc + 1],
                xat3[:, oc * 512:(oc + 1) * 512], ALU.add, ALU.add))
            nv = em.vec(lambda e, par=par: e.tensor_add(
                acs[par][:], acs[par][:], xnt3[par][:]))
            ev[f"m2t{par}"] = nv
            em.wait("sync", "tV", nv)
            nd = em.dma("sync", c.accscr[oc, :, qt * 512:(qt + 1) * 512],
                        acs[par][:])
            ev[f"acs{par}"] = nd
        ev["xat3_use"] = em.cnt["tV"]

    c.phase_barrier()
    p3.close()


def emit_phase4(c):
    nc, em = c.nc, c.em
    p4 = ExitStack()
    E4 = p4.enter_context
    qh = [E4(nc.sbuf_tensor(f"qh{i}", [128, SEG], BF16)) for i in range(2)]
    kh = [E4(nc.sbuf_tensor(f"kh{i}", [128, T], BF16)) for i in range(2)]
    vh = [E4(nc.sbuf_tensor(f"vh{i}", [128, T], BF16)) for i in range(2)]
    pt = [E4(nc.sbuf_tensor(f"pt{i}", [128, 8 * 1024], BF16)) for i in range(2)]
    outh = E4(nc.sbuf_tensor("outh", [128, SEG], F32))
    p4tmp = E4(nc.sbuf_tensor("p4tmp", [128, 512], F32))
    lrow = E4(nc.sbuf_tensor("lrow", [4, 512], F32))
    rrow = E4(nc.sbuf_tensor("rrow", [4, 512], F32))
    rrowr = E4(nc.sbuf_tensor("rrowr", [4, 512], F32R))
    rflat = E4(nc.sbuf_tensor("rflat", [1, 512], F32R))
    rbc = E4(nc.sbuf_tensor("rbc", [128, SEG], F32))
    c.attn_out = c.ex.enter_context(
        nc.sbuf_tensor("attn_out", [128, NCH * SEG], BF16, side="right"))
    pA = E4(nc.psum_tensor("pA", [128, 1024], F32))
    pB = E4(nc.psum_tensor("pB", [128, 1024], F32))
    pO = [E4(nc.psum_tensor(f"pO{i}", [128, 512], F32)) for i in range(2)]
    pL = E4(nc.psum_tensor("pL", [128, 512], F32))
    pX = E4(nc.psum_tensor("pX", [128, 512], F32))

    ev = c.bank_evac
    kv_use = [0, 0]
    ld_cnt = [0, 0]

    def load_head(h):
        par = h % 2
        em.wait("sync", "tP", kv_use[par])
        em.dma("sync", qh[par][:], c.qscr[h * 128:(h + 1) * 128, :])
        em.dma("sync", kh[par][:], c.kscr[h * 128:(h + 1) * 128, :])
        vsrc = c.vscr[h].rearrange("(kt p) d -> p kt d", p=128)
        vdst = vh[par][:].rearrange("p (kt d) -> p kt d", d=128)
        n = None
        for g in range(8):
            n = em.dma("sync", vdst[:, g * 8:(g + 1) * 8, :],
                       vsrc[:, g * 8:(g + 1) * 8, :])
        ld_cnt[par] = n

    load_head(0)
    first_exp_emitted = False

    for h in range(8):
        par = h % 2
        if h < 7:
            load_head(h + 1)
        em.wait("tensor", "dS", ld_cnt[par])
        pvl_done = {}
        exp_done = {}

        def sc_exp(qhalf, sbi, gi):
            nonlocal first_exp_emitted
            last_exp = 0
            for kk in range(8):
                gk = sbi * 8 + kk
                pS, psname = (pA, "pA") if gk % 2 == 0 else (pB, "pB")
                em.wait("tensor", "tA", ev.get(psname, 0))
                em.mm(pS[:, 0:512], kh[par][:, gk * 128:(gk + 1) * 128],
                      qh[par][:, qhalf * 1024:qhalf * 1024 + 512],
                      True, True)
                nmm = em.mm(pS[:, 512:1024],
                            kh[par][:, gk * 128:(gk + 1) * 128],
                            qh[par][:, qhalf * 1024 + 512:(qhalf + 1) * 1024],
                            True, True, inc="tP")
                if not first_exp_emitted:
                    em.wait("scalar", "tP", nmm)
                    na = em.act_ps(lambda e: e.copy(p4tmp[:], pA[:, 0:512]))
                    em.wait("vector", "tA", na)
                    nv = em.vec(lambda e: e.reduce_max(
                        c.t_c1[:], p4tmp[:], AX.X))
                    em.wait("tensor", "tV", nv)
                    ntr = em.tr(pX[0:1, 0:128], c.t_c1[:], c.t_id32[:],
                                inc="tP")
                    em.wait("scalar", "tP", ntr)
                    na = em.act_ps(lambda e: e.copy(c.t_c2[:], pX[0:1, 0:128]))
                    em.wait("vector", "tA", na)
                    em.vec(lambda e: e.reduce_max(c.t_c3[:], c.t_c2[:], AX.X))
                    em.vec(lambda e: e.tensor_scalar_mul(
                        c.t_c3[:], c.t_c3[:], -SCALE_F))
                    nv = em.vec(lambda e: e.tensor_copy(
                        c.t_c3b[:], c.t_c3[:]))
                    em.wait("tensor", "tV", nv)
                    em.wait("tensor", "tA", ev.get("pX", 0))
                    nmm2 = em.mm(pX[:, 0:1], c.t_onesrb[:],
                                 c.t_c3b[:], True, True, inc="tP")
                    em.wait("scalar", "tP", nmm2)
                    na = em.act_ps(lambda e: e.copy(c.t_cb4[:], pX[:, 0:1]))
                    ev["pX"] = na
                    first_exp_emitted = True
                em.wait("scalar", "tP", nmm)
                em.wait("scalar", "tP", pvl_done.get(gi - 2, 0))
                na = em.act_ps(lambda e, pS=pS, gi=gi, kk=kk: e.activation(
                    pt[gi % 2][:, kk * 1024:(kk + 1) * 1024], pS[:],
                    AF.Exp, bias=c.t_cb4[:], scale=SCALE_F))
                ev[psname] = na
                last_exp = na
            exp_done[gi] = last_exp

        def pv_l(qhalf, sbi, gi):
            em.wait("tensor", "tA", exp_done[gi])
            if sbi == 0:
                em.wait("tensor", "tA", ev.get("pO0", 0))
                em.wait("tensor", "tA", ev.get("pO1", 0))
                if qhalf == 0:
                    em.wait("tensor", "tA", ev.get("pL", 0))
            last = None
            for qtl in range(2):
                qt = qhalf * 2 + qtl
                for kk in range(8):
                    gk = sbi * 8 + kk
                    em.mm(pO[qtl][:],
                          vh[par][:, gk * 128:(gk + 1) * 128],
                          pt[gi % 2][:, kk * 1024 + qtl * 512:
                                     kk * 1024 + (qtl + 1) * 512],
                          start=(sbi == 0 and kk == 0),
                          stop=(sbi == 7 and kk == 7))
                    last = em.mm(pL[0:4, :],
                                 c.t_sel[:, qt * 4:(qt + 1) * 4],
                                 pt[gi % 2][:, kk * 1024 + qtl * 512:
                                            kk * 1024 + (qtl + 1) * 512],
                                 start=(qhalf == 0 and sbi == 0 and kk == 0
                                        and qtl == 0),
                                 stop=(qhalf == 1 and sbi == 7 and kk == 7
                                       and qtl == 1),
                                 inc=("tP" if (qtl == 1 and kk == 7)
                                      else None))
            pvl_done[gi] = last

        gis = [(qh_, sb_) for qh_ in range(2) for sb_ in range(8)]
        sc_exp(*gis[0], 0)
        for i, (qh_, sb_) in enumerate(gis):
            if i + 1 < len(gis):
                sc_exp(*gis[i + 1], i + 1)
            pv_l(qh_, sb_, i)
            if sb_ == 7:
                em.wait("scalar", "tP", pvl_done[i])
                for qtl in range(2):
                    qt = qh_ * 2 + qtl
                    na = em.act_ps(lambda e, qtl=qtl, qt=qt: e.copy(
                        outh[:, qt * 512:(qt + 1) * 512], pO[qtl][:]))
                    ev[f"pO{qtl}"] = na
        em.wait("scalar", "tP", pvl_done[15])
        na = em.act_ps(lambda e: e.copy(lrow[:], pL[0:4, :]))
        ev["pL"] = na
        em.wait("vector", "tA", na)
        em.vec(lambda e: e.reciprocal(rrow[:], lrow[:]))
        nv = em.vec(lambda e: e.tensor_copy(rrowr[:], rrow[:]))
        rb_ev = []
        for qt in range(4):
            em.wait("sync", "tV", nv)
            em.wait("sync", "tP", ev.get("rflat_use", 0))
            nd = em.dma("sync", rflat[:], rrowr[qt:qt + 1, :])
            em.wait("tensor", "dS", nd)
            em.wait("tensor", "tA", ev.get("pX", 0))
            nmm = em.mm(pX[:], c.t_cmcol[0:1, h * 128:(h + 1) * 128],
                        rflat[0:1, :], True, True, inc="tP")
            ev["rflat_use"] = nmm
            em.wait("scalar", "tP", nmm)
            na = em.act_ps(lambda e, qt=qt: e.copy(
                rbc[:, qt * 512:(qt + 1) * 512], pX[:]))
            ev["pX"] = na
            rb_ev.append(na)
        em.wait("vector", "tA", rb_ev[-1])
        nv = em.vec(lambda e, h=h: e.tensor_mul(
            c.attn_out[:, h * SEG:(h + 1) * SEG], outh[:], rbc[:]))
        kv_use[par] = em.cnt["tP"]
        em.wait("scalar", "tV", nv)

    if DBG:
        c.phase_barrier()
        dbg = nc.dram_tensor("dbg_attn", [128, NCH * SEG], BF16,
                             kind="ExternalOutput").ap()
        nd = em.dma("sync", dbg, c.attn_out[:])
        em.wait("sync", "dS", nd)

    c.phase_barrier()
    p4.close()


def emit_phase5(c):
    nc, em = c.nc, c.em
    p5 = ExitStack()
    E5 = p5.enter_context
    owT = E5(nc.sbuf_tensor("owT", [128, NCH * C], BF16))
    wpan5 = E5(nc.sbuf_tensor("wpan5", [128, C], BF16))
    y2sb = E5(nc.sbuf_tensor("y2sb", [128, NCH * 512], F32))
    ot5 = [E5(nc.sbuf_tensor(f"ot5{i}", [128, 512], F32)) for i in range(2)]
    acct = [E5(nc.sbuf_tensor(f"acct{i}", [128, 512], F32)) for i in range(2)]
    ytile = [E5(nc.sbuf_tensor(f"ytile{i}", [128, C], F32)) for i in range(2)]
    yout = [E5(nc.sbuf_tensor(f"yout{i}", [128, C], F32)) for i in range(2)]
    xqt = [E5(nc.sbuf_tensor(f"xqt{i}", [128, C], F32)) for i in range(2)]
    p5a = [E5(nc.psum_tensor(f"p5a{i}", [128, 512], F32)) for i in range(2)]
    p5t = [E5(nc.psum_tensor(f"p5t{i}", [128, 512], F32)) for i in range(2)]
    ev = c.bank_evac

    for r in range(8):
        em.wait("gpsimd", "tP", em.cnt["tP"])
        n = em.dma("gpsimd", wpan5[:], c.ow[r * 128:(r + 1) * 128, :])
        em.wait("tensor", "dG", n)
        for g in range(2):
            em.wait("tensor", "tA", ev.get(f"p5t{g}", 0))
            last = None
            for i in range(4):
                mc = g * 4 + i
                last = em.tr(p5t[g][:].bitcast(BF16)[:, i * 128:(i + 1) * 128],
                             wpan5[:, mc * 128:(mc + 1) * 128],
                             c.t_idb[:], inc=(None if i < 3 else "tP"))
            em.wait("scalar", "tP", last)
            dv = owT[:].rearrange("p (mc w) -> p mc w", w=C)
            na = em.act_ps(lambda e, g=g, r=r, dv=dv: e.copy(
                dv[:, g * 4:g * 4 + 4, r * 128:(r + 1) * 128],
                p5t[g][:].bitcast(BF16)[:, 0:512].rearrange(
                    "p (a w) -> p a w", w=128)))
            ev[f"p5t{g}"] = na

    if P5_PART < 1.5:
        nd = em.dma("sync", c.y[0:128, :], c.xq[0:128, :])
        em.wait("sync", "dS", nd)
        c.phase_barrier()
        p5.close()
        return
    em.wait("tensor", "tA", em.cnt["tA"])
    nlast_y = None
    for qt in range(4):
        em.wait("vector", "tP", ev.get("y2sb_use", 0))
        for oc in range(NCH):
            par = oc % 2
            em.wait("tensor", "tA", ev.get(f"p5a{par}", 0))
            last = None
            for i in range(NCH):
                last = em.mm(p5a[par][:],
                             owT[:, i * C + oc * 128:i * C + (oc + 1) * 128],
                             c.attn_out[:, i * SEG + qt * 512:
                                        i * SEG + (qt + 1) * 512],
                             i == 0, i == NCH - 1,
                             inc=("tP" if i == NCH - 1 else None))
            em.wait("scalar", "tP", last)
            em.wait("scalar", "tV", ev.get(f"ot5{par}", 0))
            na = em.act_ps(lambda e, par=par: e.copy(ot5[par][:], p5a[par][:]))
            ev[f"p5a{par}"] = na
            em.wait("sync", "tV", ev.get(f"acct{par}", 0))
            nd = em.dma("sync", acct[par][:],
                        c.accscr[oc, :, qt * 512:(qt + 1) * 512])
            em.wait("vector", "tA", na)
            em.wait("vector", "dS", nd)
            em.vec(lambda e, par=par, oc=oc: e.scalar_tensor_tensor(
                y2sb[:, oc * 512:(oc + 1) * 512], ot5[par][:],
                c.t_ob[:, oc:oc + 1],
                c.t_cm[:, oc:oc + 1].broadcast_to([128, 512]),
                ALU.add, ALU.mult))
            nv = em.vec(lambda e, par=par, oc=oc: e.tensor_add(
                y2sb[:, oc * 512:(oc + 1) * 512],
                y2sb[:, oc * 512:(oc + 1) * 512], acct[par][:]))
            ev[f"ot5{par}"] = nv
            ev[f"acct{par}"] = nv
        n_y2 = em.cnt["tV"]
        em.wait("tensor", "tV", n_y2)
        if P5_PART < 2.5:
            continue
        for tt in range(4):
            ti = qt * 4 + tt
            par = tt % 2
            em.wait("sync", "tV", ev.get(f"xqt{par}", 0))
            ndx = em.dma("sync", xqt[par][:],
                         c.xq[ti * 128:(ti + 1) * 128, :])
            for g in range(2):
                em.wait("tensor", "tA", ev.get(f"p5t{g}", 0))
                last = None
                for i in range(4):
                    ch = g * 4 + i
                    last = em.tr(p5t[g][:, i * 128:(i + 1) * 128],
                                 y2sb[:, ch * 512 + tt * 128:
                                      ch * 512 + (tt + 1) * 128],
                                 c.t_id32[:], inc=(None if i < 3 else "tP"))
                em.wait("scalar", "tP", last)
                em.wait("scalar", "tV", ev.get(f"ytile{par}", 0))
                na = em.act_ps(lambda e, g=g, par=par: e.copy(
                    ytile[par][:, g * 512:(g + 1) * 512], p5t[g][:]))
                ev[f"p5t{g}"] = na
            if P5_PART < 2.8:
                continue
            em.wait("vector", "tA", em.cnt["tA"])
            em.wait("vector", "dS", ndx)
            em.wait("vector", "dS", ev.get(f"yout{par}", 0))
            nv = em.vec(lambda e, par=par: e.tensor_add(
                yout[par][:], ytile[par][:], xqt[par][:]))
            ev[f"ytile{par}"] = nv
            ev[f"xqt{par}"] = nv
            em.wait("sync", "tV", nv)
            nlast_y = em.dma("sync", c.y[ti * 128:(ti + 1) * 128, :],
                             yout[par][:])
            ev[f"yout{par}"] = nlast_y
        ev["y2sb_use"] = em.cnt["tP"]

    if P5_PART < 2.9:
        nlast_y = None
    if nlast_y is None:
        nlast_y = em.dma("sync", c.y[0:128, :], c.xq[0:128, :])
    em.wait("sync", "dS", nlast_y)
    c.phase_barrier()
    p5.close()


def make_masks():
    mA = np.zeros((128, 128), dtype=np.float32)
    mB = np.zeros((64, 128), dtype=np.float32)
    for r in range(128):
        for cc in range(192):
            ok = (cc <= 64 + r) and (r < 64 or cc >= 64)
            if ok:
                if cc < 128:
                    mA[cc, r] = 1.0
                else:
                    mB[cc - 128, r] = 1.0
    return mA.astype(ml_dtypes.bfloat16), mB.astype(ml_dtypes.bfloat16)


def make_selmat():
    s = np.zeros((128, 4, 4), dtype=np.float32)
    for t in range(4):
        s[:, t, t] = 1.0
    return s.reshape(128, 16).astype(ml_dtypes.bfloat16)


_program_cache = {}


def get_program(stages=STAGES):
    key = (stages, DBG)
    if key not in _program_cache:
        _program_cache[key] = build_program(stages)
    return _program_cache[key]


def make_inmaps(inputs):
    x = np.ascontiguousarray(inputs["L0_attn_x"], dtype=np.float32)
    mA, mB = make_masks()
    common = {
        "wqkv": np.ascontiguousarray(inputs["L0_attn_w_qkv"]),
        "bqkv": np.ascontiguousarray(inputs["L0_attn_b_qkv"]),
        "wout": np.ascontiguousarray(inputs["L0_attn_w_out"]),
        "bout": np.ascontiguousarray(inputs["L0_attn_b_out"]),
        "ln1g": np.ascontiguousarray(inputs["L0_attn_ln1_g"]),
        "ln1b": np.ascontiguousarray(inputs["L0_attn_ln1_b"]),
        "ln2g": np.ascontiguousarray(inputs["L0_attn_ln2_g"]),
        "ln2b": np.ascontiguousarray(inputs["L0_attn_ln2_b"]),
        "w1": np.ascontiguousarray(inputs["L0_attn_w1"]),
        "b1": np.ascontiguousarray(inputs["L0_attn_b1"]),
        "w2": np.ascontiguousarray(inputs["L0_attn_w2"]),
        "b2": np.ascontiguousarray(inputs["L0_attn_b2"]),
        "nw": np.ascontiguousarray(inputs["L0_ffn_norm_w"]),
        "nb": np.ascontiguousarray(inputs["L0_ffn_norm_b"]),
        "qw": np.ascontiguousarray(inputs["L0_ffn_q_w"]),
        "qbias": np.ascontiguousarray(inputs["L0_ffn_q_b"]),
        "kw": np.ascontiguousarray(inputs["L0_ffn_k_w"]),
        "kbias": np.ascontiguousarray(inputs["L0_ffn_k_b"]),
        "vw": np.ascontiguousarray(inputs["L0_ffn_v_w"]),
        "vbias": np.ascontiguousarray(inputs["L0_ffn_v_b"]),
        "ow": np.ascontiguousarray(inputs["L0_ffn_o_w"]),
        "obias": np.ascontiguousarray(inputs["L0_ffn_o_b"]),
        "ident32": np.eye(128, dtype=np.float32),
        "identb": np.eye(128, dtype=ml_dtypes.bfloat16),
        "ones1": np.ones((1, 128), dtype=np.float32),
        "onesb": np.ones((128, 1), dtype=ml_dtypes.bfloat16),
        "onesrb": np.ones((1, 128), dtype=ml_dtypes.bfloat16),
        "selmat": make_selmat(),
        "maskA": mA, "maskB": mB,
    }
    in_maps = []
    for core in range(8):
        b, e = core // 4, core % 4
        m_e = C >> e
        cm = np.zeros(C, dtype=np.float32); cm[:m_e] = 1.0
        d = dict(common)
        d["xb"] = x[b]
        d["xq"] = x[b, e * SEG:(e + 1) * SEG]
        d["xh"] = (x[b, e * SEG - 64:e * SEG] if e > 0
                   else np.zeros((64, C), dtype=np.float32))
        d["cmaskv"] = cm
        d["cmcol"] = cm[None, :]
        d["hmaskv"] = np.full((128, 1), 1.0 if e > 0 else 0.0, dtype=np.float32)
        in_maps.append(d)
    return in_maps


def kernel(**inputs):
    nc = get_program()
    in_maps = make_inmaps(inputs)
    res = run_bass_kernel_spmd(nc, in_maps, core_ids=list(range(8)))
    out = np.empty((B, T, C), dtype=np.float32)
    for core in range(8):
        b, e = core // 4, core % 4
        out[b, e * SEG:(e + 1) * SEG] = res.results[core]["y"]
    return out


# version 12
# speedup vs baseline: 1.0621x; 1.0621x over previous
import numpy as np
import ml_dtypes
from contextlib import ExitStack
from types import SimpleNamespace as NS
import concourse.bass as bass
import concourse.mybir as mybir
from concourse.bass_utils import run_bass_kernel_spmd

F32 = mybir.dt.float32
BF16 = mybir.dt.bfloat16
F32R = mybir.dt.float32r
AF = mybir.ActivationFunctionType
ALU = mybir.AluOpType
AX = mybir.AxisListType

DBG = False
STAGES = 5
PARANOID = False
P5_PART = 3

B, T, C = 2, 8192, 1024
SEG = 2048
NCH = C // 128
MCH_J = [8, 4, 2, 1]
NQT = SEG // 512
NTT = SEG // 128
EPS = 1e-5
SCALE_F = 1.0 / 32.0
SCALE_L = 0.125


class EM:
    ENGS = ("sync", "gpsimd", "tensor", "vector", "scalar")

    def __init__(self, nc, ex):
        self.nc = nc
        self.ops = {k: [] for k in self.ENGS}
        self.sems = {}
        self.cnt = {}
        self.lastw = {k: {} for k in self.ENGS}
        for s in ("dS", "dG", "tP", "tA", "tV"):
            self.sems[s] = ex.enter_context(nc.semaphore(s))
            self.cnt[s] = 0

    def raw(self, eng, fn):
        self.ops[eng].append(fn)

    def wait(self, eng, sem, val):
        if val is None or val <= 0:
            return
        if self.lastw[eng].get(sem, 0) >= val:
            return
        self.lastw[eng][sem] = val
        h = self.sems[sem]
        self.raw(eng, lambda e, h=h, v=val: e.wait_ge(h, v))

    def op(self, eng, fn, inc=None, amt=1):
        if PARANOID is True or (PARANOID and eng in PARANOID):
            sems = ("dS", "dG", "tP", "tA", "tV")
            if isinstance(PARANOID, dict):
                sems = PARANOID[eng]
            for s in sems:
                self.wait(eng, s, self.cnt[s])
            if inc is None:
                inc = {"sync": "dS", "gpsimd": "dG", "tensor": "tP",
                       "vector": "tV", "scalar": "tA"}[eng]
                amt = 16 if inc in ("dS", "dG") else 1
                if eng in ("sync", "gpsimd"):
                    pass
        if inc is None:
            self.raw(eng, fn)
            return None
        self.cnt[inc] += amt
        h = self.sems[inc]
        self.raw(eng, lambda e, fn=fn, h=h, a=amt: fn(e).then_inc(h, a))
        return self.cnt[inc]

    def dma(self, q, out, in_, slow=False):
        inc = "dS" if q == "sync" else "dG"
        return self.op(q, lambda e, o=out, i=in_, s=slow: e.dma_start(
            out=o, in_=i, allow_slow_non_contiguous=s), inc=inc, amt=16)

    def mm(self, out, lhsT, rhs, start, stop, inc=None, tile=None):
        return self.op("tensor",
                       lambda e, o=out, l=lhsT, r=rhs, s0=start, s1=stop,
                       tp=tile: e.matmul(o, l, r, start=s0, stop=s1,
                                         tile_position=tp), inc=inc)

    def tr(self, out, in_, ident, inc=None):
        return self.op("tensor",
                       lambda e, o=out, i=in_, d=ident: e.transpose(o, i, d),
                       inc=inc)

    def act_ps(self, fn):
        n = self.op("scalar", fn, inc="tA")
        self.wait("scalar", "tA", n - 2)
        return n

    def act(self, fn):
        return self.op("scalar", fn, inc="tA")

    def vec(self, fn):
        self.wait("vector", "tV", self.cnt["tV"])
        return self.op("vector", fn, inc="tV")


def build_program(stages=STAGES):
    nc = bass.Bass("TRN2", target_bir_lowering=False)
    ex = ExitStack()
    c = NS()
    c.nc, c.ex = nc, ex

    di = lambda n, s, d=F32: nc.dram_tensor(n, s, d, kind="ExternalInput").ap()
    do = lambda n, s, d=F32: nc.dram_tensor(n, s, d, kind="ExternalOutput").ap()
    dscr = lambda n, s, d=F32: nc.dram_tensor(
        n, s, d, kind=("ExternalOutput" if DBG else "Internal")).ap()

    c.xb = di("xb", [T, C]); c.xq = di("xq", [SEG, C]); c.xh = di("xh", [64, C])
    c.wqkv = di("wqkv", [C, 3 * C]); c.bqkv = di("bqkv", [3 * C])
    c.wout = di("wout", [C, C]); c.bout = di("bout", [C])
    c.ln1g = di("ln1g", [C]); c.ln1b = di("ln1b", [C])
    c.ln2g = di("ln2g", [C]); c.ln2b = di("ln2b", [C])
    c.w1 = di("w1", [C, 4 * C]); c.b1 = di("b1", [4 * C])
    c.w2 = di("w2", [4 * C, C]); c.b2 = di("b2", [C])
    c.nw = di("nw", [C]); c.nb = di("nb", [C])
    c.qw = di("qw", [C, C]); c.qbias = di("qbias", [C])
    c.kw = di("kw", [C, C]); c.kbias = di("kbias", [C])
    c.vw = di("vw", [C, C]); c.vbias = di("vbias", [C])
    c.ow = di("ow", [C, C]); c.obias = di("obias", [C])
    c.ident32 = di("ident32", [128, 128])
    c.identb = di("identb", [128, 128], BF16)
    c.ones1 = di("ones1", [1, 128])
    c.onesb = di("onesb", [128, 1], BF16)
    c.onesrb = di("onesrb", [1, 128], BF16)
    c.selmat = di("selmat", [128, 16], BF16)
    c.maskA = di("maskA", [128, 128], BF16)
    c.maskB = di("maskB", [64, 128], BF16)
    c.cmaskv = di("cmaskv", [C])
    c.cmcol = di("cmcol", [1, C])
    c.hmaskv = di("hmaskv", [128, 1])

    c.y = do("y", [SEG, C])

    c.kscr = dscr("kscr", [C, T], BF16)
    c.vscr = dscr("vscr", [T, 8, 128], BF16)
    c.qscr = dscr("qscr", [C, SEG], BF16)
    c.xnscr = dscr("xnscr", [NCH, 128, SEG])
    c.xascr = dscr("xascr", [NCH, 128, SEG])
    c.accscr = dscr("accscr", [NCH, 128, SEG])

    em = EM(nc, ex)
    c.em = em
    E = ex.enter_context
    sbt = lambda n, s, d=F32: E(nc.sbuf_tensor(n, s, d))

    c.t_bqkv = sbt("t_bqkv", [128, 24])
    c.t_bout = sbt("t_bout", [128, 8])
    c.t_b1 = sbt("t_b1", [128, 32])
    c.t_b2 = sbt("t_b2", [128, 8])
    c.t_ln2g = sbt("t_ln2g", [128, 8]); c.t_ln2b = sbt("t_ln2b", [128, 8])
    c.t_qb = sbt("t_qb", [128, 8]); c.t_kb = sbt("t_kb", [128, 8])
    c.t_ob = sbt("t_ob", [128, 8])
    c.t_cm = sbt("t_cm", [128, 8])
    c.t_hm = sbt("t_hm", [128, 1])
    c.t_eps = sbt("t_eps", [128, 1])
    c.t_id32 = sbt("t_id32", [128, 128])
    c.t_idb = sbt("t_idb", [128, 128], BF16)
    c.t_ones1 = sbt("t_ones1", [1, 128], F32R)
    c.t_onesb = sbt("t_onesb", [128, 1], BF16)
    c.t_onesrb = sbt("t_onesrb", [1, 128], BF16)
    c.t_c3b = sbt("t_c3b", [1, 1], BF16)
    c.t_sel = sbt("t_sel", [128, 16], BF16)
    c.t_mA = sbt("t_mA", [128, 128], BF16)
    c.t_mB = sbt("t_mB", [64, 128], BF16)
    c.t_cmcol = sbt("t_cmcol", [1, C], F32R)
    c.t_cb4 = sbt("t_cb4", [128, 1])
    c.t_cb2 = sbt("t_cb2", [128, 1])
    c.t_c1 = sbt("t_c1", [128, 1])
    c.t_c2 = sbt("t_c2", [1, 128])
    c.t_c3 = sbt("t_c3", [1, 1])
    p1x = ExitStack()
    sb1 = lambda n, sh, d=F32: p1x.enter_context(nc.sbuf_tensor(n, sh, d))
    c.t_vbrep = sb1("t_vbrep", [128, C])
    c.t_nwrep = sb1("t_nwrep", [128, C]); c.t_nbrep = sb1("t_nbrep", [128, C])
    c.t_nwmrep = sb1("t_nwmrep", [128, C]); c.t_nbmrep = sb1("t_nbmrep", [128, C])
    c.t_l1grep = sb1("t_l1grep", [128, C]); c.t_l1brep = sb1("t_l1brep", [128, C])
    c.t_cmrep = sb1("t_cmrep", [128, C])
    c.t_st = [sb1(f"t_st{i}", [128, 32]) for i in range(2)]
    c.t_sq = [sb1(f"t_sq{i}", [128, 1024]) for i in range(2)]
    c.t_xt = [sb1(f"t_xt{i}", [128, 1024]) for i in range(2)]
    c.t_xn = [sb1(f"t_xn{i}", [128, 1024]) for i in range(2)]
    c.t_t2 = [sb1(f"t_t2{i}", [128, 1024], BF16) for i in range(2)]
    c.t_th = [sb1("t_th0", [128, 1024], BF16)] * 2
    c.t_row = sb1("t_row", [1, C], F32R)

    c.PS = [p1x.enter_context(nc.psum_tensor(f"ps{i}", [128, 512], F32))
            for i in range(8)]
    c.psb = lambda i: c.PS[i][:].bitcast(BF16)
    c.bank_evac = {}

    def bank_ready(bank):
        em.wait("tensor", "tA", c.bank_evac.get(bank, 0))
    c.bank_ready = bank_ready

    for (tile, src, n) in ((c.t_bqkv, c.bqkv, 24), (c.t_bout, c.bout, 8),
                           (c.t_b1, c.b1, 32), (c.t_b2, c.b2, 8),
                           (c.t_ln2g, c.ln2g, 8), (c.t_ln2b, c.ln2b, 8),
                           (c.t_qb, c.qbias, 8), (c.t_kb, c.kbias, 8),
                           (c.t_ob, c.obias, 8), (c.t_cm, c.cmaskv, 8)):
        ap = bass.AP(src.tensor, 0, [[1, 128], [128, n]])
        em.dma("sync", tile[:], ap, slow=True)
    em.dma("sync", c.t_hm[:], c.hmaskv)
    em.dma("sync", c.t_id32[:], c.ident32)
    em.dma("sync", c.t_idb[:], c.identb)
    em.dma("sync", c.t_ones1[:], c.ones1.bitcast(F32R))
    em.dma("sync", c.t_onesb[:], c.onesb)
    em.dma("sync", c.t_onesrb[:], c.onesrb)
    em.dma("sync", c.t_sel[:], c.selmat)
    em.dma("sync", c.t_mA[:], c.maskA)
    em.dma("sync", c.t_mB[:], c.maskB)
    n_ld0 = em.dma("sync", c.t_cmcol[:], c.cmcol.bitcast(F32R))
    em.wait("vector", "dS", n_ld0)
    em.vec(lambda e: e.memset(c.t_eps[:], EPS))
    em.wait("tensor", "dS", n_ld0)

    def replicate_row(src_vec, dst):
        em.wait("sync", "tP", em.cnt["tP"])
        n = em.dma("sync", c.t_row[:],
                   bass.AP(src_vec.tensor, 0, [[0, 1], [1, C]]).bitcast(F32R))
        em.wait("tensor", "dS", n)
        last = None
        for half in range(2):
            bank_ready(half)
            last = em.mm(c.PS[half][:], c.t_ones1[:],
                         c.t_row[0:1, half * 512:(half + 1) * 512],
                         True, True, inc="tP")
        em.wait("scalar", "tP", last)
        for half in range(2):
            na = em.act_ps(lambda e, h=half, d=dst: e.copy(
                d[:, h * 512:(h + 1) * 512], c.PS[h][:]))
            c.bank_evac[half] = na

    replicate_row(c.vbias, c.t_vbrep)
    replicate_row(c.nw, c.t_nwrep)
    replicate_row(c.nb, c.t_nbrep)
    replicate_row(c.ln1g, c.t_l1grep)
    replicate_row(c.ln1b, c.t_l1brep)
    replicate_row(c.cmaskv, c.t_cmrep)
    em.wait("vector", "tA", em.cnt["tA"])
    em.vec(lambda e: e.tensor_mul(c.t_nwmrep[:], c.t_nwrep[:], c.t_cmrep[:]))
    n_nwm = em.vec(lambda e: e.tensor_mul(c.t_nbmrep[:], c.t_nbrep[:],
                                          c.t_cmrep[:]))
    c.n_init_v = n_nwm

    def ln_stats(src_ap, P, col, st, sq):
        em.vec(lambda e: e.reduce_sum(st[0:P, col + 3:col + 4], src_ap, AX.X))
        na = em.act(lambda e: e.activation(
            sq[0:P, :], src_ap, AF.Square, bias=0.0, scale=1.0,
            accum_out=st[0:P, col + 4:col + 5]))
        em.wait("vector", "tA", na)
        em.vec(lambda e: e.tensor_scalar_mul(
            st[0:P, col:col + 1], st[0:P, col + 3:col + 4], 1.0 / C))
        em.vec(lambda e: e.tensor_scalar_mul(
            st[0:P, col + 5:col + 6], st[0:P, col + 4:col + 5], 1.0 / C))
        em.vec(lambda e: e.tensor_mul(
            st[0:P, col + 6:col + 7], st[0:P, col:col + 1],
            st[0:P, col:col + 1]))
        nv = em.vec(lambda e: e.tensor_sub(
            st[0:P, col + 7:col + 8], st[0:P, col + 5:col + 6],
            st[0:P, col + 6:col + 7]))
        em.wait("scalar", "tV", nv)
        na = em.act(lambda e: e.activation(
            st[0:P, col + 8:col + 9], st[0:P, col + 7:col + 8],
            AF.Sqrt, bias=c.t_eps[0:P, :], scale=1.0))
        em.wait("vector", "tA", na)
        em.vec(lambda e: e.reciprocal(
            st[0:P, col + 1:col + 2], st[0:P, col + 8:col + 9]))
        nv = em.vec(lambda e: e.scalar_tensor_tensor(
            st[0:P, col + 2:col + 3], st[0:P, col:col + 1], -1.0,
            st[0:P, col + 1:col + 2], ALU.mult, ALU.mult))
        return nv

    def norm_apply_affine(P, grep, brep, out_bf, st, sq, xn):
        em.vec(lambda e: e.scalar_tensor_tensor(
            sq[0:P, :], xn[0:P, :], st[0:P, 17:18],
            st[0:P, 18:19].broadcast_to([P, C]), ALU.mult, ALU.add))
        em.vec(lambda e: e.tensor_mul(sq[0:P, :], sq[0:P, :], grep[0:P, :]))
        return em.vec(lambda e: e.tensor_add(out_bf[0:P, :], sq[0:P, :],
                                             brep[0:P, :]))

    def phase_barrier():
        for eng in EM.ENGS:
            for s in ("dS", "dG", "tP", "tA", "tV"):
                em.wait(eng, s, em.cnt[s])
    c.phase_barrier = phase_barrier

    c.ln_stats = ln_stats
    c.norm_apply_affine = norm_apply_affine

    p1 = ExitStack()
    E1 = p1.enter_context
    c.kwT = E1(nc.sbuf_tensor("kwT", [128, NCH * C], BF16))
    c.vwT = E1(nc.sbuf_tensor("vwT", [128, NCH * C], BF16))
    c.wpan = E1(nc.sbuf_tensor("wpan", [128, C], BF16))
    c.xnn = E1(nc.sbuf_tensor("xnn", [128, 12 * SEG], BF16))
    c.kstg = [E1(nc.sbuf_tensor(f"kstg{i}", [128, 512], BF16)) for i in range(2)]
    c.vtmp = E1(nc.sbuf_tensor("vtmp", [128, 512], F32))
    c.vstg = [E1(nc.sbuf_tensor(f"vstg{i}", [128, 512], BF16)) for i in range(2)]
    c.qstg = [E1(nc.sbuf_tensor(f"qstg{i}", [128, 512], BF16)) for i in range(2)]
    c.xstg = [E1(nc.sbuf_tensor(f"xstg{i}", [128, 512], F32)) for i in range(2)]
    c.h_ctx = ExitStack()
    c.h_fm = c.h_ctx.enter_context(
        nc.sbuf_tensor("h_fm", [128, NCH * (SEG + 64)], BF16,
                       side="right"))
    HW = SEG + 64

    def build_wT(src, dst_sb):
        for r in range(8):
            em.wait("gpsimd", "tP", em.cnt["tP"])
            n = em.dma("gpsimd", c.wpan[:], src[r * 128:(r + 1) * 128, :])
            em.wait("tensor", "dG", n)
            for g in range(2):
                bank = g
                bank_ready(bank)
                last = None
                for i in range(4):
                    mc = g * 4 + i
                    last = em.tr(c.psb(bank)[:, i * 128:(i + 1) * 128],
                                 c.wpan[:, mc * 128:(mc + 1) * 128],
                                 c.t_idb[:], inc=(None if i < 3 else "tP"))
                em.wait("scalar", "tP", last)
                dv = dst_sb[:].rearrange("p (mc w) -> p mc w", w=C)
                na = em.act_ps(lambda e, b=bank, g=g, r=r, dv=dv: e.copy(
                    dv[:, g * 4:g * 4 + 4, r * 128:(r + 1) * 128],
                    c.psb(b)[:, 0:512].rearrange("p (a w) -> p a w", w=128)))
                c.bank_evac[bank] = na

    build_wT(c.kw, c.kwT)
    build_wT(c.vw, c.vwT)

    c.tile_users = [dict(tA=0, tV=0, tP=0), dict(tA=0, tV=0, tP=0)]

    def stream_tile(src_ap, P, grep, brep, par):
        u = c.tile_users[par]
        em.wait("sync", "tA", u["tA"])
        em.wait("sync", "tV", u["tV"])
        em.wait("sync", "tP", u["tP"])
        st, sq, xt, xn = c.t_st[par], c.t_sq[par], c.t_xt[par], c.t_xn[par]
        n = em.dma("sync", xt[0:P, :], src_ap)
        em.wait("vector", "dS", n)
        em.wait("scalar", "dS", n)
        nv0 = ln_stats(xt[0:P, :], P, 0, st, sq)
        em.wait("scalar", "tV", nv0)
        na = em.act(lambda e, P=P, st=st, xt=xt, xn=xn: e.activation(
            xn[0:P, :], xt[0:P, :], AF.Identity,
            bias=st[0:P, 2:3], scale=st[0:P, 1:2]))
        em.wait("vector", "tA", na)
        em.vec(lambda e, P=P, st=st: e.tensor_mul(
            st[0:P, 19:20], st[0:P, 1:2], st[0:P, 1:2]))
        nv = em.vec(lambda e, P=P, st=st: e.tensor_mul(
            st[0:P, 23:24], st[0:P, 7:8], st[0:P, 19:20]))
        em.wait("scalar", "tV", nv)
        na = em.act(lambda e, P=P, st=st: e.activation(
            st[0:P, 24:25], st[0:P, 23:24], AF.Sqrt,
            bias=c.t_eps[0:P, :], scale=1.0))
        em.wait("vector", "tA", na)
        em.vec(lambda e, P=P, st=st: e.reciprocal(
            st[0:P, 17:18], st[0:P, 24:25]))
        em.vec(lambda e, P=P, st=st: e.tensor_scalar_mul(
            st[0:P, 18:19], st[0:P, 17:18], 0.0))
        return norm_apply_affine(P, grep, brep, c.t_t2[par], st, sq, xn)

    CHOFF = {0: 0, 1: 8, 2: 0, 3: 2}
    def stream_block(j):
        mch = MCH_J[j]
        for t in range(NTT):
            par = (j * NTT + t) % 2
            nv = stream_tile(c.xb[j * SEG + t * 128: j * SEG + (t + 1) * 128, :],
                             128, c.t_nwrep, c.t_nbrep, par)
            em.wait("tensor", "tV", nv)
            bank = 2 + par
            bank_ready(bank)
            last = None
            for ch in range(mch):
                last = em.tr(c.psb(bank)[:, ch * 128:(ch + 1) * 128],
                             c.t_t2[par][:, ch * 128:(ch + 1) * 128],
                             c.t_idb[:], inc=(None if ch < mch - 1 else "tP"))
            em.wait("scalar", "tP", last)
            xv = c.xnn[:].rearrange("p (ch w) -> p ch w", w=SEG)
            na = em.act_ps(lambda e, b=bank, t=t, mch=mch, xv=xv, j=j: e.copy(
                xv[:, CHOFF[j]:CHOFF[j] + mch, t * 128:(t + 1) * 128],
                c.psb(b)[:].rearrange("p (ch w) -> p ch w", w=128)[:, 0:mch, :]))
            c.bank_evac[bank] = na
            c.tile_users[par] = dict(tA=na, tV=em.cnt["tV"], tP=last)

    def kv_block(j):
        mch = MCH_J[j]
        em.wait("tensor", "tA", em.cnt["tA"])
        for dk in range(8):
            for qt in range(NQT):
                bank = 4 + ((dk * NQT + qt) % 2)
                bank_ready(bank)
                last = None
                for i in range(mch):
                    last = em.mm(
                        c.PS[bank][:],
                        c.kwT[:, i * C + dk * 128: i * C + (dk + 1) * 128],
                        c.xnn[:, (CHOFF[j] + i) * SEG + qt * 512:
                              (CHOFF[j] + i) * SEG + (qt + 1) * 512],
                        i == 0, i == mch - 1,
                        inc=(None if i < mch - 1 else "tP"))
                em.wait("scalar", "tP", last)
                stg = c.kstg[(dk * NQT + qt) % 2]
                em.wait("scalar", "dS",
                        c.bank_evac.get(("kstg", (dk * NQT + qt) % 2), 0))
                na = em.act_ps(lambda e, b=bank, dk=dk, s=stg: e.activation(
                    s[:], c.PS[b][:], AF.Identity,
                    bias=c.t_kb[:, dk:dk + 1], scale=1.0))
                c.bank_evac[bank] = na
                em.wait("sync", "tA", na)
                nd = em.dma("sync",
                            c.kscr[dk * 128:(dk + 1) * 128,
                                   j * SEG + qt * 512: j * SEG + (qt + 1) * 512],
                            stg[:])
                c.bank_evac[("kstg", (dk * NQT + qt) % 2)] = nd

        for tt in range(NTT):
            for hf in range(2):
                bank = 6 + ((tt * 2 + hf) % 2)
                bank_ready(bank)
                last = None
                for i in range(mch):
                    last = em.mm(
                        c.PS[bank][:],
                        c.xnn[:, (CHOFF[j] + i) * SEG + tt * 128:
                              (CHOFF[j] + i) * SEG + (tt + 1) * 128],
                        c.vwT[:, i * C + hf * 512: i * C + (hf + 1) * 512],
                        i == 0, i == mch - 1,
                        inc=(None if i < mch - 1 else "tP"))
                em.wait("scalar", "tP", last)
                em.wait("scalar", "tV", c.bank_evac.get("vtmp", 0))
                na = em.act_ps(lambda e, b=bank: e.copy(c.vtmp[:], c.PS[b][:]))
                c.bank_evac[bank] = na
                em.wait("vector", "tA", na)
                sidx = (tt * 2 + hf) % 2
                em.wait("vector", "dS",
                        c.bank_evac.get(("vstg", sidx), 0))
                nv = em.vec(lambda e, hf=hf, s=c.vstg[sidx]: e.tensor_add(
                    s[:], c.vtmp[:], c.t_vbrep[:, hf * 512:(hf + 1) * 512]))
                c.bank_evac["vtmp"] = nv
                em.wait("sync", "tV", nv)
                vv = c.vstg[sidx][:].rearrange("p (h d) -> p h d", d=128)
                ndl = em.dma("sync",
                             c.vscr[j * SEG + tt * 128:j * SEG + (tt + 1) * 128,
                                    hf * 4:(hf + 1) * 4, :], vv)
                c.bank_evac[("vstg", sidx)] = ndl

    for j in (0, 1):
        stream_block(j)
    for j in (0, 1):
        kv_block(j)
    for j in (2, 3):
        stream_block(j)
    for j in (2, 3):
        kv_block(j)
    build_wT(c.qw, c.kwT)
    c.qwT = c.kwT

    def own_tile(src_ap, P, hcol, t):
        par = t % 2 if t >= 0 else 0
        nv = stream_tile(src_ap, P, c.t_nwmrep, c.t_nbmrep, par)
        last_tr = 0
        if t >= 0:
            em.wait("tensor", "tV", nv)
            bank = 2 + par
            bank_ready(bank)
            last = None
            for ch in range(8):
                last = em.tr(c.psb(bank)[:, ch * P:ch * P + P],
                             c.t_t2[par][0:P, ch * 128:(ch + 1) * 128],
                             c.t_idb[0:P, 0:P], inc=(None if ch < 7 else "tP"))
            em.wait("scalar", "tP", last)
            last_tr = last
            xv = c.xnn[:].rearrange("p (ch w) -> p ch w", w=SEG)
            na = em.act_ps(lambda e, b=bank, t=t, xv=xv: e.copy(
                xv[:, 0:8, t * 128:(t + 1) * 128],
                c.psb(b)[:].rearrange("p (ch w) -> p ch w", w=128)))
            c.bank_evac[bank] = na
        st, sq, xn = c.t_st[par], c.t_sq[par], c.t_xn[par]
        nv2 = norm_apply_affine(P, c.t_l1grep, c.t_l1brep, c.t_th[par],
                                st, sq, xn)
        em.wait("tensor", "tV", nv2)
        bank2 = 2 + ((t + 1) % 2) if t >= 0 else 3
        bank_ready(bank2)
        last = None
        for ch in range(8):
            last = em.tr(c.psb(bank2)[:, ch * P:ch * P + P],
                         c.t_th[par][0:P, ch * 128:(ch + 1) * 128],
                         c.t_idb[0:P, 0:P], inc=(None if ch < 7 else "tP"))
        em.wait("scalar", "tP", last)
        hv = c.h_fm[:].rearrange("p (ch w) -> p ch w", w=HW)
        na = em.act_ps(lambda e, b=bank2, hcol=hcol, P=P, hv=hv: e.copy(
            hv[:, :, hcol:hcol + P],
            c.psb(b)[:].rearrange("p (ch w) -> p ch w", w=P)[:, 0:8, :]))
        c.bank_evac[bank2] = na
        if t >= 0:
            for g in range(2):
                bank3 = 4 + g
                bank_ready(bank3)
                last = None
                for i in range(4):
                    ch = g * 4 + i
                    last = em.tr(c.PS[bank3][:, i * 128:(i + 1) * 128],
                                 c.t_xn[par][:, ch * 128:(ch + 1) * 128],
                                 c.t_id32[:], inc=(None if i < 3 else "tP"))
                em.wait("scalar", "tP", last)
                sidx = g
                em.wait("scalar", "dS", c.bank_evac.get(("xstg", sidx), 0))
                na = em.act_ps(lambda e, b=bank3, s=c.xstg[sidx]: e.copy(
                    s[:], c.PS[b][:]))
                c.bank_evac[bank3] = na
                em.wait("sync", "tA", na)
                nd = 0
                for i in range(4):
                    ch = g * 4 + i
                    nd = em.dma("sync",
                                c.xnscr[ch, :, t * 128:(t + 1) * 128],
                                c.xstg[sidx][:, i * 128:(i + 1) * 128])
                c.bank_evac[("xstg", sidx)] = nd
        c.tile_users[par] = dict(tA=em.cnt["tA"], tV=em.cnt["tV"],
                                 tP=em.cnt["tP"])

    own_tile(c.xh[0:64, :], 64, 0, -1)
    for t in range(NTT):
        own_tile(c.xq[t * 128:(t + 1) * 128, :], 128, 64 + t * 128, t)
    n_xnnq_done = em.cnt["tA"]

    em.wait("tensor", "tA", n_xnnq_done)
    for dq in range(8):
        for qt in range(NQT):
            bank = 6 + ((dq * NQT + qt) % 2)
            bank_ready(bank)
            last = None
            for i in range(8):
                last = em.mm(
                    c.PS[bank][:],
                    c.qwT[:, i * C + dq * 128: i * C + (dq + 1) * 128],
                    c.xnn[:, i * SEG + qt * 512: i * SEG + (qt + 1) * 512],
                    i == 0, i == 7, inc=(None if i < 7 else "tP"))
            em.wait("scalar", "tP", last)
            sidx = (dq * NQT + qt) % 2
            em.wait("scalar", "dS", c.bank_evac.get(("qstg", sidx), 0))
            na = em.act_ps(lambda e, b=bank, dq=dq, s=c.qstg[sidx]: e.activation(
                s[:], c.PS[b][:], AF.Identity, bias=c.t_qb[:, dq:dq + 1], scale=1.0))
            c.bank_evac[bank] = na
            em.wait("sync", "tA", na)
            nd = em.dma("sync",
                        c.qscr[dq * 128:(dq + 1) * 128,
                               qt * 512:(qt + 1) * 512],
                        c.qstg[sidx][:])
            c.bank_evac[("qstg", sidx)] = nd

    if DBG:
        dbg1 = do("dbg_xnn", [128, NCH * SEG], BF16)
        dbg2 = do("dbg_h", [128, NCH * HW], BF16)
        phase_barrier()
        em.dma("sync", dbg1, c.xnn[:])
        em.dma("sync", dbg2, c.h_fm[:])

    phase_barrier()
    p1.close()
    p1x.close()

    if isinstance(stages, int):
        stages = ["p1", "p2", "p3", "p4", "p5"][:stages]
    if "p2" in stages:
        emit_phase2(c)
        phase_barrier()
    if "p3" in stages:
        c.h_ctx.close()
        emit_phase3(c)
        phase_barrier()
    if "p4" in stages:
        if "p3" not in stages:
            c.h_ctx.close()
        emit_phase4(c)
        phase_barrier()
    if "p5" in stages:
        emit_phase5(c)
    else:
        nd = em.dma("sync", c.y[0:128, :],
                    bass.AP(c.xq.tensor, 0, [[1024, 128], [1, 1024]]))
        em.wait("sync", "dS", nd)

    with nc.Block() as block:
        @block.sync
        def _(e):
            for f in em.ops["sync"]:
                f(e)

        @block.gpsimd
        def _(e):
            for f in em.ops["gpsimd"]:
                f(e)

        @block.tensor
        def _(e):
            for f in em.ops["tensor"]:
                f(e)

        @block.vector
        def _(e):
            for f in em.ops["vector"]:
                f(e)

        @block.scalar
        def _(e):
            for f in em.ops["scalar"]:
                f(e)

    ex.close()
    return nc


def emit_phase2(c):
    nc, em = c.nc, c.em
    HW = SEG + 64
    p2 = ExitStack()
    E2 = p2.enter_context
    wqkv_sb = E2(nc.sbuf_tensor("wqkv_sb", [128, NCH * 3 * C], BF16))
    wout_sb = E2(nc.sbuf_tensor("wout_sb", [128, NCH * C], BF16))
    qkvl = E2(nc.sbuf_tensor("qkvl", [128, 24 * 576], BF16))
    vtl = E2(nc.sbuf_tensor("vtl", [128, 5 * 1024], BF16))
    ptA = [E2(nc.sbuf_tensor(f"ptA{i}", [128, 1024], BF16)) for i in range(2)]
    ptB = [E2(nc.sbuf_tensor(f"ptB{i}", [64, 1024], BF16)) for i in range(2)]
    rbl = E2(nc.sbuf_tensor("rbl", [128, 1024], F32))
    outl = E2(nc.sbuf_tensor("outl", [128, 8 * 512], BF16))
    l2 = E2(nc.sbuf_tensor("l2", [1, 1024], F32))
    r2 = E2(nc.sbuf_tensor("r2", [1, 1024], F32))
    r2r = E2(nc.sbuf_tensor("r2r", [1, 1024], F32R))
    xat = [E2(nc.sbuf_tensor(f"xat{i}", [128, 512], F32)) for i in range(2)]
    xnt = [E2(nc.sbuf_tensor(f"xnt{i}", [128, 512], F32)) for i in range(2)]
    xas = [E2(nc.sbuf_tensor(f"xas{i}", [128, 512], F32)) for i in range(2)]
    pSA = E2(nc.psum_tensor("pSA", [128, 1024], F32))
    pSB = E2(nc.psum_tensor("pSB", [64, 1024], F32))
    pPV = E2(nc.psum_tensor("pPV", [128, 512], F32))
    pL2 = E2(nc.psum_tensor("pL2", [128, 512], F32))
    pX2 = [E2(nc.psum_tensor(f"pX2{i}", [128, 512], F32)) for i in range(2)]

    ev = c.bank_evac

    nw_ = None
    for r in range(8):
        nw_ = em.dma("gpsimd", wqkv_sb[:, r * 3 * C:(r + 1) * 3 * C],
                     c.wqkv[r * 128:(r + 1) * 128, :])
        nw_ = em.dma("gpsimd", wout_sb[:, r * C:(r + 1) * C],
                     c.wout[r * 128:(r + 1) * 128, :])
    em.wait("tensor", "dG", nw_)
    cb2_done = [False]

    def emit_scores(cc, qb, grp):
        kc0 = qb * 128
        qc0 = 64 + qb * 128
        em.wait("tensor", "tA", ev.get("pSA", 0))
        em.wait("tensor", "tA", ev.get("pSB", 0))
        lastB = None
        for hl in range(8):
            hh = grp * 8 + hl
            kch = 8 + hh // 2
            qch = hh // 2
            pb = (hh % 2) * 64
            lhsA = qkvl[pb:pb + 64, kch * 576 + kc0:kch * 576 + kc0 + 128]
            lhsB = qkvl[pb:pb + 64, kch * 576 + kc0 + 128:
                        kch * 576 + kc0 + 192]
            rhsq = qkvl[pb:pb + 64, qch * 576 + qc0:qch * 576 + qc0 + 128]
            em.mm(pSA[:, hl * 128:(hl + 1) * 128], lhsA, rhsq, True, True,
                  inc=("tP" if hl == 7 else None))
            lastB = em.mm(pSB[:, hl * 128:(hl + 1) * 128], lhsB, rhsq,
                          True, True, inc=("tP" if hl == 7 else None))
        return lastB

    def emit_exp(par, nscores):
        em.wait("scalar", "tP", nscores)
        if not cb2_done[0]:
            na = em.act_ps(lambda e: e.copy(rbl[:, 0:512], pSA[:, 0:512]))
            em.wait("vector", "tA", na)
            nv = em.vec(lambda e: e.reduce_max(c.t_c1[:], rbl[:, 0:512], AX.X))
            em.wait("tensor", "tV", nv)
            ntr = em.tr(pX2[0][0:1, 0:128], c.t_c1[:], c.t_id32[:], inc="tP")
            em.wait("scalar", "tP", ntr)
            na = em.act_ps(lambda e: e.copy(c.t_c2[:], pX2[0][0:1, 0:128]))
            em.wait("vector", "tA", na)
            em.vec(lambda e: e.reduce_max(c.t_c3[:], c.t_c2[:], AX.X))
            em.vec(lambda e: e.tensor_scalar_mul(c.t_c3[:], c.t_c3[:], -SCALE_L))
            nv = em.vec(lambda e: e.tensor_copy(c.t_c3b[:], c.t_c3[:]))
            em.wait("tensor", "tV", nv)
            nmm2 = em.mm(pX2[0][:, 0:1], c.t_onesrb[:], c.t_c3b[:],
                         True, True, inc="tP")
            em.wait("scalar", "tP", nmm2)
            na = em.act_ps(lambda e: e.copy(c.t_cb2[:], pX2[0][:, 0:1]))
            ev["pX20"] = na
            cb2_done[0] = True
        em.wait("scalar", "tV", ev.get(f"pt_use{par}", 0))
        em.wait("scalar", "tP", ev.get(f"pt_pv{par}", 0))
        naA = em.act_ps(lambda e, par=par: e.activation(
            ptA[par][:], pSA[:], AF.Exp, bias=c.t_cb2[:], scale=SCALE_L))
        ev["pSA"] = naA
        naB = em.act_ps(lambda e, par=par: e.activation(
            ptB[par][:], pSB[:], AF.Exp, bias=c.t_cb2[0:64, :], scale=SCALE_L))
        ev["pSB"] = naB
        return naB

    def emit_rest(cc, qb, grp, par, nexp):
        em.wait("vector", "tA", nexp)
        vA = ptA[par][:].rearrange("p (o w) -> p o w", w=128)
        mAb = c.t_mA[:].rearrange("p (o w) -> p o w", w=128).broadcast_to(
            [128, 8, 128])
        em.vec(lambda e: e.tensor_mul(vA, vA, mAb))
        vB = ptB[par][:].rearrange("p (o w) -> p o w", w=128)
        mBb = c.t_mB[:].rearrange("p (o w) -> p o w", w=128).broadcast_to(
            [64, 8, 128])
        nv = em.vec(lambda e: e.tensor_mul(vB, vB, mBb))
        em.wait("tensor", "tV", nv)
        em.wait("tensor", "tA", ev.get("pL2", 0))
        em.wait("tensor", "tA", ev.get("pX21", 0))
        em.mm(pL2[0:1, :], c.t_onesb[:], ptA[par][:, 0:512], True, False)
        em.mm(pL2[0:1, :], c.t_onesb[0:64, :], ptB[par][:, 0:512], False, True)
        em.mm(pX2[1][0:1, :], c.t_onesb[:], ptA[par][:, 512:1024], True, False)
        nl = em.mm(pX2[1][0:1, :], c.t_onesb[0:64, :], ptB[par][:, 512:1024],
                   False, True, inc="tP")
        em.wait("scalar", "tP", nl)
        na = em.act_ps(lambda e: e.copy(l2[0:1, 0:512], pL2[0:1, :]))
        ev["pL2"] = na
        na = em.act_ps(lambda e: e.copy(l2[0:1, 512:1024], pX2[1][0:1, :]))
        ev["pX21"] = na
        em.wait("vector", "tA", na)
        em.vec(lambda e: e.reciprocal(r2[:], l2[:]))
        nv = em.vec(lambda e: e.tensor_copy(r2r[:], r2[:]))
        em.wait("tensor", "tV", nv)
        for half in range(2):
            em.wait("tensor", "tA", ev.get("pX20", 0))
            nmm = em.mm(pX2[0][:], c.t_ones1[:],
                        r2r[0:1, half * 512:(half + 1) * 512], True, True,
                        inc="tP")
            em.wait("scalar", "tP", nmm)
            na = em.act_ps(lambda e, half=half: e.copy(
                rbl[:, half * 512:(half + 1) * 512], pX2[0][:]))
            ev["pX20"] = na
        em.wait("vector", "tA", ev["pX20"])
        rA = rbl[:].rearrange("p (o w) -> p o w", w=128)
        em.vec(lambda e: e.tensor_mul(vA, vA, rA))
        rB = rbl[0:64, :].rearrange("p (o w) -> p o w", w=128)
        nv = em.vec(lambda e: e.tensor_mul(vB, vB, rB))
        ev[f"pt_use{par}"] = nv
        em.wait("tensor", "tV", nv)
        em.wait("tensor", "tA", ev.get("pPV", 0))
        last = None
        for hl in range(8):
            hh = grp * 8 + hl
            p = hl // 2
            pb = (hh % 2) * 64
            dvc = (hh % 2) * 64
            rc = grp * 4 + p
            lhsA = vtl[:, qb * 1024 + rc * 128 + dvc:
                       qb * 1024 + rc * 128 + dvc + 64]
            lhsB = vtl[0:64, (qb + 1) * 1024 + rc * 128 + dvc:
                       (qb + 1) * 1024 + rc * 128 + dvc + 64]
            outap = pPV[pb:pb + 64, p * 128:(p + 1) * 128]
            tp = (0, pb)
            em.op("tensor", lambda e, o=outap, l=lhsA,
                  r=ptA[par][:, hl * 128:(hl + 1) * 128], tp=tp:
                  e.matmul(o, l, r, start=True, stop=False, tile_position=tp))
            last = em.op("tensor", lambda e, o=outap, l=lhsB,
                         r=ptB[par][:, hl * 128:(hl + 1) * 128], tp=tp:
                         e.matmul(o, l, r, start=False, stop=True,
                                  tile_position=tp),
                         inc=("tP" if hl == 7 else None))
        ev[f"pt_pv{par}"] = last
        em.wait("scalar", "tP", last)
        ov = outl[:].rearrange("p (o w) -> p o w", w=512)
        na = em.act_ps(lambda e, grp=grp, qb=qb, ov=ov: e.copy(
            ov[:, grp * 4:(grp + 1) * 4, qb * 128:(qb + 1) * 128],
            pPV[:].rearrange("p (o w) -> p o w", w=128)))
        ev["pPV"] = na

    for cc in range(4):
        hw0 = cc * 512
        qkv_banksA = [("pX20", pX2[0][:]), ("pSA", pSA[:, 0:512]),
                      ("pPV", pPV[:])]
        qkv_banksB = [("pX21", pX2[1][:, 0:64]), ("pSA", pSA[:, 512:576]),
                      ("pL2", pL2[:, 0:64])]
        for oc in range(24):
            rot = oc % 3
            kA, bA = qkv_banksA[rot]
            kB, bB = qkv_banksB[rot]
            em.wait("tensor", "tA", ev.get(kA, 0))
            em.wait("tensor", "tA", ev.get(kB, 0))
            lastB = None
            for i in range(NCH):
                em.mm(bA,
                      wqkv_sb[:, i * 3 * C + oc * 128:i * 3 * C + (oc + 1) * 128],
                      c.h_fm[:, i * HW + hw0:i * HW + hw0 + 512],
                      i == 0, i == NCH - 1)
                lastB = em.mm(
                    bB,
                    wqkv_sb[:, i * 3 * C + oc * 128:i * 3 * C + (oc + 1) * 128],
                    c.h_fm[:, i * HW + hw0 + 512:i * HW + hw0 + 576],
                    i == 0, i == NCH - 1,
                    inc=("tP" if i == NCH - 1 else None))
            em.wait("scalar", "tP", lastB)
            na = em.act_ps(lambda e, oc=oc, bA=bA: e.activation(
                qkvl[:, oc * 576:oc * 576 + 512], bA,
                AF.Identity, bias=c.t_bqkv[:, oc:oc + 1], scale=1.0))
            ev[kA] = na
            na = em.act_ps(lambda e, oc=oc, bB=bB: e.activation(
                qkvl[:, oc * 576 + 512:oc * 576 + 576], bB,
                AF.Identity, bias=c.t_bqkv[:, oc:oc + 1], scale=1.0))
            ev[kB] = na
        n_qkv = em.cnt["tA"]
        if cc == 0:
            em.wait("vector", "tA", n_qkv)
            nv = None
            for oc in range(8, 24):
                nv = em.vec(lambda e, oc=oc: e.tensor_scalar_mul(
                    qkvl[:, oc * 576:oc * 576 + 64],
                    qkvl[:, oc * 576:oc * 576 + 64], c.t_hm[:, 0:1]))
            em.wait("tensor", "tV", nv)
        em.wait("tensor", "tA", n_qkv)
        for vc in range(8):
            par = vc % 2
            em.wait("tensor", "tA", ev.get(f"pX2{par}", 0))
            last = None
            for tt in range(4):
                last = em.tr(pX2[par][:].bitcast(BF16)[:, tt * 128:
                                                       (tt + 1) * 128],
                             qkvl[:, (16 + vc) * 576 + tt * 128:
                                  (16 + vc) * 576 + (tt + 1) * 128],
                             c.t_idb[:], inc=(None if tt < 3 else "tP"))
            em.wait("scalar", "tP", last)
            vv = vtl[:].rearrange("p (tt w) -> p tt w", w=1024)
            na = em.act_ps(lambda e, par=par, vc=vc, vv=vv: e.copy(
                vv[:, 0:4, vc * 128:(vc + 1) * 128],
                pX2[par][:].bitcast(BF16)[:, 0:512].rearrange(
                    "p (a w) -> p a w", w=128)))
            ev[f"pX2{par}"] = na
            par2 = (vc + 1) % 2
            em.wait("tensor", "tA", ev.get(f"pX2{par2}", 0))
            ntr = em.tr(pX2[par2][:].bitcast(BF16)[0:64, 0:128],
                        qkvl[:, (16 + vc) * 576 + 512:(16 + vc) * 576 + 576],
                        c.t_idb[:], inc="tP")
            em.wait("scalar", "tP", ntr)
            na = em.act_ps(lambda e, par2=par2, vc=vc: e.copy(
                vtl[0:64, 4 * 1024 + vc * 128:4 * 1024 + (vc + 1) * 128],
                pX2[par2][:].bitcast(BF16)[0:64, 0:128]))
            ev[f"pX2{par2}"] = na
        n_vtl = em.cnt["tA"]
        em.wait("tensor", "tA", n_vtl)
        its = [(qb, grp) for qb in range(4) for grp in range(2)]
        ns = emit_scores(cc, *its[0])
        pend_exp = emit_exp(0, ns)
        for i, it in enumerate(its):
            ne = pend_exp
            if i + 1 < len(its):
                ns = emit_scores(cc, *its[i + 1])
                pend_exp = emit_exp((i + 1) % 2, ns)
            emit_rest(cc, it[0], it[1], i % 2, ne)
        for oc in range(8):
            par = oc % 2
            em.wait("tensor", "tA", ev.get(f"pX2{par}", 0))
            last = None
            for i in range(NCH):
                last = em.mm(pX2[par][:],
                             wout_sb[:, i * C + oc * 128:i * C + (oc + 1) * 128],
                             outl[:, i * 512:(i + 1) * 512],
                             i == 0, i == NCH - 1,
                             inc=("tP" if i == NCH - 1 else None))
            em.wait("scalar", "tP", last)
            em.wait("scalar", "tV", ev.get(f"xat{par}", 0))
            na = em.act_ps(lambda e, par=par: e.copy(xat[par][:], pX2[par][:]))
            ev[f"pX2{par}"] = na
            nd = em.dma("sync", xnt[par][:],
                        c.xnscr[oc, :, cc * 512:(cc + 1) * 512])
            em.wait("vector", "tA", na)
            em.wait("vector", "dS", nd)
            em.wait("vector", "dS", ev.get(f"xas{par}", 0))
            nv = em.vec(lambda e, par=par, oc=oc: e.scalar_tensor_tensor(
                xas[par][:], xat[par][:], c.t_bout[:, oc:oc + 1],
                xnt[par][:], ALU.add, ALU.add))
            ev[f"xat{par}"] = nv
            em.wait("sync", "tV", nv)
            nd = em.dma("sync", c.xascr[oc, :, cc * 512:(cc + 1) * 512],
                        xas[par][:])
            ev[f"xas{par}"] = nd

    if DBG:
        c.phase_barrier()
        dbg = nc.dram_tensor("dbg_h2", [128, NCH * HW], BF16,
                             kind="ExternalOutput").ap()
        nd = em.dma("sync", dbg, c.h_fm[:])
        em.wait("sync", "dS", nd)

    c.phase_barrier()
    p2.close()


def emit_phase3(c):
    nc, em = c.nc, c.em
    p3 = ExitStack()
    E3 = p3.enter_context
    w2sb = E3(nc.sbuf_tensor("w2sb", [128, 32 * C], BF16))
    w1g = [E3(nc.sbuf_tensor(f"w1g{i}", [128, NCH * 512], BF16))
           for i in range(2)]
    xat3 = E3(nc.sbuf_tensor("xat3", [128, NCH * 512], F32))
    xab = E3(nc.sbuf_tensor("xab", [128, 4096], BF16))
    xsqb = E3(nc.sbuf_tensor("xsqb", [128, 4096], BF16))
    h2b = E3(nc.sbuf_tensor("h2b", [128, 4096], BF16))
    gsb = E3(nc.sbuf_tensor("gsb", [128, 32 * 512], BF16))
    st2 = E3(nc.sbuf_tensor("st2", [2, 512], F32))
    st3 = E3(nc.sbuf_tensor("st3", [1, 512 * 6], F32))
    strr = E3(nc.sbuf_tensor("strr", [1, 1024], F32R))
    murep = E3(nc.sbuf_tensor("murep", [128, 512], F32))
    rsrep = E3(nc.sbuf_tensor("rsrep", [128, 512], F32))
    t3a = E3(nc.sbuf_tensor("t3a", [128, 512], F32))
    t3b = E3(nc.sbuf_tensor("t3b", [128, 512], F32))
    m2t = [E3(nc.sbuf_tensor(f"m2t{i}", [128, 512], F32)) for i in range(2)]
    xnt3 = [E3(nc.sbuf_tensor(f"xnt3{i}", [128, 512], F32)) for i in range(2)]
    acs = [E3(nc.sbuf_tensor(f"acs{i}", [128, 512], F32)) for i in range(2)]
    p3F = [E3(nc.psum_tensor(f"p3F{i}", [128, 512], F32)) for i in range(4)]
    p3L = E3(nc.psum_tensor("p3L", [128, 512], F32))
    p3X = E3(nc.psum_tensor("p3X", [128, 512], F32))

    ev = c.bank_evac
    nw_ = None
    for r in range(32):
        nw_ = em.dma("gpsimd", w2sb[:, r * C:(r + 1) * C],
                     c.w2[r * 128:(r + 1) * 128, :])
    n_w2 = nw_

    def load_w1g(grp, par):
        n = None
        em.wait("gpsimd", "tP", ev.get(f"w1g{par}", 0))
        for r in range(8):
            n = em.dma("gpsimd", w1g[par][:, r * 512:(r + 1) * 512],
                       c.w1[r * 128:(r + 1) * 128, grp * 512:(grp + 1) * 512])
        return n

    w1_ld = load_w1g(0, 0)

    for qt in range(4):
        em.wait("sync", "tV", ev.get("xat3_use", 0))
        nd = None
        for ch in range(NCH):
            nd = em.dma("sync", xat3[:, ch * 512:(ch + 1) * 512],
                        c.xascr[ch, :, qt * 512:(qt + 1) * 512])
        em.wait("vector", "dS", nd)
        nv = em.vec(lambda e: e.tensor_copy(xab[:], xat3[:]))
        em.wait("scalar", "tV", nv)
        na = em.act(lambda e: e.activation(xsqb[:], xab[:], AF.Square,
                                           bias=0.0, scale=1.0))
        em.wait("tensor", "tA", na)
        em.wait("tensor", "tV", nv)
        em.wait("tensor", "tA", ev.get("p3L", 0))
        last = None
        for ch in range(NCH):
            em.mm(p3L[0:2, :], c.t_sel[:, 0:2], xab[:, ch * 512:(ch + 1) * 512],
                  ch == 0, False)
            last = em.mm(p3L[0:2, :], c.t_sel[:, 4:6],
                         xsqb[:, ch * 512:(ch + 1) * 512],
                         False, ch == NCH - 1,
                         inc=("tP" if ch == NCH - 1 else None))
        em.wait("scalar", "tP", last)
        na = em.act_ps(lambda e: e.copy(st2[:], p3L[0:2, :]))
        ev["p3L"] = na
        em.wait("sync", "tA", na)
        nd = em.dma("sync", st3[0:1, 512:1024], st2[1:2, :])
        nd = em.dma("sync", st3[0:1, 0:512], st2[0:1, :])
        em.wait("vector", "dS", nd)
        em.vec(lambda e: e.tensor_scalar_mul(st3[0:1, 0:1024],
                                             st3[0:1, 0:1024], 1.0 / C))
        em.vec(lambda e: e.tensor_mul(st3[0:1, 1024:1536], st3[0:1, 0:512],
                                      st3[0:1, 0:512]))
        nv = em.vec(lambda e: e.tensor_sub(st3[0:1, 1536:2048],
                                           st3[0:1, 512:1024],
                                           st3[0:1, 1024:1536]))
        em.wait("scalar", "tV", nv)
        na = em.act(lambda e: e.activation(st3[0:1, 2048:2560],
                                           st3[0:1, 1536:2048], AF.Sqrt,
                                           bias=c.t_eps[0:1, :], scale=1.0))
        em.wait("vector", "tA", na)
        em.vec(lambda e: e.reciprocal(st3[0:1, 2560:3072],
                                      st3[0:1, 2048:2560]))
        em.vec(lambda e: e.tensor_copy(strr[0:1, 0:512], st3[0:1, 0:512]))
        nv = em.vec(lambda e: e.tensor_copy(strr[0:1, 512:1024],
                                            st3[0:1, 2560:3072]))
        em.wait("tensor", "tV", nv)
        for k, dst in ((0, murep), (1, rsrep)):
            em.wait("tensor", "tA", ev.get("p3X", 0))
            nmm = em.mm(p3X[:], c.t_ones1[:], strr[0:1, k * 512:(k + 1) * 512],
                        True, True, inc="tP")
            em.wait("scalar", "tP", nmm)
            na = em.act_ps(lambda e, dst=dst: e.copy(dst[:], p3X[:]))
            ev["p3X"] = na
        em.wait("vector", "tA", ev["p3X"])
        nvh = None
        for ch in range(NCH):
            em.vec(lambda e, ch=ch: e.tensor_sub(
                t3a[:], xat3[:, ch * 512:(ch + 1) * 512], murep[:]))
            em.vec(lambda e: e.tensor_mul(t3b[:], t3a[:], rsrep[:]))
            nvh = em.vec(lambda e, ch=ch: e.scalar_tensor_tensor(
                h2b[:, ch * 512:(ch + 1) * 512], t3b[:],
                c.t_ln2g[:, ch:ch + 1],
                c.t_ln2b[:, ch:ch + 1].broadcast_to([128, 512]),
                ALU.mult, ALU.add))
        em.wait("tensor", "tV", nvh)
        for grp in range(8):
            par = grp % 2
            em.wait("tensor", "dG", w1_ld)
            have_next = (grp < 7) or (qt < 3)
            if have_next:
                w1_ld_next = load_w1g((grp + 1) % 8, 1 - par)
            for dh in range(4):
                em.wait("tensor", "tA", ev.get(f"p3F{dh}", 0))
                last = None
                for i in range(NCH):
                    last = em.mm(p3F[dh][:],
                                 w1g[par][:, i * 512 + dh * 128:
                                          i * 512 + (dh + 1) * 128],
                                 h2b[:, i * 512:(i + 1) * 512],
                                 i == 0, i == NCH - 1,
                                 inc=("tP" if i == NCH - 1 else None))
                em.wait("scalar", "tP", last)
                na = em.act_ps(lambda e, dh=dh, grp=grp: e.activation(
                    gsb[:, (grp * 4 + dh) * 512:(grp * 4 + dh + 1) * 512],
                    p3F[dh][:], AF.Gelu,
                    bias=c.t_b1[:, grp * 4 + dh:grp * 4 + dh + 1], scale=1.0))
                ev[f"p3F{dh}"] = na
            ev[f"w1g{par}"] = em.cnt["tP"]
            if have_next:
                w1_ld = w1_ld_next
        n_gelu = em.cnt["tA"]
        em.wait("tensor", "tA", n_gelu)
        em.wait("tensor", "dG", n_w2)
        for oc in range(NCH):
            par = oc % 2
            em.wait("tensor", "tA", ev.get(f"p3F{par}", 0))
            last = None
            for i in range(32):
                last = em.mm(p3F[par][:],
                             w2sb[:, i * C + oc * 128:i * C + (oc + 1) * 128],
                             gsb[:, i * 512:(i + 1) * 512],
                             i == 0, i == 31,
                             inc=("tP" if i == 31 else None))
            em.wait("scalar", "tP", last)
            em.wait("scalar", "tV", ev.get(f"m2t{par}", 0))
            na = em.act_ps(lambda e, par=par: e.copy(m2t[par][:], p3F[par][:]))
            ev[f"p3F{par}"] = na
            nd = em.dma("sync", xnt3[par][:],
                        c.xnscr[oc, :, qt * 512:(qt + 1) * 512])
            em.wait("vector", "tA", na)
            em.wait("vector", "dS", nd)
            em.wait("vector", "dS", ev.get(f"acs{par}", 0))
            em.vec(lambda e, par=par, oc=oc: e.scalar_tensor_tensor(
                acs[par][:], m2t[par][:], c.t_b2[:, oc:oc + 1],
                xat3[:, oc * 512:(oc + 1) * 512], ALU.add, ALU.add))
            nv = em.vec(lambda e, par=par: e.tensor_add(
                acs[par][:], acs[par][:], xnt3[par][:]))
            ev[f"m2t{par}"] = nv
            em.wait("sync", "tV", nv)
            nd = em.dma("sync", c.accscr[oc, :, qt * 512:(qt + 1) * 512],
                        acs[par][:])
            ev[f"acs{par}"] = nd
        ev["xat3_use"] = em.cnt["tV"]

    c.phase_barrier()
    p3.close()


def emit_phase4(c):
    nc, em = c.nc, c.em
    p4 = ExitStack()
    E4 = p4.enter_context
    qh = [E4(nc.sbuf_tensor(f"qh{i}", [128, SEG], BF16)) for i in range(2)]
    kh = [E4(nc.sbuf_tensor(f"kh{i}", [128, T], BF16)) for i in range(2)]
    vh = [E4(nc.sbuf_tensor(f"vh{i}", [128, T], BF16)) for i in range(2)]
    pt = [E4(nc.sbuf_tensor(f"pt{i}", [128, 8 * 1024], BF16)) for i in range(3)]
    outh = E4(nc.sbuf_tensor("outh", [128, SEG], F32))
    p4tmp = E4(nc.sbuf_tensor("p4tmp", [128, 512], F32))
    lrow = E4(nc.sbuf_tensor("lrow", [4, 512], F32))
    rrow = E4(nc.sbuf_tensor("rrow", [4, 512], F32))
    rrowr = E4(nc.sbuf_tensor("rrowr", [4, 512], F32R))
    rflat = E4(nc.sbuf_tensor("rflat", [1, 512], F32R))
    rbc = E4(nc.sbuf_tensor("rbc", [128, SEG], F32))
    c.attn_out = c.ex.enter_context(
        nc.sbuf_tensor("attn_out", [128, NCH * SEG], BF16, side="right"))
    pA = E4(nc.psum_tensor("pA", [128, 1024], F32))
    pB = E4(nc.psum_tensor("pB", [128, 1024], F32))
    pO = [E4(nc.psum_tensor(f"pO{i}", [128, 512], F32)) for i in range(2)]
    pL = E4(nc.psum_tensor("pL", [128, 512], F32))
    pX = E4(nc.psum_tensor("pX", [128, 512], F32))

    ev = c.bank_evac
    kv_use = [0, 0]
    ld_cnt = [0, 0]

    def load_head(h):
        par = h % 2
        em.wait("sync", "tP", kv_use[par])
        em.dma("sync", qh[par][:], c.qscr[h * 128:(h + 1) * 128, :])
        em.dma("sync", kh[par][:], c.kscr[h * 128:(h + 1) * 128, :])
        vsrc = c.vscr[:, h, :].rearrange("(kt p) d -> p kt d", p=128)
        vdst = vh[par][:].rearrange("p (kt d) -> p kt d", d=128)
        n = None
        for g in range(8):
            n = em.dma("sync", vdst[:, g * 8:(g + 1) * 8, :],
                       vsrc[:, g * 8:(g + 1) * 8, :])
        ld_cnt[par] = n

    load_head(0)
    first_exp_emitted = False

    for h in range(8):
        par = h % 2
        if h < 7:
            load_head(h + 1)
        em.wait("tensor", "dS", ld_cnt[par])
        pvl_done = {}
        exp_done = {}

        def sc_exp(qhalf, sbi, gi):
            nonlocal first_exp_emitted
            last_exp = 0
            for kk in range(8):
                gk = sbi * 8 + kk
                pS, psname = (pA, "pA") if gk % 2 == 0 else (pB, "pB")
                em.wait("tensor", "tA", ev.get(psname, 0))
                em.mm(pS[:, 0:512], kh[par][:, gk * 128:(gk + 1) * 128],
                      qh[par][:, qhalf * 1024:qhalf * 1024 + 512],
                      True, True)
                nmm = em.mm(pS[:, 512:1024],
                            kh[par][:, gk * 128:(gk + 1) * 128],
                            qh[par][:, qhalf * 1024 + 512:(qhalf + 1) * 1024],
                            True, True, inc="tP")
                if not first_exp_emitted:
                    em.wait("scalar", "tP", nmm)
                    na = em.act_ps(lambda e: e.copy(p4tmp[:], pA[:, 0:512]))
                    em.wait("vector", "tA", na)
                    nv = em.vec(lambda e: e.reduce_max(
                        c.t_c1[:], p4tmp[:], AX.X))
                    em.wait("tensor", "tV", nv)
                    ntr = em.tr(pX[0:1, 0:128], c.t_c1[:], c.t_id32[:],
                                inc="tP")
                    em.wait("scalar", "tP", ntr)
                    na = em.act_ps(lambda e: e.copy(c.t_c2[:], pX[0:1, 0:128]))
                    em.wait("vector", "tA", na)
                    em.vec(lambda e: e.reduce_max(c.t_c3[:], c.t_c2[:], AX.X))
                    em.vec(lambda e: e.tensor_scalar_mul(
                        c.t_c3[:], c.t_c3[:], -SCALE_F))
                    nv = em.vec(lambda e: e.tensor_copy(
                        c.t_c3b[:], c.t_c3[:]))
                    em.wait("tensor", "tV", nv)
                    em.wait("tensor", "tA", ev.get("pX", 0))
                    nmm2 = em.mm(pX[:, 0:1], c.t_onesrb[:],
                                 c.t_c3b[:], True, True, inc="tP")
                    em.wait("scalar", "tP", nmm2)
                    na = em.act_ps(lambda e: e.copy(c.t_cb4[:], pX[:, 0:1]))
                    ev["pX"] = na
                    first_exp_emitted = True
                em.wait("scalar", "tP", nmm)
                em.wait("scalar", "tP", pvl_done.get(gi - 3, 0))
                na = em.act_ps(lambda e, pS=pS, gi=gi, kk=kk: e.activation(
                    pt[gi % 3][:, kk * 1024:(kk + 1) * 1024], pS[:],
                    AF.Exp, bias=c.t_cb4[:], scale=SCALE_F))
                ev[psname] = na
                last_exp = na
            exp_done[gi] = last_exp

        def pv_l(qhalf, sbi, gi):
            em.wait("tensor", "tA", exp_done[gi])
            if sbi == 0:
                em.wait("tensor", "tA", ev.get("pO0", 0))
                em.wait("tensor", "tA", ev.get("pO1", 0))
                if qhalf == 0:
                    em.wait("tensor", "tA", ev.get("pL", 0))
            last = None
            for qtl in range(2):
                qt = qhalf * 2 + qtl
                for kk in range(8):
                    gk = sbi * 8 + kk
                    em.mm(pO[qtl][:],
                          vh[par][:, gk * 128:(gk + 1) * 128],
                          pt[gi % 3][:, kk * 1024 + qtl * 512:
                                     kk * 1024 + (qtl + 1) * 512],
                          start=(sbi == 0 and kk == 0),
                          stop=(sbi == 7 and kk == 7))
                    last = em.mm(pL[0:4, :],
                                 c.t_sel[:, qt * 4:(qt + 1) * 4],
                                 pt[gi % 3][:, kk * 1024 + qtl * 512:
                                            kk * 1024 + (qtl + 1) * 512],
                                 start=(qhalf == 0 and sbi == 0 and kk == 0
                                        and qtl == 0),
                                 stop=(qhalf == 1 and sbi == 7 and kk == 7
                                       and qtl == 1),
                                 inc=("tP" if (qtl == 1 and kk == 7)
                                      else None))
            pvl_done[gi] = last

        gis = [(qh_, sb_) for qh_ in range(2) for sb_ in range(8)]
        sc_exp(*gis[0], 0)
        for i, (qh_, sb_) in enumerate(gis):
            if i + 1 < len(gis):
                sc_exp(*gis[i + 1], i + 1)
            pv_l(qh_, sb_, i)
            if sb_ == 7:
                em.wait("scalar", "tP", pvl_done[i])
                for qtl in range(2):
                    qt = qh_ * 2 + qtl
                    na = em.act_ps(lambda e, qtl=qtl, qt=qt: e.copy(
                        outh[:, qt * 512:(qt + 1) * 512], pO[qtl][:]))
                    ev[f"pO{qtl}"] = na
        em.wait("scalar", "tP", pvl_done[15])
        na = em.act_ps(lambda e: e.copy(lrow[:], pL[0:4, :]))
        ev["pL"] = na
        em.wait("vector", "tA", na)
        em.vec(lambda e: e.reciprocal(rrow[:], lrow[:]))
        nv = em.vec(lambda e: e.tensor_copy(rrowr[:], rrow[:]))
        rb_ev = []
        for qt in range(4):
            em.wait("sync", "tV", nv)
            em.wait("sync", "tP", ev.get("rflat_use", 0))
            nd = em.dma("sync", rflat[:], rrowr[qt:qt + 1, :])
            em.wait("tensor", "dS", nd)
            em.wait("tensor", "tA", ev.get("pX", 0))
            nmm = em.mm(pX[:], c.t_cmcol[0:1, h * 128:(h + 1) * 128],
                        rflat[0:1, :], True, True, inc="tP")
            ev["rflat_use"] = nmm
            em.wait("scalar", "tP", nmm)
            na = em.act_ps(lambda e, qt=qt: e.copy(
                rbc[:, qt * 512:(qt + 1) * 512], pX[:]))
            ev["pX"] = na
            rb_ev.append(na)
        em.wait("vector", "tA", rb_ev[-1])
        nv = em.vec(lambda e, h=h: e.tensor_mul(
            c.attn_out[:, h * SEG:(h + 1) * SEG], outh[:], rbc[:]))
        kv_use[par] = em.cnt["tP"]
        em.wait("scalar", "tV", nv)

    if DBG:
        c.phase_barrier()
        dbg = nc.dram_tensor("dbg_attn", [128, NCH * SEG], BF16,
                             kind="ExternalOutput").ap()
        nd = em.dma("sync", dbg, c.attn_out[:])
        em.wait("sync", "dS", nd)

    c.phase_barrier()
    p4.close()


def emit_phase5(c):
    nc, em = c.nc, c.em
    p5 = ExitStack()
    E5 = p5.enter_context
    owT = E5(nc.sbuf_tensor("owT", [128, NCH * C], BF16))
    wpan5 = E5(nc.sbuf_tensor("wpan5", [128, C], BF16))
    y2sb = E5(nc.sbuf_tensor("y2sb", [128, NCH * 512], F32))
    ot5 = [E5(nc.sbuf_tensor(f"ot5{i}", [128, 512], F32)) for i in range(2)]
    acct = [E5(nc.sbuf_tensor(f"acct{i}", [128, 512], F32)) for i in range(2)]
    ytile = [E5(nc.sbuf_tensor(f"ytile{i}", [128, C], F32)) for i in range(2)]
    yout = [E5(nc.sbuf_tensor(f"yout{i}", [128, C], F32)) for i in range(2)]
    xqt = [E5(nc.sbuf_tensor(f"xqt{i}", [128, C], F32)) for i in range(2)]
    p5a = [E5(nc.psum_tensor(f"p5a{i}", [128, 512], F32)) for i in range(2)]
    p5t = [E5(nc.psum_tensor(f"p5t{i}", [128, 512], F32)) for i in range(2)]
    ev = c.bank_evac

    for r in range(8):
        em.wait("gpsimd", "tP", em.cnt["tP"])
        n = em.dma("gpsimd", wpan5[:], c.ow[r * 128:(r + 1) * 128, :])
        em.wait("tensor", "dG", n)
        for g in range(2):
            em.wait("tensor", "tA", ev.get(f"p5t{g}", 0))
            last = None
            for i in range(4):
                mc = g * 4 + i
                last = em.tr(p5t[g][:].bitcast(BF16)[:, i * 128:(i + 1) * 128],
                             wpan5[:, mc * 128:(mc + 1) * 128],
                             c.t_idb[:], inc=(None if i < 3 else "tP"))
            em.wait("scalar", "tP", last)
            dv = owT[:].rearrange("p (mc w) -> p mc w", w=C)
            na = em.act_ps(lambda e, g=g, r=r, dv=dv: e.copy(
                dv[:, g * 4:g * 4 + 4, r * 128:(r + 1) * 128],
                p5t[g][:].bitcast(BF16)[:, 0:512].rearrange(
                    "p (a w) -> p a w", w=128)))
            ev[f"p5t{g}"] = na

    if P5_PART < 1.5:
        nd = em.dma("sync", c.y[0:128, :], c.xq[0:128, :])
        em.wait("sync", "dS", nd)
        c.phase_barrier()
        p5.close()
        return
    em.wait("tensor", "tA", em.cnt["tA"])
    nlast_y = None
    for qt in range(4):
        em.wait("vector", "tP", ev.get("y2sb_use", 0))
        for oc in range(NCH):
            par = oc % 2
            em.wait("tensor", "tA", ev.get(f"p5a{par}", 0))
            last = None
            for i in range(NCH):
                last = em.mm(p5a[par][:],
                             owT[:, i * C + oc * 128:i * C + (oc + 1) * 128],
                             c.attn_out[:, i * SEG + qt * 512:
                                        i * SEG + (qt + 1) * 512],
                             i == 0, i == NCH - 1,
                             inc=("tP" if i == NCH - 1 else None))
            em.wait("scalar", "tP", last)
            em.wait("scalar", "tV", ev.get(f"ot5{par}", 0))
            na = em.act_ps(lambda e, par=par: e.copy(ot5[par][:], p5a[par][:]))
            ev[f"p5a{par}"] = na
            em.wait("sync", "tV", ev.get(f"acct{par}", 0))
            nd = em.dma("sync", acct[par][:],
                        c.accscr[oc, :, qt * 512:(qt + 1) * 512])
            em.wait("vector", "tA", na)
            em.wait("vector", "dS", nd)
            em.vec(lambda e, par=par, oc=oc: e.scalar_tensor_tensor(
                y2sb[:, oc * 512:(oc + 1) * 512], ot5[par][:],
                c.t_ob[:, oc:oc + 1],
                c.t_cm[:, oc:oc + 1].broadcast_to([128, 512]),
                ALU.add, ALU.mult))
            nv = em.vec(lambda e, par=par, oc=oc: e.tensor_add(
                y2sb[:, oc * 512:(oc + 1) * 512],
                y2sb[:, oc * 512:(oc + 1) * 512], acct[par][:]))
            ev[f"ot5{par}"] = nv
            ev[f"acct{par}"] = nv
        n_y2 = em.cnt["tV"]
        em.wait("tensor", "tV", n_y2)
        if P5_PART < 2.5:
            continue
        for tt in range(4):
            ti = qt * 4 + tt
            par = tt % 2
            em.wait("sync", "tV", ev.get(f"xqt{par}", 0))
            ndx = em.dma("sync", xqt[par][:],
                         c.xq[ti * 128:(ti + 1) * 128, :])
            for g in range(2):
                em.wait("tensor", "tA", ev.get(f"p5t{g}", 0))
                last = None
                for i in range(4):
                    ch = g * 4 + i
                    last = em.tr(p5t[g][:, i * 128:(i + 1) * 128],
                                 y2sb[:, ch * 512 + tt * 128:
                                      ch * 512 + (tt + 1) * 128],
                                 c.t_id32[:], inc=(None if i < 3 else "tP"))
                em.wait("scalar", "tP", last)
                em.wait("scalar", "tV", ev.get(f"ytile{par}", 0))
                na = em.act_ps(lambda e, g=g, par=par: e.copy(
                    ytile[par][:, g * 512:(g + 1) * 512], p5t[g][:]))
                ev[f"p5t{g}"] = na
            if P5_PART < 2.8:
                continue
            em.wait("vector", "tA", em.cnt["tA"])
            em.wait("vector", "dS", ndx)
            em.wait("vector", "dS", ev.get(f"yout{par}", 0))
            nv = em.vec(lambda e, par=par: e.tensor_add(
                yout[par][:], ytile[par][:], xqt[par][:]))
            ev[f"ytile{par}"] = nv
            ev[f"xqt{par}"] = nv
            em.wait("sync", "tV", nv)
            nlast_y = em.dma("sync", c.y[ti * 128:(ti + 1) * 128, :],
                             yout[par][:])
            ev[f"yout{par}"] = nlast_y
        ev["y2sb_use"] = em.cnt["tP"]

    if P5_PART < 2.9:
        nlast_y = None
    if nlast_y is None:
        nlast_y = em.dma("sync", c.y[0:128, :], c.xq[0:128, :])
    em.wait("sync", "dS", nlast_y)
    c.phase_barrier()
    p5.close()


def make_masks():
    mA = np.zeros((128, 128), dtype=np.float32)
    mB = np.zeros((64, 128), dtype=np.float32)
    for r in range(128):
        for cc in range(192):
            ok = (cc <= 64 + r) and (r < 64 or cc >= 64)
            if ok:
                if cc < 128:
                    mA[cc, r] = 1.0
                else:
                    mB[cc - 128, r] = 1.0
    return mA.astype(ml_dtypes.bfloat16), mB.astype(ml_dtypes.bfloat16)


def make_selmat():
    s = np.zeros((128, 4, 4), dtype=np.float32)
    for t in range(4):
        s[:, t, t] = 1.0
    return s.reshape(128, 16).astype(ml_dtypes.bfloat16)


_program_cache = {}


def get_program(stages=STAGES):
    key = (stages, DBG)
    if key not in _program_cache:
        _program_cache[key] = build_program(stages)
    return _program_cache[key]


def make_inmaps(inputs):
    x = np.ascontiguousarray(inputs["L0_attn_x"], dtype=np.float32)
    mA, mB = make_masks()
    common = {
        "wqkv": np.ascontiguousarray(inputs["L0_attn_w_qkv"]),
        "bqkv": np.ascontiguousarray(inputs["L0_attn_b_qkv"]),
        "wout": np.ascontiguousarray(inputs["L0_attn_w_out"]),
        "bout": np.ascontiguousarray(inputs["L0_attn_b_out"]),
        "ln1g": np.ascontiguousarray(inputs["L0_attn_ln1_g"]),
        "ln1b": np.ascontiguousarray(inputs["L0_attn_ln1_b"]),
        "ln2g": np.ascontiguousarray(inputs["L0_attn_ln2_g"]),
        "ln2b": np.ascontiguousarray(inputs["L0_attn_ln2_b"]),
        "w1": np.ascontiguousarray(inputs["L0_attn_w1"]),
        "b1": np.ascontiguousarray(inputs["L0_attn_b1"]),
        "w2": np.ascontiguousarray(inputs["L0_attn_w2"]),
        "b2": np.ascontiguousarray(inputs["L0_attn_b2"]),
        "nw": np.ascontiguousarray(inputs["L0_ffn_norm_w"]),
        "nb": np.ascontiguousarray(inputs["L0_ffn_norm_b"]),
        "qw": np.ascontiguousarray(inputs["L0_ffn_q_w"]),
        "qbias": np.ascontiguousarray(inputs["L0_ffn_q_b"]),
        "kw": np.ascontiguousarray(inputs["L0_ffn_k_w"]),
        "kbias": np.ascontiguousarray(inputs["L0_ffn_k_b"]),
        "vw": np.ascontiguousarray(inputs["L0_ffn_v_w"]),
        "vbias": np.ascontiguousarray(inputs["L0_ffn_v_b"]),
        "ow": np.ascontiguousarray(inputs["L0_ffn_o_w"]),
        "obias": np.ascontiguousarray(inputs["L0_ffn_o_b"]),
        "ident32": np.eye(128, dtype=np.float32),
        "identb": np.eye(128, dtype=ml_dtypes.bfloat16),
        "ones1": np.ones((1, 128), dtype=np.float32),
        "onesb": np.ones((128, 1), dtype=ml_dtypes.bfloat16),
        "onesrb": np.ones((1, 128), dtype=ml_dtypes.bfloat16),
        "selmat": make_selmat(),
        "maskA": mA, "maskB": mB,
    }
    in_maps = []
    for core in range(8):
        b, e = core // 4, core % 4
        m_e = C >> e
        cm = np.zeros(C, dtype=np.float32); cm[:m_e] = 1.0
        d = dict(common)
        d["xb"] = x[b]
        d["xq"] = x[b, e * SEG:(e + 1) * SEG]
        d["xh"] = (x[b, e * SEG - 64:e * SEG] if e > 0
                   else np.zeros((64, C), dtype=np.float32))
        d["cmaskv"] = cm
        d["cmcol"] = cm[None, :]
        d["hmaskv"] = np.full((128, 1), 1.0 if e > 0 else 0.0, dtype=np.float32)
        in_maps.append(d)
    return in_maps


def kernel(**inputs):
    nc = get_program()
    in_maps = make_inmaps(inputs)
    res = run_bass_kernel_spmd(nc, in_maps, core_ids=list(range(8)))
    out = np.empty((B, T, C), dtype=np.float32)
    for core in range(8):
        b, e = core // 4, core % 4
        out[b, e * SEG:(e + 1) * SEG] = res.results[core]["y"]
    return out


# version 15
# speedup vs baseline: 1.1769x; 1.1081x over previous
import numpy as np
import ml_dtypes
from contextlib import ExitStack
from types import SimpleNamespace as NS
import concourse.bass as bass
import concourse.mybir as mybir
from concourse.bass_utils import run_bass_kernel_spmd

F32 = mybir.dt.float32
BF16 = mybir.dt.bfloat16
F32R = mybir.dt.float32r
AF = mybir.ActivationFunctionType
ALU = mybir.AluOpType
AX = mybir.AxisListType

DBG = False
STAGES = 5
PARANOID = False
P5_PART = 3

B, T, C = 2, 8192, 1024
SEG = 2048
NCH = C // 128
MCH_J = [8, 4, 2, 1]
NQT = SEG // 512
NTT = SEG // 128
EPS = 1e-5
SCALE_F = 1.0 / 32.0
SCALE_L = 0.125


class EM:
    ENGS = ("sync", "gpsimd", "tensor", "vector", "scalar")

    def __init__(self, nc, ex):
        self.nc = nc
        self.ops = {k: [] for k in self.ENGS}
        self.sems = {}
        self.cnt = {}
        self.lastw = {k: {} for k in self.ENGS}
        for s in ("dS", "dG", "tP", "tA", "tV"):
            self.sems[s] = ex.enter_context(nc.semaphore(s))
            self.cnt[s] = 0

    def raw(self, eng, fn):
        self.ops[eng].append(fn)

    def wait(self, eng, sem, val):
        if val is None or val <= 0:
            return
        if self.lastw[eng].get(sem, 0) >= val:
            return
        self.lastw[eng][sem] = val
        h = self.sems[sem]
        self.raw(eng, lambda e, h=h, v=val: e.wait_ge(h, v))

    def op(self, eng, fn, inc=None, amt=1):
        if PARANOID is True or (PARANOID and eng in PARANOID):
            sems = ("dS", "dG", "tP", "tA", "tV")
            if isinstance(PARANOID, dict):
                sems = PARANOID[eng]
            for s in sems:
                self.wait(eng, s, self.cnt[s])
            if inc is None:
                inc = {"sync": "dS", "gpsimd": "dG", "tensor": "tP",
                       "vector": "tV", "scalar": "tA"}[eng]
                amt = 16 if inc in ("dS", "dG") else 1
                if eng in ("sync", "gpsimd"):
                    pass
        if inc is None:
            self.raw(eng, fn)
            return None
        self.cnt[inc] += amt
        h = self.sems[inc]
        self.raw(eng, lambda e, fn=fn, h=h, a=amt: fn(e).then_inc(h, a))
        return self.cnt[inc]

    def dma(self, q, out, in_, slow=False):
        inc = "dS" if q == "sync" else "dG"
        return self.op(q, lambda e, o=out, i=in_, s=slow: e.dma_start(
            out=o, in_=i, allow_slow_non_contiguous=s), inc=inc, amt=16)

    def mm(self, out, lhsT, rhs, start, stop, inc=None, tile=None):
        return self.op("tensor",
                       lambda e, o=out, l=lhsT, r=rhs, s0=start, s1=stop,
                       tp=tile: e.matmul(o, l, r, start=s0, stop=s1,
                                         tile_position=tp), inc=inc)

    def tr(self, out, in_, ident, inc=None):
        return self.op("tensor",
                       lambda e, o=out, i=in_, d=ident: e.transpose(o, i, d),
                       inc=inc)

    def act_ps(self, fn):
        n = self.op("scalar", fn, inc="tA")
        self.wait("scalar", "tA", n - 2)
        return n

    def act(self, fn):
        return self.op("scalar", fn, inc="tA")

    def vec(self, fn):
        self.wait("vector", "tV", self.cnt["tV"])
        return self.op("vector", fn, inc="tV")


def build_program(stages=STAGES):
    nc = bass.Bass("TRN2", target_bir_lowering=False)
    ex = ExitStack()
    c = NS()
    c.nc, c.ex = nc, ex

    di = lambda n, s, d=F32: nc.dram_tensor(n, s, d, kind="ExternalInput").ap()
    do = lambda n, s, d=F32: nc.dram_tensor(n, s, d, kind="ExternalOutput").ap()
    dscr = lambda n, s, d=F32: nc.dram_tensor(
        n, s, d, kind=("ExternalOutput" if DBG else "Internal")).ap()

    c.xb = di("xb", [T, C]); c.xq = di("xq", [SEG, C]); c.xh = di("xh", [64, C])
    c.wqkv = di("wqkv", [C, 3 * C]); c.bqkv = di("bqkv", [3 * C])
    c.wout = di("wout", [C, C]); c.bout = di("bout", [C])
    c.ln1g = di("ln1g", [C]); c.ln1b = di("ln1b", [C])
    c.ln2g = di("ln2g", [C]); c.ln2b = di("ln2b", [C])
    c.w1 = di("w1", [C, 4 * C]); c.b1 = di("b1", [4 * C])
    c.w2 = di("w2", [4 * C, C]); c.b2 = di("b2", [C])
    c.nw = di("nw", [C]); c.nb = di("nb", [C])
    c.qw = di("qw", [C, C]); c.qbias = di("qbias", [C])
    c.kw = di("kw", [C, C]); c.kbias = di("kbias", [C])
    c.vw = di("vw", [C, C]); c.vbias = di("vbias", [C])
    c.ow = di("ow", [C, C]); c.obias = di("obias", [C])
    c.ident32 = di("ident32", [128, 128])
    c.identb = di("identb", [128, 128], BF16)
    c.ones1 = di("ones1", [1, 128])
    c.onesb = di("onesb", [128, 1], BF16)
    c.onesrb = di("onesrb", [1, 128], BF16)
    c.selmat = di("selmat", [128, 16], BF16)
    c.maskA = di("maskA", [128, 128], BF16)
    c.maskB = di("maskB", [64, 128], BF16)
    c.cmaskv = di("cmaskv", [C])
    c.cmcol = di("cmcol", [1, C])
    c.hmaskv = di("hmaskv", [128, 1])

    c.y = do("y", [SEG, C])

    c.kscr = dscr("kscr", [C, T], BF16)
    c.vscr = dscr("vscr", [T, 8, 128], BF16)
    c.qscr = dscr("qscr", [C, SEG], BF16)
    c.xnscr = dscr("xnscr", [NCH, 128, SEG])
    c.xascr = dscr("xascr", [NCH, 128, SEG])
    c.accscr = dscr("accscr", [NCH, 128, SEG])

    em = EM(nc, ex)
    c.em = em
    E = ex.enter_context
    sbt = lambda n, s, d=F32: E(nc.sbuf_tensor(n, s, d))

    c.t_bqkv = sbt("t_bqkv", [128, 24])
    c.t_bout = sbt("t_bout", [128, 8])
    c.t_b1 = sbt("t_b1", [128, 32])
    c.t_b2 = sbt("t_b2", [128, 8])
    c.t_ln2g = sbt("t_ln2g", [128, 8]); c.t_ln2b = sbt("t_ln2b", [128, 8])
    c.t_qb = sbt("t_qb", [128, 8]); c.t_kb = sbt("t_kb", [128, 8])
    c.t_ob = sbt("t_ob", [128, 8])
    c.t_cm = sbt("t_cm", [128, 8])
    c.t_hm = sbt("t_hm", [128, 1])
    c.t_eps = sbt("t_eps", [128, 1])
    c.t_id32 = sbt("t_id32", [128, 128])
    c.t_idb = sbt("t_idb", [128, 128], BF16)
    c.t_ones1 = sbt("t_ones1", [1, 128], F32R)
    c.t_onesb = sbt("t_onesb", [128, 1], BF16)
    c.t_onesrb = sbt("t_onesrb", [1, 128], BF16)
    c.t_c3b = sbt("t_c3b", [1, 1], BF16)
    c.t_sel = sbt("t_sel", [128, 16], BF16)
    c.t_mA = sbt("t_mA", [128, 128], BF16)
    c.t_mB = sbt("t_mB", [64, 128], BF16)
    c.t_cmcol = sbt("t_cmcol", [1, C], F32R)
    c.t_cb4 = sbt("t_cb4", [128, 1])
    c.t_cb2 = sbt("t_cb2", [128, 1])
    c.t_c1 = sbt("t_c1", [128, 1])
    c.t_c2 = sbt("t_c2", [1, 128])
    c.t_c3 = sbt("t_c3", [1, 1])
    p1x = ExitStack()
    sb1 = lambda n, sh, d=F32: p1x.enter_context(nc.sbuf_tensor(n, sh, d))
    c.t_vbrep = sb1("t_vbrep", [128, C])
    c.t_nwrep = sb1("t_nwrep", [128, C]); c.t_nbrep = sb1("t_nbrep", [128, C])
    c.t_nwmrep = sb1("t_nwmrep", [128, C]); c.t_nbmrep = sb1("t_nbmrep", [128, C])
    c.t_l1grep = sb1("t_l1grep", [128, C]); c.t_l1brep = sb1("t_l1brep", [128, C])
    c.t_cmrep = sb1("t_cmrep", [128, C])
    c.t_st = [sb1(f"t_st{i}", [128, 32]) for i in range(2)]
    c.t_sq = [sb1(f"t_sq{i}", [128, 1024]) for i in range(2)]
    c.t_xt = [sb1(f"t_xt{i}", [128, 1024]) for i in range(2)]
    c.t_xn = [sb1(f"t_xn{i}", [128, 1024]) for i in range(2)]
    c.t_t2 = [sb1(f"t_t2{i}", [128, 1024], BF16) for i in range(2)]
    c.t_th = [sb1("t_th0", [128, 1024], BF16)] * 2
    c.t_row = sb1("t_row", [1, C], F32R)

    c.PS = [p1x.enter_context(nc.psum_tensor(f"ps{i}", [128, 512], F32))
            for i in range(8)]
    c.psb = lambda i: c.PS[i][:].bitcast(BF16)
    c.bank_evac = {}

    def bank_ready(bank):
        em.wait("tensor", "tA", c.bank_evac.get(bank, 0))
    c.bank_ready = bank_ready

    for (tile, src, n) in ((c.t_bqkv, c.bqkv, 24), (c.t_bout, c.bout, 8),
                           (c.t_b1, c.b1, 32), (c.t_b2, c.b2, 8),
                           (c.t_ln2g, c.ln2g, 8), (c.t_ln2b, c.ln2b, 8),
                           (c.t_qb, c.qbias, 8), (c.t_kb, c.kbias, 8),
                           (c.t_ob, c.obias, 8), (c.t_cm, c.cmaskv, 8)):
        ap = bass.AP(src.tensor, 0, [[1, 128], [128, n]])
        em.dma("sync", tile[:], ap, slow=True)
    em.dma("sync", c.t_hm[:], c.hmaskv)
    em.dma("sync", c.t_id32[:], c.ident32)
    em.dma("sync", c.t_idb[:], c.identb)
    em.dma("sync", c.t_ones1[:], c.ones1.bitcast(F32R))
    em.dma("sync", c.t_onesb[:], c.onesb)
    em.dma("sync", c.t_onesrb[:], c.onesrb)
    em.dma("sync", c.t_sel[:], c.selmat)
    em.dma("sync", c.t_mA[:], c.maskA)
    em.dma("sync", c.t_mB[:], c.maskB)
    n_ld0 = em.dma("sync", c.t_cmcol[:], c.cmcol.bitcast(F32R))
    em.wait("vector", "dS", n_ld0)
    em.vec(lambda e: e.memset(c.t_eps[:], EPS))
    em.wait("tensor", "dS", n_ld0)

    def replicate_row(src_vec, dst):
        em.wait("sync", "tP", em.cnt["tP"])
        n = em.dma("sync", c.t_row[:],
                   bass.AP(src_vec.tensor, 0, [[0, 1], [1, C]]).bitcast(F32R))
        em.wait("tensor", "dS", n)
        last = None
        for half in range(2):
            bank_ready(half)
            last = em.mm(c.PS[half][:], c.t_ones1[:],
                         c.t_row[0:1, half * 512:(half + 1) * 512],
                         True, True, inc="tP")
        em.wait("scalar", "tP", last)
        for half in range(2):
            na = em.act_ps(lambda e, h=half, d=dst: e.copy(
                d[:, h * 512:(h + 1) * 512], c.PS[h][:]))
            c.bank_evac[half] = na

    replicate_row(c.vbias, c.t_vbrep)
    replicate_row(c.nw, c.t_nwrep)
    replicate_row(c.nb, c.t_nbrep)
    replicate_row(c.ln1g, c.t_l1grep)
    replicate_row(c.ln1b, c.t_l1brep)
    replicate_row(c.cmaskv, c.t_cmrep)
    em.wait("vector", "tA", em.cnt["tA"])
    em.vec(lambda e: e.tensor_mul(c.t_nwmrep[:], c.t_nwrep[:], c.t_cmrep[:]))
    n_nwm = em.vec(lambda e: e.tensor_mul(c.t_nbmrep[:], c.t_nbrep[:],
                                          c.t_cmrep[:]))
    c.n_init_v = n_nwm

    def ln_stats(src_ap, P, col, st, sq):
        em.vec(lambda e: e.reduce_sum(st[0:P, col + 3:col + 4], src_ap, AX.X))
        na = em.act(lambda e: e.activation(
            sq[0:P, :], src_ap, AF.Square, bias=0.0, scale=1.0,
            accum_out=st[0:P, col + 4:col + 5]))
        em.wait("vector", "tA", na)
        em.vec(lambda e: e.tensor_scalar_mul(
            st[0:P, col:col + 1], st[0:P, col + 3:col + 4], 1.0 / C))
        em.vec(lambda e: e.tensor_scalar_mul(
            st[0:P, col + 5:col + 6], st[0:P, col + 4:col + 5], 1.0 / C))
        em.vec(lambda e: e.tensor_mul(
            st[0:P, col + 6:col + 7], st[0:P, col:col + 1],
            st[0:P, col:col + 1]))
        nv = em.vec(lambda e: e.tensor_sub(
            st[0:P, col + 7:col + 8], st[0:P, col + 5:col + 6],
            st[0:P, col + 6:col + 7]))
        em.wait("scalar", "tV", nv)
        na = em.act(lambda e: e.activation(
            st[0:P, col + 8:col + 9], st[0:P, col + 7:col + 8],
            AF.Sqrt, bias=c.t_eps[0:P, :], scale=1.0))
        em.wait("vector", "tA", na)
        em.vec(lambda e: e.reciprocal(
            st[0:P, col + 1:col + 2], st[0:P, col + 8:col + 9]))
        nv = em.vec(lambda e: e.scalar_tensor_tensor(
            st[0:P, col + 2:col + 3], st[0:P, col:col + 1], -1.0,
            st[0:P, col + 1:col + 2], ALU.mult, ALU.mult))
        return nv

    def norm_apply_affine(P, grep, brep, out_bf, st, sq, xn):
        em.vec(lambda e: e.tensor_mul(sq[0:P, :], xn[0:P, :], grep[0:P, :]))
        return em.vec(lambda e: e.scalar_tensor_tensor(
            out_bf[0:P, :], sq[0:P, :], st[0:P, 17:18],
            brep[0:P, :], ALU.mult, ALU.add))

    def phase_barrier():
        for eng in EM.ENGS:
            for s in ("dS", "dG", "tP", "tA", "tV"):
                em.wait(eng, s, em.cnt[s])
    c.phase_barrier = phase_barrier

    c.ln_stats = ln_stats
    c.norm_apply_affine = norm_apply_affine

    p1 = ExitStack()
    E1 = p1.enter_context
    c.kwT = E1(nc.sbuf_tensor("kwT", [128, NCH * C], BF16))
    c.vwT = E1(nc.sbuf_tensor("vwT", [128, NCH * C], BF16))
    c.wpan = E1(nc.sbuf_tensor("wpan", [128, C], BF16))
    c.xnn = E1(nc.sbuf_tensor("xnn", [128, 12 * SEG], BF16))
    c.kstg = [E1(nc.sbuf_tensor(f"kstg{i}", [128, 512], BF16)) for i in range(2)]
    c.vtmp = E1(nc.sbuf_tensor("vtmp", [128, 512], F32))
    c.vstg = [E1(nc.sbuf_tensor(f"vstg{i}", [128, 512], BF16)) for i in range(2)]
    c.qstg = [E1(nc.sbuf_tensor(f"qstg{i}", [128, 512], BF16)) for i in range(2)]
    c.xstg = [E1(nc.sbuf_tensor(f"xstg{i}", [128, 512], F32)) for i in range(2)]
    c.h_ctx = ExitStack()
    c.h_fm = c.h_ctx.enter_context(
        nc.sbuf_tensor("h_fm", [128, NCH * (SEG + 64)], BF16,
                       side="right"))
    HW = SEG + 64

    def build_wT(src, dst_sb):
        for r in range(8):
            em.wait("gpsimd", "tP", em.cnt["tP"])
            n = em.dma("gpsimd", c.wpan[:], src[r * 128:(r + 1) * 128, :])
            em.wait("tensor", "dG", n)
            for g in range(2):
                bank = g
                bank_ready(bank)
                last = None
                for i in range(4):
                    mc = g * 4 + i
                    last = em.tr(c.psb(bank)[:, i * 128:(i + 1) * 128],
                                 c.wpan[:, mc * 128:(mc + 1) * 128],
                                 c.t_idb[:], inc=(None if i < 3 else "tP"))
                em.wait("scalar", "tP", last)
                dv = dst_sb[:].rearrange("p (mc w) -> p mc w", w=C)
                na = em.act_ps(lambda e, b=bank, g=g, r=r, dv=dv: e.copy(
                    dv[:, g * 4:g * 4 + 4, r * 128:(r + 1) * 128],
                    c.psb(b)[:, 0:512].rearrange("p (a w) -> p a w", w=128)))
                c.bank_evac[bank] = na

    build_wT(c.kw, c.kwT)
    build_wT(c.vw, c.vwT)

    c.tile_users = [dict(tA=0, tV=0, tP=0), dict(tA=0, tV=0, tP=0)]

    def stream_tile(src_ap, P, grep, brep, par):
        u = c.tile_users[par]
        em.wait("sync", "tA", u["tA"])
        em.wait("sync", "tV", u["tV"])
        em.wait("sync", "tP", u["tP"])
        st, sq, xt, xn = c.t_st[par], c.t_sq[par], c.t_xt[par], c.t_xn[par]
        n = em.dma("sync", xt[0:P, :], src_ap)
        em.wait("vector", "dS", n)
        em.wait("scalar", "dS", n)
        nv0 = ln_stats(xt[0:P, :], P, 0, st, sq)
        em.wait("scalar", "tV", nv0)
        na = em.act(lambda e, P=P, st=st, xt=xt, xn=xn: e.activation(
            xn[0:P, :], xt[0:P, :], AF.Identity,
            bias=st[0:P, 2:3], scale=st[0:P, 1:2]))
        em.wait("vector", "tA", na)
        em.vec(lambda e, P=P, st=st: e.tensor_mul(
            st[0:P, 19:20], st[0:P, 1:2], st[0:P, 1:2]))
        nv = em.vec(lambda e, P=P, st=st: e.tensor_mul(
            st[0:P, 23:24], st[0:P, 7:8], st[0:P, 19:20]))
        em.wait("scalar", "tV", nv)
        na = em.act(lambda e, P=P, st=st: e.activation(
            st[0:P, 24:25], st[0:P, 23:24], AF.Sqrt,
            bias=c.t_eps[0:P, :], scale=1.0))
        em.wait("vector", "tA", na)
        em.vec(lambda e, P=P, st=st: e.reciprocal(
            st[0:P, 17:18], st[0:P, 24:25]))
        return norm_apply_affine(P, grep, brep, c.t_t2[par], st, sq, xn)

    CHOFF = {0: 0, 1: 8, 2: 0, 3: 2}
    def stream_block(j):
        mch = MCH_J[j]
        for t in range(NTT):
            par = (j * NTT + t) % 2
            nv = stream_tile(c.xb[j * SEG + t * 128: j * SEG + (t + 1) * 128, :],
                             128, c.t_nwrep, c.t_nbrep, par)
            em.wait("tensor", "tV", nv)
            bank = 2 + par
            bank_ready(bank)
            last = None
            for ch in range(mch):
                last = em.tr(c.psb(bank)[:, ch * 128:(ch + 1) * 128],
                             c.t_t2[par][:, ch * 128:(ch + 1) * 128],
                             c.t_idb[:], inc=(None if ch < mch - 1 else "tP"))
            em.wait("scalar", "tP", last)
            xv = c.xnn[:].rearrange("p (ch w) -> p ch w", w=SEG)
            na = em.act_ps(lambda e, b=bank, t=t, mch=mch, xv=xv, j=j: e.copy(
                xv[:, CHOFF[j]:CHOFF[j] + mch, t * 128:(t + 1) * 128],
                c.psb(b)[:].rearrange("p (ch w) -> p ch w", w=128)[:, 0:mch, :]))
            c.bank_evac[bank] = na
            c.tile_users[par] = dict(tA=na, tV=em.cnt["tV"], tP=last)

    def kv_block(j):
        mch = MCH_J[j]
        em.wait("tensor", "tA", em.cnt["tA"])
        for dk in range(8):
            for qt in range(NQT):
                bank = 4 + ((dk * NQT + qt) % 2)
                bank_ready(bank)
                last = None
                for i in range(mch):
                    last = em.mm(
                        c.PS[bank][:],
                        c.kwT[:, i * C + dk * 128: i * C + (dk + 1) * 128],
                        c.xnn[:, (CHOFF[j] + i) * SEG + qt * 512:
                              (CHOFF[j] + i) * SEG + (qt + 1) * 512],
                        i == 0, i == mch - 1,
                        inc=(None if i < mch - 1 else "tP"))
                em.wait("scalar", "tP", last)
                stg = c.kstg[(dk * NQT + qt) % 2]
                em.wait("scalar", "dS",
                        c.bank_evac.get(("kstg", (dk * NQT + qt) % 2), 0))
                na = em.act_ps(lambda e, b=bank, dk=dk, s=stg: e.activation(
                    s[:], c.PS[b][:], AF.Identity,
                    bias=c.t_kb[:, dk:dk + 1], scale=1.0))
                c.bank_evac[bank] = na
                em.wait("sync", "tA", na)
                nd = em.dma("sync",
                            c.kscr[dk * 128:(dk + 1) * 128,
                                   j * SEG + qt * 512: j * SEG + (qt + 1) * 512],
                            stg[:])
                c.bank_evac[("kstg", (dk * NQT + qt) % 2)] = nd

        for tt in range(NTT):
            for hf in range(2):
                bank = 6 + ((tt * 2 + hf) % 2)
                bank_ready(bank)
                last = None
                for i in range(mch):
                    last = em.mm(
                        c.PS[bank][:],
                        c.xnn[:, (CHOFF[j] + i) * SEG + tt * 128:
                              (CHOFF[j] + i) * SEG + (tt + 1) * 128],
                        c.vwT[:, i * C + hf * 512: i * C + (hf + 1) * 512],
                        i == 0, i == mch - 1,
                        inc=(None if i < mch - 1 else "tP"))
                em.wait("scalar", "tP", last)
                em.wait("scalar", "tV", c.bank_evac.get("vtmp", 0))
                na = em.act_ps(lambda e, b=bank: e.copy(c.vtmp[:], c.PS[b][:]))
                c.bank_evac[bank] = na
                em.wait("vector", "tA", na)
                sidx = (tt * 2 + hf) % 2
                em.wait("vector", "dS",
                        c.bank_evac.get(("vstg", sidx), 0))
                nv = em.vec(lambda e, hf=hf, s=c.vstg[sidx]: e.tensor_add(
                    s[:], c.vtmp[:], c.t_vbrep[:, hf * 512:(hf + 1) * 512]))
                c.bank_evac["vtmp"] = nv
                em.wait("sync", "tV", nv)
                vv = c.vstg[sidx][:].rearrange("p (h d) -> p h d", d=128)
                ndl = em.dma("sync",
                             c.vscr[j * SEG + tt * 128:j * SEG + (tt + 1) * 128,
                                    hf * 4:(hf + 1) * 4, :], vv)
                c.bank_evac[("vstg", sidx)] = ndl

    for j in (0, 1):
        stream_block(j)
    for j in (0, 1):
        kv_block(j)
    for j in (2, 3):
        stream_block(j)
    for j in (2, 3):
        kv_block(j)
    build_wT(c.qw, c.kwT)
    c.qwT = c.kwT

    def own_tile(src_ap, P, hcol, t):
        par = t % 2 if t >= 0 else 0
        nv = stream_tile(src_ap, P, c.t_nwmrep, c.t_nbmrep, par)
        last_tr = 0
        if t >= 0:
            em.wait("tensor", "tV", nv)
            bank = 2 + par
            bank_ready(bank)
            last = None
            for ch in range(8):
                last = em.tr(c.psb(bank)[:, ch * P:ch * P + P],
                             c.t_t2[par][0:P, ch * 128:(ch + 1) * 128],
                             c.t_idb[0:P, 0:P], inc=(None if ch < 7 else "tP"))
            em.wait("scalar", "tP", last)
            last_tr = last
            xv = c.xnn[:].rearrange("p (ch w) -> p ch w", w=SEG)
            na = em.act_ps(lambda e, b=bank, t=t, xv=xv: e.copy(
                xv[:, 0:8, t * 128:(t + 1) * 128],
                c.psb(b)[:].rearrange("p (ch w) -> p ch w", w=128)))
            c.bank_evac[bank] = na
        st, sq, xn = c.t_st[par], c.t_sq[par], c.t_xn[par]
        nv2 = norm_apply_affine(P, c.t_l1grep, c.t_l1brep, c.t_th[par],
                                st, sq, xn)
        em.wait("tensor", "tV", nv2)
        bank2 = 2 + ((t + 1) % 2) if t >= 0 else 3
        bank_ready(bank2)
        last = None
        for ch in range(8):
            last = em.tr(c.psb(bank2)[:, ch * P:ch * P + P],
                         c.t_th[par][0:P, ch * 128:(ch + 1) * 128],
                         c.t_idb[0:P, 0:P], inc=(None if ch < 7 else "tP"))
        em.wait("scalar", "tP", last)
        hv = c.h_fm[:].rearrange("p (ch w) -> p ch w", w=HW)
        na = em.act_ps(lambda e, b=bank2, hcol=hcol, P=P, hv=hv: e.copy(
            hv[:, :, hcol:hcol + P],
            c.psb(b)[:].rearrange("p (ch w) -> p ch w", w=P)[:, 0:8, :]))
        c.bank_evac[bank2] = na
        if t >= 0:
            for g in range(2):
                bank3 = 4 + g
                bank_ready(bank3)
                last = None
                for i in range(4):
                    ch = g * 4 + i
                    last = em.tr(c.PS[bank3][:, i * 128:(i + 1) * 128],
                                 c.t_xn[par][:, ch * 128:(ch + 1) * 128],
                                 c.t_id32[:], inc=(None if i < 3 else "tP"))
                em.wait("scalar", "tP", last)
                sidx = g
                em.wait("scalar", "dS", c.bank_evac.get(("xstg", sidx), 0))
                na = em.act_ps(lambda e, b=bank3, s=c.xstg[sidx]: e.copy(
                    s[:], c.PS[b][:]))
                c.bank_evac[bank3] = na
                em.wait("sync", "tA", na)
                nd = 0
                for i in range(4):
                    ch = g * 4 + i
                    nd = em.dma("sync",
                                c.xnscr[ch, :, t * 128:(t + 1) * 128],
                                c.xstg[sidx][:, i * 128:(i + 1) * 128])
                c.bank_evac[("xstg", sidx)] = nd
        c.tile_users[par] = dict(tA=em.cnt["tA"], tV=em.cnt["tV"],
                                 tP=em.cnt["tP"])

    own_tile(c.xh[0:64, :], 64, 0, -1)
    for t in range(NTT):
        own_tile(c.xq[t * 128:(t + 1) * 128, :], 128, 64 + t * 128, t)
    n_xnnq_done = em.cnt["tA"]

    em.wait("tensor", "tA", n_xnnq_done)
    for dq in range(8):
        for qt in range(NQT):
            bank = 6 + ((dq * NQT + qt) % 2)
            bank_ready(bank)
            last = None
            for i in range(8):
                last = em.mm(
                    c.PS[bank][:],
                    c.qwT[:, i * C + dq * 128: i * C + (dq + 1) * 128],
                    c.xnn[:, i * SEG + qt * 512: i * SEG + (qt + 1) * 512],
                    i == 0, i == 7, inc=(None if i < 7 else "tP"))
            em.wait("scalar", "tP", last)
            sidx = (dq * NQT + qt) % 2
            em.wait("scalar", "dS", c.bank_evac.get(("qstg", sidx), 0))
            na = em.act_ps(lambda e, b=bank, dq=dq, s=c.qstg[sidx]: e.activation(
                s[:], c.PS[b][:], AF.Identity, bias=c.t_qb[:, dq:dq + 1], scale=1.0))
            c.bank_evac[bank] = na
            em.wait("sync", "tA", na)
            nd = em.dma("sync",
                        c.qscr[dq * 128:(dq + 1) * 128,
                               qt * 512:(qt + 1) * 512],
                        c.qstg[sidx][:])
            c.bank_evac[("qstg", sidx)] = nd

    if DBG:
        dbg1 = do("dbg_xnn", [128, NCH * SEG], BF16)
        dbg2 = do("dbg_h", [128, NCH * HW], BF16)
        phase_barrier()
        em.dma("sync", dbg1, c.xnn[:])
        em.dma("sync", dbg2, c.h_fm[:])

    phase_barrier()
    p1.close()
    p1x.close()

    if isinstance(stages, int):
        stages = ["p1", "p2", "p3", "p4", "p5"][:stages]
    if "p2" in stages:
        emit_phase2(c)
        phase_barrier()
    if "p3" in stages:
        c.h_ctx.close()
        emit_phase3(c)
        phase_barrier()
    if "p4" in stages:
        if "p3" not in stages:
            c.h_ctx.close()
        emit_phase4(c)
        phase_barrier()
    if "p5" in stages:
        emit_phase5(c)
    else:
        nd = em.dma("sync", c.y[0:128, :],
                    bass.AP(c.xq.tensor, 0, [[1024, 128], [1, 1024]]))
        em.wait("sync", "dS", nd)

    with nc.Block() as block:
        @block.sync
        def _(e):
            for f in em.ops["sync"]:
                f(e)

        @block.gpsimd
        def _(e):
            for f in em.ops["gpsimd"]:
                f(e)

        @block.tensor
        def _(e):
            for f in em.ops["tensor"]:
                f(e)

        @block.vector
        def _(e):
            for f in em.ops["vector"]:
                f(e)

        @block.scalar
        def _(e):
            for f in em.ops["scalar"]:
                f(e)

    ex.close()
    return nc


def emit_phase2(c):
    nc, em = c.nc, c.em
    HW = SEG + 64
    p2 = ExitStack()
    E2 = p2.enter_context
    wqkv_sb = E2(nc.sbuf_tensor("wqkv_sb", [128, NCH * 3 * C], BF16))
    wout_sb = E2(nc.sbuf_tensor("wout_sb", [128, NCH * C], BF16))
    qkvl = E2(nc.sbuf_tensor("qkvl", [128, 24 * 576], BF16))
    vtl = E2(nc.sbuf_tensor("vtl", [128, 5 * 1024], BF16))
    ptA = [E2(nc.sbuf_tensor(f"ptA{i}", [128, 1024], BF16)) for i in range(4)]
    ptB = [E2(nc.sbuf_tensor(f"ptB{i}", [64, 1024], BF16)) for i in range(4)]
    rbl = [E2(nc.sbuf_tensor(f"rbl{i}", [128, 1024], F32)) for i in range(2)]
    outl = E2(nc.sbuf_tensor("outl", [128, 8 * 512], BF16))
    l2 = [E2(nc.sbuf_tensor("l2_0", [1, 1024], F32))] * 2
    r2 = [E2(nc.sbuf_tensor("r2_0", [1, 1024], F32))] * 2
    r2r = [E2(nc.sbuf_tensor("r2r_0", [1, 1024], F32R))] * 2
    xat = [E2(nc.sbuf_tensor(f"xat{i}", [128, 512], F32)) for i in range(2)]
    xnt = [E2(nc.sbuf_tensor(f"xnt{i}", [128, 512], F32)) for i in range(2)]
    xas = [E2(nc.sbuf_tensor(f"xas{i}", [128, 512], F32)) for i in range(2)]
    pSA = E2(nc.psum_tensor("pSA", [128, 1024], F32))
    pSB = E2(nc.psum_tensor("pSB", [64, 1024], F32))
    pPV = E2(nc.psum_tensor("pPV", [128, 512], F32))
    pL2 = E2(nc.psum_tensor("pL2", [128, 512], F32))
    pX2 = [E2(nc.psum_tensor(f"pX2{i}", [128, 512], F32)) for i in range(2)]

    ev = c.bank_evac

    nw_ = None
    for r in range(8):
        nw_ = em.dma("gpsimd", wqkv_sb[:, r * 3 * C:(r + 1) * 3 * C],
                     c.wqkv[r * 128:(r + 1) * 128, :])
        nw_ = em.dma("gpsimd", wout_sb[:, r * C:(r + 1) * C],
                     c.wout[r * 128:(r + 1) * 128, :])
    em.wait("tensor", "dG", nw_)
    cb2_done = [False]

    def emit_scores(cc, qb, grp):
        kc0 = qb * 128
        qc0 = 64 + qb * 128
        em.wait("tensor", "tA", ev.get("pSA", 0))
        em.wait("tensor", "tA", ev.get("pSB", 0))
        lastB = None
        for hl in range(8):
            hh = grp * 8 + hl
            kch = 8 + hh // 2
            qch = hh // 2
            pb = (hh % 2) * 64
            lhsA = qkvl[pb:pb + 64, kch * 576 + kc0:kch * 576 + kc0 + 128]
            lhsB = qkvl[pb:pb + 64, kch * 576 + kc0 + 128:
                        kch * 576 + kc0 + 192]
            rhsq = qkvl[pb:pb + 64, qch * 576 + qc0:qch * 576 + qc0 + 128]
            em.mm(pSA[:, hl * 128:(hl + 1) * 128], lhsA, rhsq, True, True,
                  inc=("tP" if hl == 7 else None))
            lastB = em.mm(pSB[:, hl * 128:(hl + 1) * 128], lhsB, rhsq,
                          True, True, inc=("tP" if hl == 7 else None))
        return lastB

    def emit_exp(par, nscores):
        em.wait("scalar", "tP", nscores)
        if not cb2_done[0]:
            na = em.act_ps(lambda e: e.copy(rbl[0][:, 0:512], pSA[:, 0:512]))
            em.wait("vector", "tA", na)
            nv = em.vec(lambda e: e.reduce_max(c.t_c1[:], rbl[0][:, 0:512],
                                               AX.X))
            em.wait("tensor", "tV", nv)
            ntr = em.tr(pX2[0][0:1, 0:128], c.t_c1[:], c.t_id32[:], inc="tP")
            em.wait("scalar", "tP", ntr)
            na = em.act_ps(lambda e: e.copy(c.t_c2[:], pX2[0][0:1, 0:128]))
            em.wait("vector", "tA", na)
            em.vec(lambda e: e.reduce_max(c.t_c3[:], c.t_c2[:], AX.X))
            em.vec(lambda e: e.tensor_scalar_mul(c.t_c3[:], c.t_c3[:], -SCALE_L))
            nv = em.vec(lambda e: e.tensor_copy(c.t_c3b[:], c.t_c3[:]))
            em.wait("tensor", "tV", nv)
            nmm2 = em.mm(pX2[0][:, 0:1], c.t_onesrb[:], c.t_c3b[:],
                         True, True, inc="tP")
            em.wait("scalar", "tP", nmm2)
            na = em.act_ps(lambda e: e.copy(c.t_cb2[:], pX2[0][:, 0:1]))
            ev["pX20"] = na
            cb2_done[0] = True
        em.wait("scalar", "tV", ev.get(f"pt_use{par}", 0))
        em.wait("scalar", "tP", ev.get(f"pt_pv{par}", 0))
        naA = em.act_ps(lambda e, par=par: e.activation(
            ptA[par][:], pSA[:], AF.Exp, bias=c.t_cb2[:], scale=SCALE_L))
        ev["pSA"] = naA
        naB = em.act_ps(lambda e, par=par: e.activation(
            ptB[par][:], pSB[:], AF.Exp, bias=c.t_cb2[0:64, :], scale=SCALE_L))
        ev["pSB"] = naB
        return naB

    P2DEPTH = 3

    def emit_rest(cc, qb, grp, par, nexp):
        cpar = par % 2
        rblc, l2c, r2c, r2rc = rbl[cpar], l2[cpar], r2[cpar], r2r[cpar]
        em.wait("vector", "tA", nexp)
        vA = ptA[par][:].rearrange("p (o w) -> p o w", w=128)
        mAb = c.t_mA[:].rearrange("p (o w) -> p o w", w=128).broadcast_to(
            [128, 8, 128])
        em.vec(lambda e: e.tensor_mul(vA, vA, mAb))
        vB = ptB[par][:].rearrange("p (o w) -> p o w", w=128)
        mBb = c.t_mB[:].rearrange("p (o w) -> p o w", w=128).broadcast_to(
            [64, 8, 128])
        nv = em.vec(lambda e: e.tensor_mul(vB, vB, mBb))
        em.wait("tensor", "tV", nv)
        em.wait("tensor", "tA", ev.get("pL2", 0))
        em.wait("tensor", "tA", ev.get("pX21", 0))
        em.mm(pL2[0:1, :], c.t_onesb[:], ptA[par][:, 0:512], True, False)
        em.mm(pL2[0:1, :], c.t_onesb[0:64, :], ptB[par][:, 0:512], False, True)
        em.mm(pX2[1][0:1, :], c.t_onesb[:], ptA[par][:, 512:1024], True, False)
        nl = em.mm(pX2[1][0:1, :], c.t_onesb[0:64, :], ptB[par][:, 512:1024],
                   False, True, inc="tP")
        em.wait("scalar", "tP", nl)
        em.wait("scalar", "tV", ev.get("l2_use", 0))
        na = em.act_ps(lambda e, l2c=l2c: e.copy(l2c[0:1, 0:512], pL2[0:1, :]))
        ev["pL2"] = na
        na = em.act_ps(lambda e, l2c=l2c: e.copy(l2c[0:1, 512:1024], pX2[1][0:1, :]))
        ev["pX21"] = na
        em.wait("vector", "tA", na)
        em.wait("vector", "tP", ev.get("r2r_use", 0))
        em.vec(lambda e, r2c=r2c, l2c=l2c: e.reciprocal(r2c[:], l2c[:]))
        ev["l2_use"] = em.cnt["tV"]
        nv = em.vec(lambda e, r2rc=r2rc, r2c=r2c: e.tensor_copy(r2rc[:], r2c[:]))
        em.wait("tensor", "tV", nv)
        for half in range(2):
            em.wait("tensor", "tA", ev.get("pX20", 0))
            nmm = em.mm(pX2[0][:], c.t_ones1[:],
                        r2rc[0:1, half * 512:(half + 1) * 512], True, True,
                        inc="tP")
            ev["r2r_use"] = nmm
            em.wait("scalar", "tP", nmm)
            na = em.act_ps(lambda e, half=half, rblc=rblc: e.copy(
                rblc[:, half * 512:(half + 1) * 512], pX2[0][:]))
            ev["pX20"] = na
        em.wait("vector", "tA", ev["pX20"])
        rA = rblc[:].rearrange("p (o w) -> p o w", w=128)
        em.vec(lambda e: e.tensor_mul(vA, vA, rA))
        rB = rblc[0:64, :].rearrange("p (o w) -> p o w", w=128)
        nv = em.vec(lambda e: e.tensor_mul(vB, vB, rB))
        ev[f"pt_use{par}"] = nv
        em.wait("tensor", "tV", nv)
        em.wait("tensor", "tA", ev.get("pPV", 0))
        last = None
        for hl in range(8):
            hh = grp * 8 + hl
            p = hl // 2
            pb = (hh % 2) * 64
            dvc = (hh % 2) * 64
            rc = grp * 4 + p
            lhsA = vtl[:, qb * 1024 + rc * 128 + dvc:
                       qb * 1024 + rc * 128 + dvc + 64]
            lhsB = vtl[0:64, (qb + 1) * 1024 + rc * 128 + dvc:
                       (qb + 1) * 1024 + rc * 128 + dvc + 64]
            outap = pPV[pb:pb + 64, p * 128:(p + 1) * 128]
            tp = (0, pb)
            em.op("tensor", lambda e, o=outap, l=lhsA,
                  r=ptA[par][:, hl * 128:(hl + 1) * 128], tp=tp:
                  e.matmul(o, l, r, start=True, stop=False, tile_position=tp))
            last = em.op("tensor", lambda e, o=outap, l=lhsB,
                         r=ptB[par][:, hl * 128:(hl + 1) * 128], tp=tp:
                         e.matmul(o, l, r, start=False, stop=True,
                                  tile_position=tp),
                         inc=("tP" if hl == 7 else None))
        ev[f"pt_pv{par}"] = last
        em.wait("scalar", "tP", last)
        ov = outl[:].rearrange("p (o w) -> p o w", w=512)
        na = em.act_ps(lambda e, grp=grp, qb=qb, ov=ov: e.copy(
            ov[:, grp * 4:(grp + 1) * 4, qb * 128:(qb + 1) * 128],
            pPV[:].rearrange("p (o w) -> p o w", w=128)))
        ev["pPV"] = na

    for cc in range(4):
        hw0 = cc * 512
        qkv_banksA = [("pX20", pX2[0][:]), ("pSA", pSA[:, 0:512]),
                      ("pPV", pPV[:])]
        qkv_banksB = [("pX21", pX2[1][:, 0:64]), ("pSA", pSA[:, 512:576]),
                      ("pL2", pL2[:, 0:64])]
        for oc in range(24):
            rot = oc % 3
            kA, bA = qkv_banksA[rot]
            kB, bB = qkv_banksB[rot]
            em.wait("tensor", "tA", ev.get(kA, 0))
            em.wait("tensor", "tA", ev.get(kB, 0))
            lastB = None
            for i in range(NCH):
                em.mm(bA,
                      wqkv_sb[:, i * 3 * C + oc * 128:i * 3 * C + (oc + 1) * 128],
                      c.h_fm[:, i * HW + hw0:i * HW + hw0 + 512],
                      i == 0, i == NCH - 1)
                lastB = em.mm(
                    bB,
                    wqkv_sb[:, i * 3 * C + oc * 128:i * 3 * C + (oc + 1) * 128],
                    c.h_fm[:, i * HW + hw0 + 512:i * HW + hw0 + 576],
                    i == 0, i == NCH - 1,
                    inc=("tP" if i == NCH - 1 else None))
            em.wait("scalar", "tP", lastB)
            na = em.act_ps(lambda e, oc=oc, bA=bA: e.activation(
                qkvl[:, oc * 576:oc * 576 + 512], bA,
                AF.Identity, bias=c.t_bqkv[:, oc:oc + 1], scale=1.0))
            ev[kA] = na
            na = em.act_ps(lambda e, oc=oc, bB=bB: e.activation(
                qkvl[:, oc * 576 + 512:oc * 576 + 576], bB,
                AF.Identity, bias=c.t_bqkv[:, oc:oc + 1], scale=1.0))
            ev[kB] = na
        n_qkv = em.cnt["tA"]
        if cc == 0:
            em.wait("vector", "tA", n_qkv)
            nv = None
            for oc in range(8, 24):
                nv = em.vec(lambda e, oc=oc: e.tensor_scalar_mul(
                    qkvl[:, oc * 576:oc * 576 + 64],
                    qkvl[:, oc * 576:oc * 576 + 64], c.t_hm[:, 0:1]))
            em.wait("tensor", "tV", nv)
        em.wait("tensor", "tA", n_qkv)
        for vc in range(8):
            par = vc % 2
            em.wait("tensor", "tA", ev.get(f"pX2{par}", 0))
            last = None
            for tt in range(4):
                last = em.tr(pX2[par][:].bitcast(BF16)[:, tt * 128:
                                                       (tt + 1) * 128],
                             qkvl[:, (16 + vc) * 576 + tt * 128:
                                  (16 + vc) * 576 + (tt + 1) * 128],
                             c.t_idb[:], inc=(None if tt < 3 else "tP"))
            em.wait("scalar", "tP", last)
            vv = vtl[:].rearrange("p (tt w) -> p tt w", w=1024)
            na = em.act_ps(lambda e, par=par, vc=vc, vv=vv: e.copy(
                vv[:, 0:4, vc * 128:(vc + 1) * 128],
                pX2[par][:].bitcast(BF16)[:, 0:512].rearrange(
                    "p (a w) -> p a w", w=128)))
            ev[f"pX2{par}"] = na
            par2 = (vc + 1) % 2
            em.wait("tensor", "tA", ev.get(f"pX2{par2}", 0))
            ntr = em.tr(pX2[par2][:].bitcast(BF16)[0:64, 0:128],
                        qkvl[:, (16 + vc) * 576 + 512:(16 + vc) * 576 + 576],
                        c.t_idb[:], inc="tP")
            em.wait("scalar", "tP", ntr)
            na = em.act_ps(lambda e, par2=par2, vc=vc: e.copy(
                vtl[0:64, 4 * 1024 + vc * 128:4 * 1024 + (vc + 1) * 128],
                pX2[par2][:].bitcast(BF16)[0:64, 0:128]))
            ev[f"pX2{par2}"] = na
        n_vtl = em.cnt["tA"]
        em.wait("tensor", "tA", n_vtl)
        its = [(qb, grp) for qb in range(4) for grp in range(2)]
        exps = {}
        ahead = 3
        for k in range(min(ahead, len(its))):
            ns = emit_scores(cc, *its[k])
            exps[k] = emit_exp(k % 4, ns)
        for i, it in enumerate(its):
            ne = exps[i]
            if i + ahead < len(its):
                ns = emit_scores(cc, *its[i + ahead])
                exps[i + ahead] = emit_exp((i + ahead) % 4, ns)
            emit_rest(cc, it[0], it[1], i % 4, ne)
        for oc in range(8):
            par = oc % 2
            em.wait("tensor", "tA", ev.get(f"pX2{par}", 0))
            last = None
            for i in range(NCH):
                last = em.mm(pX2[par][:],
                             wout_sb[:, i * C + oc * 128:i * C + (oc + 1) * 128],
                             outl[:, i * 512:(i + 1) * 512],
                             i == 0, i == NCH - 1,
                             inc=("tP" if i == NCH - 1 else None))
            em.wait("scalar", "tP", last)
            em.wait("scalar", "tV", ev.get(f"xat{par}", 0))
            na = em.act_ps(lambda e, par=par: e.copy(xat[par][:], pX2[par][:]))
            ev[f"pX2{par}"] = na
            nd = em.dma("sync", xnt[par][:],
                        c.xnscr[oc, :, cc * 512:(cc + 1) * 512])
            em.wait("vector", "tA", na)
            em.wait("vector", "dS", nd)
            em.wait("vector", "dS", ev.get(f"xas{par}", 0))
            nv = em.vec(lambda e, par=par, oc=oc: e.scalar_tensor_tensor(
                xas[par][:], xat[par][:], c.t_bout[:, oc:oc + 1],
                xnt[par][:], ALU.add, ALU.add))
            ev[f"xat{par}"] = nv
            em.wait("sync", "tV", nv)
            nd = em.dma("sync", c.xascr[oc, :, cc * 512:(cc + 1) * 512],
                        xas[par][:])
            ev[f"xas{par}"] = nd

    if DBG:
        c.phase_barrier()
        dbg = nc.dram_tensor("dbg_h2", [128, NCH * HW], BF16,
                             kind="ExternalOutput").ap()
        nd = em.dma("sync", dbg, c.h_fm[:])
        em.wait("sync", "dS", nd)

    c.phase_barrier()
    p2.close()


def emit_phase3(c):
    nc, em = c.nc, c.em
    p3 = ExitStack()
    E3 = p3.enter_context
    w2sb = E3(nc.sbuf_tensor("w2sb", [128, 32 * C], BF16))
    w1g = [E3(nc.sbuf_tensor(f"w1g{i}", [128, NCH * 512], BF16))
           for i in range(3)]
    xat3 = E3(nc.sbuf_tensor("xat3", [128, NCH * 512], F32))
    xab = E3(nc.sbuf_tensor("xab", [128, 4096], BF16))
    xsqb = E3(nc.sbuf_tensor("xsqb", [128, 4096], BF16))
    h2b = E3(nc.sbuf_tensor("h2b", [128, 4096], BF16))
    gsb = E3(nc.sbuf_tensor("gsb", [128, 32 * 512], BF16))
    st2 = E3(nc.sbuf_tensor("st2", [2, 512], F32))
    st3 = E3(nc.sbuf_tensor("st3", [1, 512 * 6], F32))
    strr = E3(nc.sbuf_tensor("strr", [1, 1024], F32R))
    murep = E3(nc.sbuf_tensor("murep", [128, 512], F32))
    rsrep = E3(nc.sbuf_tensor("rsrep", [128, 512], F32))
    t3a = E3(nc.sbuf_tensor("t3a", [128, 512], F32))
    t3b = E3(nc.sbuf_tensor("t3b", [128, 512], F32))
    m2t = [E3(nc.sbuf_tensor(f"m2t{i}", [128, 512], F32)) for i in range(2)]
    xnt3 = [E3(nc.sbuf_tensor(f"xnt3{i}", [128, 512], F32)) for i in range(2)]
    acs = [E3(nc.sbuf_tensor(f"acs{i}", [128, 512], F32)) for i in range(2)]
    p3F = [E3(nc.psum_tensor(f"p3F{i}", [128, 512], F32)) for i in range(4)]
    p3L = E3(nc.psum_tensor("p3L", [128, 512], F32))
    p3X = E3(nc.psum_tensor("p3X", [128, 512], F32))

    ev = c.bank_evac
    nw_ = None
    for r in range(32):
        nw_ = em.dma("gpsimd", w2sb[:, r * C:(r + 1) * C],
                     c.w2[r * 128:(r + 1) * 128, :])
    n_w2 = nw_

    def load_w1g(grp, par):
        em.wait("gpsimd", "tP", ev.get(f"w1g{par}", 0))
        n = None
        for r in range(8):
            n = em.dma("gpsimd", w1g[par][:, r * 512:(r + 1) * 512],
                       c.w1[r * 128:(r + 1) * 128, grp * 512:(grp + 1) * 512])
        return n

    w1_lds = [load_w1g(0, 0), load_w1g(1, 1)]
    w1_next = [2]

    for qt in range(4):
        em.wait("sync", "tV", ev.get("xat3_use", 0))
        nd = None
        for ch in range(NCH):
            nd = em.dma("sync", xat3[:, ch * 512:(ch + 1) * 512],
                        c.xascr[ch, :, qt * 512:(qt + 1) * 512])
        em.wait("vector", "dS", nd)
        nv = em.vec(lambda e: e.tensor_copy(xab[:], xat3[:]))
        em.wait("scalar", "tV", nv)
        na = em.act(lambda e: e.activation(xsqb[:], xab[:], AF.Square,
                                           bias=0.0, scale=1.0))
        em.wait("tensor", "tA", na)
        em.wait("tensor", "tV", nv)
        em.wait("tensor", "tA", ev.get("p3L", 0))
        last = None
        for ch in range(NCH):
            em.mm(p3L[0:2, :], c.t_sel[:, 0:2], xab[:, ch * 512:(ch + 1) * 512],
                  ch == 0, False)
            last = em.mm(p3L[0:2, :], c.t_sel[:, 4:6],
                         xsqb[:, ch * 512:(ch + 1) * 512],
                         False, ch == NCH - 1,
                         inc=("tP" if ch == NCH - 1 else None))
        em.wait("scalar", "tP", last)
        na = em.act_ps(lambda e: e.copy(st2[:], p3L[0:2, :]))
        ev["p3L"] = na
        em.wait("sync", "tA", na)
        nd = em.dma("sync", st3[0:1, 512:1024], st2[1:2, :])
        nd = em.dma("sync", st3[0:1, 0:512], st2[0:1, :])
        em.wait("vector", "dS", nd)
        em.vec(lambda e: e.tensor_scalar_mul(st3[0:1, 0:1024],
                                             st3[0:1, 0:1024], 1.0 / C))
        em.vec(lambda e: e.tensor_mul(st3[0:1, 1024:1536], st3[0:1, 0:512],
                                      st3[0:1, 0:512]))
        nv = em.vec(lambda e: e.tensor_sub(st3[0:1, 1536:2048],
                                           st3[0:1, 512:1024],
                                           st3[0:1, 1024:1536]))
        em.wait("scalar", "tV", nv)
        na = em.act(lambda e: e.activation(st3[0:1, 2048:2560],
                                           st3[0:1, 1536:2048], AF.Sqrt,
                                           bias=c.t_eps[0:1, :], scale=1.0))
        em.wait("vector", "tA", na)
        em.vec(lambda e: e.reciprocal(st3[0:1, 2560:3072],
                                      st3[0:1, 2048:2560]))
        em.vec(lambda e: e.tensor_copy(strr[0:1, 0:512], st3[0:1, 0:512]))
        nv = em.vec(lambda e: e.tensor_copy(strr[0:1, 512:1024],
                                            st3[0:1, 2560:3072]))
        em.wait("tensor", "tV", nv)
        for k, dst in ((0, murep), (1, rsrep)):
            em.wait("tensor", "tA", ev.get("p3X", 0))
            nmm = em.mm(p3X[:], c.t_ones1[:], strr[0:1, k * 512:(k + 1) * 512],
                        True, True, inc="tP")
            em.wait("scalar", "tP", nmm)
            na = em.act_ps(lambda e, dst=dst: e.copy(dst[:], p3X[:]))
            ev["p3X"] = na
        em.wait("vector", "tA", ev["p3X"])
        nvh = None
        for ch in range(NCH):
            em.vec(lambda e, ch=ch: e.tensor_sub(
                t3a[:], xat3[:, ch * 512:(ch + 1) * 512], murep[:]))
            em.vec(lambda e: e.tensor_mul(t3b[:], t3a[:], rsrep[:]))
            nvh = em.vec(lambda e, ch=ch: e.scalar_tensor_tensor(
                h2b[:, ch * 512:(ch + 1) * 512], t3b[:],
                c.t_ln2g[:, ch:ch + 1],
                c.t_ln2b[:, ch:ch + 1].broadcast_to([128, 512]),
                ALU.mult, ALU.add))
        em.wait("tensor", "tV", nvh)
        for grp in range(8):
            gg = qt * 8 + grp
            par = gg % 3
            em.wait("tensor", "dG", w1_lds[gg] if gg < len(w1_lds) else 0)
            if w1_next[0] < 32:
                w1_lds.append(load_w1g(w1_next[0] % 8, w1_next[0] % 3))
                w1_next[0] += 1
            for dh in range(4):
                em.wait("tensor", "tA", ev.get(f"p3F{dh}", 0))
                last = None
                for i in range(NCH):
                    last = em.mm(p3F[dh][:],
                                 w1g[gg % 3][:, i * 512 + dh * 128:
                                             i * 512 + (dh + 1) * 128],
                                 h2b[:, i * 512:(i + 1) * 512],
                                 i == 0, i == NCH - 1,
                                 inc=("tP" if i == NCH - 1 else None))
                em.wait("scalar", "tP", last)
                na = em.act_ps(lambda e, dh=dh, grp=grp: e.activation(
                    gsb[:, (grp * 4 + dh) * 512:(grp * 4 + dh + 1) * 512],
                    p3F[dh][:], AF.Gelu,
                    bias=c.t_b1[:, grp * 4 + dh:grp * 4 + dh + 1], scale=1.0))
                ev[f"p3F{dh}"] = na
            ev[f"w1g{gg % 3}"] = em.cnt["tP"]
        n_gelu = em.cnt["tA"]
        em.wait("tensor", "tA", n_gelu)
        em.wait("tensor", "dG", n_w2)
        for oc in range(NCH):
            par = oc % 2
            em.wait("tensor", "tA", ev.get(f"p3F{par}", 0))
            last = None
            for i in range(32):
                last = em.mm(p3F[par][:],
                             w2sb[:, i * C + oc * 128:i * C + (oc + 1) * 128],
                             gsb[:, i * 512:(i + 1) * 512],
                             i == 0, i == 31,
                             inc=("tP" if i == 31 else None))
            em.wait("scalar", "tP", last)
            em.wait("scalar", "tV", ev.get(f"m2t{par}", 0))
            na = em.act_ps(lambda e, par=par: e.copy(m2t[par][:], p3F[par][:]))
            ev[f"p3F{par}"] = na
            nd = em.dma("sync", xnt3[par][:],
                        c.xnscr[oc, :, qt * 512:(qt + 1) * 512])
            em.wait("vector", "tA", na)
            em.wait("vector", "dS", nd)
            em.wait("vector", "dS", ev.get(f"acs{par}", 0))
            em.vec(lambda e, par=par, oc=oc: e.scalar_tensor_tensor(
                acs[par][:], m2t[par][:], c.t_b2[:, oc:oc + 1],
                xat3[:, oc * 512:(oc + 1) * 512], ALU.add, ALU.add))
            nv = em.vec(lambda e, par=par: e.tensor_add(
                acs[par][:], acs[par][:], xnt3[par][:]))
            ev[f"m2t{par}"] = nv
            em.wait("sync", "tV", nv)
            nd = em.dma("sync", c.accscr[oc, :, qt * 512:(qt + 1) * 512],
                        acs[par][:])
            ev[f"acs{par}"] = nd
        ev["xat3_use"] = em.cnt["tV"]

    c.phase_barrier()
    p3.close()


def emit_phase4(c):
    nc, em = c.nc, c.em
    p4 = ExitStack()
    E4 = p4.enter_context
    qh = [E4(nc.sbuf_tensor(f"qh{i}", [128, SEG], BF16)) for i in range(2)]
    kh = [E4(nc.sbuf_tensor(f"kh{i}", [128, T], BF16)) for i in range(2)]
    vh = [E4(nc.sbuf_tensor(f"vh{i}", [128, T], BF16)) for i in range(2)]
    pt = [E4(nc.sbuf_tensor(f"pt{i}", [128, 8 * 1024], BF16)) for i in range(3)]
    outh = E4(nc.sbuf_tensor("outh", [128, SEG], F32))
    p4tmp = E4(nc.sbuf_tensor("p4tmp", [128, 512], F32))
    lrow = E4(nc.sbuf_tensor("lrow", [4, 512], F32))
    rrow = E4(nc.sbuf_tensor("rrow", [4, 512], F32))
    rrowr = E4(nc.sbuf_tensor("rrowr", [4, 512], F32R))
    rflat = E4(nc.sbuf_tensor("rflat", [1, 512], F32R))
    rbc = E4(nc.sbuf_tensor("rbc", [128, SEG], F32))
    c.attn_out = c.ex.enter_context(
        nc.sbuf_tensor("attn_out", [128, NCH * SEG], BF16, side="right"))
    pA = E4(nc.psum_tensor("pA", [128, 1024], F32))
    pB = E4(nc.psum_tensor("pB", [128, 1024], F32))
    pO = [E4(nc.psum_tensor(f"pO{i}", [128, 512], F32)) for i in range(2)]
    pL = E4(nc.psum_tensor("pL", [128, 512], F32))
    pX = E4(nc.psum_tensor("pX", [128, 512], F32))

    ev = c.bank_evac
    kv_use = [0, 0]
    ld_cnt = [0, 0]

    def load_head(h):
        par = h % 2
        em.wait("sync", "tP", kv_use[par])
        em.dma("sync", qh[par][:], c.qscr[h * 128:(h + 1) * 128, :])
        em.dma("sync", kh[par][:], c.kscr[h * 128:(h + 1) * 128, :])
        vsrc = c.vscr[:, h, :].rearrange("(kt p) d -> p kt d", p=128)
        vdst = vh[par][:].rearrange("p (kt d) -> p kt d", d=128)
        n = None
        for g in range(8):
            n = em.dma("sync", vdst[:, g * 8:(g + 1) * 8, :],
                       vsrc[:, g * 8:(g + 1) * 8, :])
        ld_cnt[par] = n

    load_head(0)
    first_exp_emitted = False

    for h in range(8):
        par = h % 2
        if h < 7:
            load_head(h + 1)
        em.wait("tensor", "dS", ld_cnt[par])
        pvl_done = {}
        exp_done = {}

        def sc_exp(qhalf, sbi, gi):
            nonlocal first_exp_emitted
            last_exp = 0
            for kk in range(8):
                gk = sbi * 8 + kk
                pS, psname = (pA, "pA") if gk % 2 == 0 else (pB, "pB")
                em.wait("tensor", "tA", ev.get(psname, 0))
                em.mm(pS[:, 0:512], kh[par][:, gk * 128:(gk + 1) * 128],
                      qh[par][:, qhalf * 1024:qhalf * 1024 + 512],
                      True, True)
                nmm = em.mm(pS[:, 512:1024],
                            kh[par][:, gk * 128:(gk + 1) * 128],
                            qh[par][:, qhalf * 1024 + 512:(qhalf + 1) * 1024],
                            True, True, inc="tP")
                if not first_exp_emitted:
                    em.wait("scalar", "tP", nmm)
                    na = em.act_ps(lambda e: e.copy(p4tmp[:], pA[:, 0:512]))
                    em.wait("vector", "tA", na)
                    nv = em.vec(lambda e: e.reduce_max(
                        c.t_c1[:], p4tmp[:], AX.X))
                    em.wait("tensor", "tV", nv)
                    ntr = em.tr(pX[0:1, 0:128], c.t_c1[:], c.t_id32[:],
                                inc="tP")
                    em.wait("scalar", "tP", ntr)
                    na = em.act_ps(lambda e: e.copy(c.t_c2[:], pX[0:1, 0:128]))
                    em.wait("vector", "tA", na)
                    em.vec(lambda e: e.reduce_max(c.t_c3[:], c.t_c2[:], AX.X))
                    em.vec(lambda e: e.tensor_scalar_mul(
                        c.t_c3[:], c.t_c3[:], -SCALE_F))
                    nv = em.vec(lambda e: e.tensor_copy(
                        c.t_c3b[:], c.t_c3[:]))
                    em.wait("tensor", "tV", nv)
                    em.wait("tensor", "tA", ev.get("pX", 0))
                    nmm2 = em.mm(pX[:, 0:1], c.t_onesrb[:],
                                 c.t_c3b[:], True, True, inc="tP")
                    em.wait("scalar", "tP", nmm2)
                    na = em.act_ps(lambda e: e.copy(c.t_cb4[:], pX[:, 0:1]))
                    ev["pX"] = na
                    first_exp_emitted = True
                em.wait("scalar", "tP", nmm)
                em.wait("scalar", "tP", pvl_done.get(gi - 3, 0))
                na = em.act_ps(lambda e, pS=pS, gi=gi, kk=kk: e.activation(
                    pt[gi % 3][:, kk * 1024:(kk + 1) * 1024], pS[:],
                    AF.Exp, bias=c.t_cb4[:], scale=SCALE_F))
                ev[psname] = na
                last_exp = na
            exp_done[gi] = last_exp

        def pv_l(qhalf, sbi, gi):
            em.wait("tensor", "tA", exp_done[gi])
            if sbi == 0:
                em.wait("tensor", "tA", ev.get("pO0", 0))
                em.wait("tensor", "tA", ev.get("pO1", 0))
                if qhalf == 0:
                    em.wait("tensor", "tA", ev.get("pL", 0))
            last = None
            for qtl in range(2):
                qt = qhalf * 2 + qtl
                for kk in range(8):
                    gk = sbi * 8 + kk
                    em.mm(pO[qtl][:],
                          vh[par][:, gk * 128:(gk + 1) * 128],
                          pt[gi % 3][:, kk * 1024 + qtl * 512:
                                     kk * 1024 + (qtl + 1) * 512],
                          start=(sbi == 0 and kk == 0),
                          stop=(sbi == 7 and kk == 7))
                    last = em.mm(pL[0:4, :],
                                 c.t_sel[:, qt * 4:(qt + 1) * 4],
                                 pt[gi % 3][:, kk * 1024 + qtl * 512:
                                            kk * 1024 + (qtl + 1) * 512],
                                 start=(qhalf == 0 and sbi == 0 and kk == 0
                                        and qtl == 0),
                                 stop=(qhalf == 1 and sbi == 7 and kk == 7
                                       and qtl == 1),
                                 inc=("tP" if (qtl == 1 and kk == 7)
                                      else None))
            pvl_done[gi] = last

        gis = [(qh_, sb_) for qh_ in range(2) for sb_ in range(8)]

        def fused_step(cur, nxt):
            if cur is not None:
                cqh, csb = gis[cur]
                em.wait("tensor", "tA", exp_done[cur])
                if csb == 0:
                    em.wait("tensor", "tA", ev.get("pO0", 0))
                    em.wait("tensor", "tA", ev.get("pO1", 0))
                    if cqh == 0:
                        em.wait("tensor", "tA", ev.get("pL", 0))
            last_pv = None
            for kk in range(8):
                if nxt is not None:
                    nqh, nsb = gis[nxt]
                    gk = nsb * 8 + kk
                    pS, psname = (pA, "pA") if gk % 2 == 0 else (pB, "pB")
                    em.wait("tensor", "tA", ev.get(psname, 0))
                    em.mm(pS[:, 0:512], kh[par][:, gk * 128:(gk + 1) * 128],
                          qh[par][:, nqh * 1024:nqh * 1024 + 512],
                          True, True)
                    nmm = em.mm(pS[:, 512:1024],
                                kh[par][:, gk * 128:(gk + 1) * 128],
                                qh[par][:, nqh * 1024 + 512:(nqh + 1) * 1024],
                                True, True, inc="tP")
                if cur is not None:
                    cqh, csb = gis[cur]
                    gk2 = csb * 8 + kk
                    for qtl in range(2):
                        qt = cqh * 2 + qtl
                        em.mm(pO[qtl][:],
                              vh[par][:, gk2 * 128:(gk2 + 1) * 128],
                              pt[cur % 3][:, kk * 1024 + qtl * 512:
                                          kk * 1024 + (qtl + 1) * 512],
                              start=(csb == 0 and kk == 0),
                              stop=(csb == 7 and kk == 7))
                        last_pv = em.mm(
                            pL[0:4, :], c.t_sel[:, qt * 4:(qt + 1) * 4],
                            pt[cur % 3][:, kk * 1024 + qtl * 512:
                                        kk * 1024 + (qtl + 1) * 512],
                            start=(cqh == 0 and csb == 0 and kk == 0
                                   and qtl == 0),
                            stop=(cqh == 1 and csb == 7 and kk == 7
                                  and qtl == 1),
                            inc=("tP" if (qtl == 1 and kk == 7) else None))
                if nxt is not None:
                    em.wait("scalar", "tP", nmm)
                    em.wait("scalar", "tP", pvl_done.get(nxt - 3, 0))
                    na = em.act_ps(lambda e, pS=pS, nxt=nxt, kk=kk:
                                   e.activation(
                        pt[nxt % 3][:, kk * 1024:(kk + 1) * 1024], pS[:],
                        AF.Exp, bias=c.t_cb4[:], scale=SCALE_F))
                    ev[psname] = na
                    if kk == 7:
                        exp_done[nxt] = na
            if cur is not None:
                pvl_done[cur] = last_pv
                cqh, csb = gis[cur]
                if csb == 7:
                    em.wait("scalar", "tP", last_pv)
                    for qtl in range(2):
                        qt = cqh * 2 + qtl
                        na = em.act_ps(lambda e, qtl=qtl, qt=qt: e.copy(
                            outh[:, qt * 512:(qt + 1) * 512], pO[qtl][:]))
                        ev[f"pO{qtl}"] = na

        if not first_exp_emitted:
            nqh, nsb = gis[0]
            em.wait("tensor", "tA", ev.get("pA", 0))
            em.mm(pA[:, 0:512], kh[par][:, 0:128], qh[par][:, 0:512],
                  True, True)
            nmm = em.mm(pA[:, 512:1024], kh[par][:, 0:128],
                        qh[par][:, 512:1024], True, True, inc="tP")
            em.wait("scalar", "tP", nmm)
            na = em.act_ps(lambda e: e.copy(p4tmp[:], pA[:, 0:512]))
            em.wait("vector", "tA", na)
            nv = em.vec(lambda e: e.reduce_max(c.t_c1[:], p4tmp[:], AX.X))
            em.wait("tensor", "tV", nv)
            ntr = em.tr(pX[0:1, 0:128], c.t_c1[:], c.t_id32[:], inc="tP")
            em.wait("scalar", "tP", ntr)
            na = em.act_ps(lambda e: e.copy(c.t_c2[:], pX[0:1, 0:128]))
            em.wait("vector", "tA", na)
            em.vec(lambda e: e.reduce_max(c.t_c3[:], c.t_c2[:], AX.X))
            em.vec(lambda e: e.tensor_scalar_mul(c.t_c3[:], c.t_c3[:],
                                                 -SCALE_F))
            nv = em.vec(lambda e: e.tensor_copy(c.t_c3b[:], c.t_c3[:]))
            em.wait("tensor", "tV", nv)
            em.wait("tensor", "tA", ev.get("pX", 0))
            nmm2 = em.mm(pX[:, 0:1], c.t_onesrb[:], c.t_c3b[:],
                         True, True, inc="tP")
            em.wait("scalar", "tP", nmm2)
            na = em.act_ps(lambda e: e.copy(c.t_cb4[:], pX[:, 0:1]))
            ev["pX"] = na
            ev["pA"] = em.act_ps(lambda e: e.activation(
                pt[0][:, 0:1024], pA[:], AF.Exp, bias=c.t_cb4[:],
                scale=SCALE_F))
            first_exp_emitted = True
            skip_first_tile = True
        else:
            skip_first_tile = False

        for step in range(len(gis) + 1):
            nxt = step if step < len(gis) else None
            cur = step - 1 if step >= 1 else None
            if skip_first_tile and step == 0:
                for kk in range(1, 8):
                    gk = kk
                    pS, psname = (pA, "pA") if gk % 2 == 0 else (pB, "pB")
                    em.wait("tensor", "tA", ev.get(psname, 0))
                    em.mm(pS[:, 0:512], kh[par][:, gk * 128:(gk + 1) * 128],
                          qh[par][:, 0:512], True, True)
                    nmm = em.mm(pS[:, 512:1024],
                                kh[par][:, gk * 128:(gk + 1) * 128],
                                qh[par][:, 512:1024], True, True, inc="tP")
                    em.wait("scalar", "tP", nmm)
                    na = em.act_ps(lambda e, pS=pS, kk=kk: e.activation(
                        pt[0][:, kk * 1024:(kk + 1) * 1024], pS[:],
                        AF.Exp, bias=c.t_cb4[:], scale=SCALE_F))
                    ev[psname] = na
                    if kk == 7:
                        exp_done[0] = na
                continue
            fused_step(cur, nxt)
        em.wait("scalar", "tP", pvl_done[15])
        na = em.act_ps(lambda e: e.copy(lrow[:], pL[0:4, :]))
        ev["pL"] = na
        em.wait("vector", "tA", na)
        em.vec(lambda e: e.reciprocal(rrow[:], lrow[:]))
        nv = em.vec(lambda e: e.tensor_copy(rrowr[:], rrow[:]))
        rb_ev = []
        for qt in range(4):
            em.wait("sync", "tV", nv)
            em.wait("sync", "tP", ev.get("rflat_use", 0))
            nd = em.dma("sync", rflat[:], rrowr[qt:qt + 1, :])
            em.wait("tensor", "dS", nd)
            em.wait("tensor", "tA", ev.get("pX", 0))
            nmm = em.mm(pX[:], c.t_cmcol[0:1, h * 128:(h + 1) * 128],
                        rflat[0:1, :], True, True, inc="tP")
            ev["rflat_use"] = nmm
            em.wait("scalar", "tP", nmm)
            na = em.act_ps(lambda e, qt=qt: e.copy(
                rbc[:, qt * 512:(qt + 1) * 512], pX[:]))
            ev["pX"] = na
            rb_ev.append(na)
        em.wait("vector", "tA", rb_ev[-1])
        nv = em.vec(lambda e, h=h: e.tensor_mul(
            c.attn_out[:, h * SEG:(h + 1) * SEG], outh[:], rbc[:]))
        kv_use[par] = em.cnt["tP"]
        em.wait("scalar", "tV", nv)

    if DBG:
        c.phase_barrier()
        dbg = nc.dram_tensor("dbg_attn", [128, NCH * SEG], BF16,
                             kind="ExternalOutput").ap()
        nd = em.dma("sync", dbg, c.attn_out[:])
        em.wait("sync", "dS", nd)

    c.phase_barrier()
    p4.close()


def emit_phase5(c):
    nc, em = c.nc, c.em
    p5 = ExitStack()
    E5 = p5.enter_context
    owT = E5(nc.sbuf_tensor("owT", [128, NCH * C], BF16))
    wpan5 = E5(nc.sbuf_tensor("wpan5", [128, C], BF16))
    y2sb = E5(nc.sbuf_tensor("y2sb", [128, NCH * 512], F32))
    ot5 = [E5(nc.sbuf_tensor(f"ot5{i}", [128, 512], F32)) for i in range(2)]
    acct = [E5(nc.sbuf_tensor(f"acct{i}", [128, 512], F32)) for i in range(2)]
    ytile = [E5(nc.sbuf_tensor(f"ytile{i}", [128, C], F32)) for i in range(2)]
    yout = [E5(nc.sbuf_tensor(f"yout{i}", [128, C], F32)) for i in range(2)]
    xqt = [E5(nc.sbuf_tensor(f"xqt{i}", [128, C], F32)) for i in range(2)]
    p5a = [E5(nc.psum_tensor(f"p5a{i}", [128, 512], F32)) for i in range(2)]
    p5t = [E5(nc.psum_tensor(f"p5t{i}", [128, 512], F32)) for i in range(2)]
    ev = c.bank_evac

    for r in range(8):
        em.wait("gpsimd", "tP", em.cnt["tP"])
        n = em.dma("gpsimd", wpan5[:], c.ow[r * 128:(r + 1) * 128, :])
        em.wait("tensor", "dG", n)
        for g in range(2):
            em.wait("tensor", "tA", ev.get(f"p5t{g}", 0))
            last = None
            for i in range(4):
                mc = g * 4 + i
                last = em.tr(p5t[g][:].bitcast(BF16)[:, i * 128:(i + 1) * 128],
                             wpan5[:, mc * 128:(mc + 1) * 128],
                             c.t_idb[:], inc=(None if i < 3 else "tP"))
            em.wait("scalar", "tP", last)
            dv = owT[:].rearrange("p (mc w) -> p mc w", w=C)
            na = em.act_ps(lambda e, g=g, r=r, dv=dv: e.copy(
                dv[:, g * 4:g * 4 + 4, r * 128:(r + 1) * 128],
                p5t[g][:].bitcast(BF16)[:, 0:512].rearrange(
                    "p (a w) -> p a w", w=128)))
            ev[f"p5t{g}"] = na

    if P5_PART < 1.5:
        nd = em.dma("sync", c.y[0:128, :], c.xq[0:128, :])
        em.wait("sync", "dS", nd)
        c.phase_barrier()
        p5.close()
        return
    em.wait("tensor", "tA", em.cnt["tA"])
    nlast_y = None
    for qt in range(4):
        em.wait("vector", "tP", ev.get("y2sb_use", 0))
        for oc in range(NCH):
            par = oc % 2
            em.wait("tensor", "tA", ev.get(f"p5a{par}", 0))
            last = None
            for i in range(NCH):
                last = em.mm(p5a[par][:],
                             owT[:, i * C + oc * 128:i * C + (oc + 1) * 128],
                             c.attn_out[:, i * SEG + qt * 512:
                                        i * SEG + (qt + 1) * 512],
                             i == 0, i == NCH - 1,
                             inc=("tP" if i == NCH - 1 else None))
            em.wait("scalar", "tP", last)
            em.wait("scalar", "tV", ev.get(f"ot5{par}", 0))
            na = em.act_ps(lambda e, par=par: e.copy(ot5[par][:], p5a[par][:]))
            ev[f"p5a{par}"] = na
            em.wait("sync", "tV", ev.get(f"acct{par}", 0))
            nd = em.dma("sync", acct[par][:],
                        c.accscr[oc, :, qt * 512:(qt + 1) * 512])
            em.wait("vector", "tA", na)
            em.wait("vector", "dS", nd)
            em.vec(lambda e, par=par, oc=oc: e.scalar_tensor_tensor(
                y2sb[:, oc * 512:(oc + 1) * 512], ot5[par][:],
                c.t_ob[:, oc:oc + 1],
                c.t_cm[:, oc:oc + 1].broadcast_to([128, 512]),
                ALU.add, ALU.mult))
            nv = em.vec(lambda e, par=par, oc=oc: e.tensor_add(
                y2sb[:, oc * 512:(oc + 1) * 512],
                y2sb[:, oc * 512:(oc + 1) * 512], acct[par][:]))
            ev[f"ot5{par}"] = nv
            ev[f"acct{par}"] = nv
        n_y2 = em.cnt["tV"]
        em.wait("tensor", "tV", n_y2)
        if P5_PART < 2.5:
            continue
        for tt in range(4):
            ti = qt * 4 + tt
            par = tt % 2
            em.wait("sync", "tV", ev.get(f"xqt{par}", 0))
            ndx = em.dma("sync", xqt[par][:],
                         c.xq[ti * 128:(ti + 1) * 128, :])
            for g in range(2):
                em.wait("tensor", "tA", ev.get(f"p5t{g}", 0))
                last = None
                for i in range(4):
                    ch = g * 4 + i
                    last = em.tr(p5t[g][:, i * 128:(i + 1) * 128],
                                 y2sb[:, ch * 512 + tt * 128:
                                      ch * 512 + (tt + 1) * 128],
                                 c.t_id32[:], inc=(None if i < 3 else "tP"))
                em.wait("scalar", "tP", last)
                em.wait("scalar", "tV", ev.get(f"ytile{par}", 0))
                na = em.act_ps(lambda e, g=g, par=par: e.copy(
                    ytile[par][:, g * 512:(g + 1) * 512], p5t[g][:]))
                ev[f"p5t{g}"] = na
            if P5_PART < 2.8:
                continue
            em.wait("vector", "tA", em.cnt["tA"])
            em.wait("vector", "dS", ndx)
            em.wait("vector", "dS", ev.get(f"yout{par}", 0))
            nv = em.vec(lambda e, par=par: e.tensor_add(
                yout[par][:], ytile[par][:], xqt[par][:]))
            ev[f"ytile{par}"] = nv
            ev[f"xqt{par}"] = nv
            em.wait("sync", "tV", nv)
            nlast_y = em.dma("sync", c.y[ti * 128:(ti + 1) * 128, :],
                             yout[par][:])
            ev[f"yout{par}"] = nlast_y
        ev["y2sb_use"] = em.cnt["tP"]

    if P5_PART < 2.9:
        nlast_y = None
    if nlast_y is None:
        nlast_y = em.dma("sync", c.y[0:128, :], c.xq[0:128, :])
    em.wait("sync", "dS", nlast_y)
    c.phase_barrier()
    p5.close()


def make_masks():
    mA = np.zeros((128, 128), dtype=np.float32)
    mB = np.zeros((64, 128), dtype=np.float32)
    for r in range(128):
        for cc in range(192):
            ok = (cc <= 64 + r) and (r < 64 or cc >= 64)
            if ok:
                if cc < 128:
                    mA[cc, r] = 1.0
                else:
                    mB[cc - 128, r] = 1.0
    return mA.astype(ml_dtypes.bfloat16), mB.astype(ml_dtypes.bfloat16)


def make_selmat():
    s = np.zeros((128, 4, 4), dtype=np.float32)
    for t in range(4):
        s[:, t, t] = 1.0
    return s.reshape(128, 16).astype(ml_dtypes.bfloat16)


_program_cache = {}


def get_program(stages=STAGES):
    key = (stages, DBG)
    if key not in _program_cache:
        _program_cache[key] = build_program(stages)
    return _program_cache[key]


def make_inmaps(inputs):
    x = np.ascontiguousarray(inputs["L0_attn_x"], dtype=np.float32)
    mA, mB = make_masks()
    common = {
        "wqkv": np.ascontiguousarray(inputs["L0_attn_w_qkv"]),
        "bqkv": np.ascontiguousarray(inputs["L0_attn_b_qkv"]),
        "wout": np.ascontiguousarray(inputs["L0_attn_w_out"]),
        "bout": np.ascontiguousarray(inputs["L0_attn_b_out"]),
        "ln1g": np.ascontiguousarray(inputs["L0_attn_ln1_g"]),
        "ln1b": np.ascontiguousarray(inputs["L0_attn_ln1_b"]),
        "ln2g": np.ascontiguousarray(inputs["L0_attn_ln2_g"]),
        "ln2b": np.ascontiguousarray(inputs["L0_attn_ln2_b"]),
        "w1": np.ascontiguousarray(inputs["L0_attn_w1"]),
        "b1": np.ascontiguousarray(inputs["L0_attn_b1"]),
        "w2": np.ascontiguousarray(inputs["L0_attn_w2"]),
        "b2": np.ascontiguousarray(inputs["L0_attn_b2"]),
        "nw": np.ascontiguousarray(inputs["L0_ffn_norm_w"]),
        "nb": np.ascontiguousarray(inputs["L0_ffn_norm_b"]),
        "qw": np.ascontiguousarray(inputs["L0_ffn_q_w"]),
        "qbias": np.ascontiguousarray(inputs["L0_ffn_q_b"]),
        "kw": np.ascontiguousarray(inputs["L0_ffn_k_w"]),
        "kbias": np.ascontiguousarray(inputs["L0_ffn_k_b"]),
        "vw": np.ascontiguousarray(inputs["L0_ffn_v_w"]),
        "vbias": np.ascontiguousarray(inputs["L0_ffn_v_b"]),
        "ow": np.ascontiguousarray(inputs["L0_ffn_o_w"]),
        "obias": np.ascontiguousarray(inputs["L0_ffn_o_b"]),
        "ident32": np.eye(128, dtype=np.float32),
        "identb": np.eye(128, dtype=ml_dtypes.bfloat16),
        "ones1": np.ones((1, 128), dtype=np.float32),
        "onesb": np.ones((128, 1), dtype=ml_dtypes.bfloat16),
        "onesrb": np.ones((1, 128), dtype=ml_dtypes.bfloat16),
        "selmat": make_selmat(),
        "maskA": mA, "maskB": mB,
    }
    in_maps = []
    for core in range(8):
        b, e = core // 4, core % 4
        m_e = C >> e
        cm = np.zeros(C, dtype=np.float32); cm[:m_e] = 1.0
        d = dict(common)
        d["xb"] = x[b]
        d["xq"] = x[b, e * SEG:(e + 1) * SEG]
        d["xh"] = (x[b, e * SEG - 64:e * SEG] if e > 0
                   else np.zeros((64, C), dtype=np.float32))
        d["cmaskv"] = cm
        d["cmcol"] = cm[None, :]
        d["hmaskv"] = np.full((128, 1), 1.0 if e > 0 else 0.0, dtype=np.float32)
        in_maps.append(d)
    return in_maps


def kernel(**inputs):
    nc = get_program()
    in_maps = make_inmaps(inputs)
    res = run_bass_kernel_spmd(nc, in_maps, core_ids=list(range(8)))
    out = np.empty((B, T, C), dtype=np.float32)
    for core in range(8):
        b, e = core // 4, core % 4
        out[b, e * SEG:(e + 1) * SEG] = res.results[core]["y"]
    return out
